# version 11
# baseline (speedup 1.0000x reference)
import numpy as np
import ml_dtypes
import concourse.bass as bass
import concourse.mybir as mybir
from concourse.bass_utils import run_bass_kernel_spmd
from contextlib import ExitStack

F32 = mybir.dt.float32
BF16 = mybir.dt.bfloat16
I32 = mybir.dt.int32
AF = mybir.ActivationFunctionType
ALU = mybir.AluOpType
AX = mybir.AxisListType
bfdt = ml_dtypes.bfloat16

B, T, D = 2, 16384, 2048
H, DK, DV, M = 16, 64, 128, 64
KD, HM, VD, HID = 1024, 1024, 2048, 8192
NC = 8
EPS = 1e-5
EPS_FFN = float(np.finfo(np.float32).eps)
MAGIC = 0x5F3759DF
QTHR = 0.5 + 2.0 ** -10

ENGS = ("pe", "dve", "act", "pool", "sync")
SEMS = ("dw", "dwA", "dwB", "dfA", "dfB", "dgA", "dgB", "dx", "ds", "pw", "pwA", "pwB", "ps", "psA", "psB", "slo", "dbg", "pe", "gate", "dve", "act", "pool", "dout", "coll")


PHASELOG = []


class Rec:
    def __init__(self):
        self.streams = {e: [] for e in ENGS}
        self.cnt = {s: 0 for s in SEMS}

    def mark(self, label):
        PHASELOG.append((label, dict(self.cnt),
                         {e: len(s) for e, s in self.streams.items()}))

    def wait(self, eng, sem, val):
        if val > 0:
            self.streams[eng].append(("w", sem, val))

    def emit(self, eng, fn, sem=None, n=1):
        if sem is not None:
            self.cnt[sem] += n
        self.streams[eng].append(("i", fn, sem, n))
        return self.cnt[sem] if sem else None

    def dma(self, eng, sem, out, in_):
        self.cnt[sem] += 16
        self.streams[eng].append(
            ("i", lambda e, out=out, in_=in_: e.dma_start(out=out, in_=in_),
             sem, 16))
        return self.cnt[sem]


def build(NT=8, dbg=False):
    TILE, NTC, NJ = 512, 4, 16
    TCORE = NT * TILE
    NHFC = HID // 128

    nc = bass.Bass("TRN2", target_bir_lowering=False, debug=False)
    dr = lambda n, s, d, k="ExternalInput": nc.dram_tensor(n, s, d, kind=k).ap()
    x_in = dr("x", [TCORE, D], F32)
    xT_in = dr("xT", [D, TCORE], F32)
    w_in = {"wq": dr("wq", [D, KD], F32), "wk": dr("wk", [D, KD], F32),
            "ws": dr("ws", [D, HM], F32), "wv": dr("wv", [D, VD], F32),
            "wg": dr("wg", [D, VD], F32), "wo": dr("wo", [VD, D], F32)}
    wfcT_in = dr("wfcT", [D, HID], F32)
    wprojT_in = dr("wprojT", [HID, D], F32)
    gnw_in = dr("gnw", [1, VD], F32)
    ident_in = dr("ident", [128, 128], BF16)
    triu_in = dr("triu", [128, 128], BF16)
    onescol_in = dr("onescol", [128, 1], BF16)
    ones2_in = dr("ones2", [2, 128], BF16)
    sel2_in = dr("sel2", [2, 128], BF16)
    sel4_in = dr("sel4", [4, 128], BF16)
    selg_in = dr("selg", [128, 2], BF16)
    halfrow_in = dr("halfrow", [1, 128], BF16)
    pmask_in = dr("pmask", [128, 8], F32)
    out_dram = dr("out", [TCORE, D], F32, "ExternalOutput")

    wsc = {n: dr(n + "_s", list(w_in[n].shape), BF16, "Internal") for n in w_in}
    fcq_s = dr("fcq_s", [NJ, 128, HID], BF16, "Internal")
    projq_s = dr("projq_s", [NHFC, 128, D], BF16, "Internal")
    st_loc = dr("st_loc", [128, 2048], F32, "Internal")
    st_all = dr("st_all", [128 * NC, 2048], F32, "Internal")
    dbg_t = {}
    if dbg:
        for n, shp, dt_ in [("q_fm", [128, 4096], BF16),
                            ("k_fm", [128, 4096], BF16),
                            ("p_tm", [128, 4096], BF16),
                            ("v_tm", [128, 8192], BF16),
                            ("sgb", [128, 8192], BF16),
                            ("rcs0", [128, 1024], F32),
                            ("oks0", [128, 1024], F32),
                            ("pvn0", [128, 1024], BF16),
                            ("ogtm0", [128, 2048], BF16),
                            ("xrb0", [128, 2048], F32)]:
            dbg_t[n] = dr("dbg_" + n, shp, dt_, "Internal")

    with ExitStack() as top:
        E = top.enter_context
        sb = lambda n, s, d=F32: E(nc.sbuf_tensor(n, s, d))

        ident = sb("sb_ident", [128, 128], BF16)
        triu = sb("sb_triu", [128, 128], BF16)
        onescol = sb("sb_onescol", [128, 1], BF16)
        ones2 = sb("sb_ones2", [2, 128], BF16)
        sel2 = sb("sb_sel2", [2, 128], BF16)
        sel4 = sb("sb_sel4", [4, 128], BF16)
        selg = sb("sb_selg", [128, 2], BF16)
        halfrow = sb("sb_halfrow", [1, 128], BF16)
        pmask = sb("sb_pmask", [128, 8])
        gnw_h = sb("gnw_h", [128, VD], BF16)
        kh_f = sb("kh_f", [64, HM])
        vh_f = sb("vh_f", [64, VD])
        kh_b = sb("kh_b", [128, HM], BF16)
        vh_b = sb("vh_b", [128, VD], BF16)
        s_st = sb("s_st", [1, HM])
        s_hl = sb("s_hl", [2, HM], BF16)
        s_lo = sb("s_lo", [1, HM], BF16)
        xt_f = sb("xt_f", [128, D])
        xf_pin = sb("xf_pin", [128, 8192], BF16)
        x2_f = sb("x2_f", [128, NTC * D])
        wstr = sb("wstr", [128, 2 * 2048], BF16)
        wffn = sb("wffn", [128, 2 * 2048], BF16)
        sm = [sb(f"sm{i}", [128, 32]) for i in range(8)]

        AR_BYTES = 100 * 1024
        arena = sb("arena", [128, AR_BYTES // 2], BF16)

        def ar(off_b, nelem, dt=BF16):
            esz = 4 if dt in (F32, I32) else 2
            assert off_b % 4 == 0 and off_b + nelem * esz <= AR_BYTES
            v = arena[:, off_b // 2:(off_b + nelem * esz) // 2]
            return v.bitcast(dt) if dt != BF16 else v

        def mkmap(spec):
            mp, off = {}, 0
            for name, nbytes in spec:
                mp[name] = off
                off += (nbytes + 63) // 64 * 64
            assert off <= AR_BYTES, ("arena overflow", off)
            return mp

        oA = mkmap([("q_fm", 8192), ("k_fm", 8192), ("p_tm", 8192),
                    ("v_tm", 16384), ("sg", 16384), ("ogT", 16384),
                    ("kt", 2048), ("p_fm", 2048),
                    ("pvn_tm", 2048), ("pvn_fm", 2048), ("oks", 4096),
                    ("pexp", 4096), ("rcs", 4096), ("skm", 512),
                    ("pvm", 512)])
        oA["xr"] = oA["q_fm"]
        oA["sclip"] = oA["oks"]
        oA["thg"] = oA["pvn_tm"]
        oA["osq"] = oA["rcs"]
        oA["osq2"] = oA["rcs"] + 2048
        oF = mkmap([("h_sr", 65536), ("x2fm", 16384), ("rbuf", 2048),
                    ("hsq", 2048)])
        oF["x2bf"] = oF["rbuf"]
        oF["xr2a"] = oF["x2fm"]
        oF["xr2b"] = oF["rbuf"]
        oF["x3"] = 0
        oP = mkmap([("c_acc", 65536), ("xt_bf", 16384),
                    ("p1_tm", 8192), ("s1clip", 4096), ("pad_", 4096),
                    ("srow", 4096)])
        assert oP["srow"] >= 98304
        oQ = mkmap([("qwb", 8192), ("qaw", 8192), ("qa", 8192),
                    ("qm1", 2048), ("qm2", 2048), ("qmu", 2048),
                    ("qgs", 8192), ("qscf", 8192), ("thc", 8192),
                    ("qpad", 10240), ("c_bf", 32768)])
        assert oQ["c_bf"] == 65536
        srow_f = ar(oP["srow"], 1024, F32)[0:1, :]
        gnrow_f = ar(0, 2048, F32)[0:1, :]
        gnrow_b = ar(8192, 2048)[0:1, :]

        pA = E(nc.psum_tensor("pA", [128, 2048], F32))
        pB = E(nc.psum_tensor("pB", [128, 2048], F32))

        def bank(i, p0=0, np_=128):
            t = pA if i < 4 else pB
            return t[p0:p0 + np_, (i % 4) * 512:(i % 4 + 1) * 512]

        def bank_bf(i):
            t = pA if i < 4 else pB
            return t[:, (i % 4) * 512:(i % 4 + 1) * 512].bitcast(BF16)

        sems = {s: E(nc.semaphore(s)) for s in SEMS}
        block = E(nc.Block())
        r = Rec()

        def dump(name, ap, sem, val):
            if dbg and name in dbg_t:
                r.wait("act", sem, val)
                r.dma("act", "dbg", dbg_t[name], ap)
                for engn in ("dve", "pe", "pool", "sync", "act"):
                    r.wait(engn, "dbg", r.cnt["dbg"])

        def mm(out, lhsT, rhs, start, stop):
            r.emit("pe", lambda e, a=out, b=lhsT, c=rhs, st=start, sp=stop:
                   e.matmul(a, b, c, start=st, stop=sp), "pe", 1)

        def tr(out_psum, in_sb):
            r.emit("pe", lambda e, a=out_psum, b=in_sb:
                   e.transpose(a, b, ident[:, :]), "pe", 1)

        def pe_mark():
            for idx in range(len(r.streams["pe"]) - 1, -1, -1):
                it = r.streams["pe"][idx]
                if it[0] == "i":
                    if it[2] == "pe":
                        return r.cnt["pe"]
                    assert it[2] is None
                    r.cnt["pe"] += 1
                    r.streams["pe"][idx] = ("i", it[1], "pe", 1)
                    return r.cnt["pe"]
            raise AssertionError("pe_mark with no prior op")

        def pe_gate():
            r.emit("pe", lambda e: e.nop(), "gate")
            return ("gate", r.cnt["gate"])

        def dve(fn):
            return r.emit("dve", fn, "dve")

        def act(fn):
            return r.emit("act", fn, "act")

        def rsqrt_rec(v_ap, out_ap, t1, t2, iters=2):
            dve(lambda e, a=v_ap, b=t1: e.tensor_scalar(
                out=b.bitcast(I32), in0=a.bitcast(I32), scalar1=1,
                scalar2=None, op0=ALU.logical_shift_right))
            dve(lambda e, a=t1, b=out_ap: e.tensor_scalar(
                out=b.bitcast(I32), in0=a.bitcast(I32), scalar1=-1,
                scalar2=MAGIC, op0=ALU.mult, op1=ALU.add))
            for _ in range(iters):
                dve(lambda e, a=v_ap, b=out_ap, c=t1: e.tensor_tensor(
                    out=c, in0=a, in1=b, op=ALU.mult))
                dve(lambda e, a=t1, b=out_ap, c=t2: e.tensor_tensor(
                    out=c, in0=a, in1=b, op=ALU.mult))
                dve(lambda e, a=t2, b=t1: e.tensor_scalar(
                    out=b, in0=a, scalar1=-0.5, scalar2=1.5, op0=ALU.mult,
                    op1=ALU.add))
                dve(lambda e, a=t1, b=out_ap: e.tensor_tensor(
                    out=b, in0=b, in1=a, op=ALU.mult))
            return r.cnt["dve"]

        def slot_load(buf, slot_cols, srcs, i, guard, guard_sem, sem=None,
                      queue="sync"):
            if isinstance(guard, tuple):
                r.wait(queue, guard[0], guard[1])
            else:
                r.wait(queue, guard_sem, guard)
            sl = (i % 2) * slot_cols
            out = buf[:, sl:sl + slot_cols]
            s = srcs[i]
            if len(s.shape) == 3:
                out = out.rearrange("p (a b) -> p a b", b=s.shape[2])
            if sem is None:
                sem = "dwA" if i % 2 == 0 else "dwB"
            r.dma(queue, sem, out, s)
            return sem, r.cnt[sem]

        def stream(buf, slot_cols, srcs, compute, guard0, guard_sem="pe",
                   pre=None, dual=False):
            n = len(srcs)
            cwl, marks = [0] * n, [0] * n

            def load(i, guard):
                if dual and i % 2 == 1:
                    cwl[i] = slot_load(buf, slot_cols, srcs, i, guard,
                                       guard_sem, sem="pwB", queue="pool")
                else:
                    cwl[i] = slot_load(buf, slot_cols, srcs, i, guard,
                                       guard_sem)

            if pre is not None:
                cwl[0], cwl[1] = pre
            else:
                load(0, guard0)
                if n > 1:
                    load(1, guard0)
            for i in range(n):
                r.wait("pe", cwl[i][0], cwl[i][1])
                marks[i] = compute(i, buf[:, (i % 2) * slot_cols:
                                          (i % 2) * slot_cols + slot_cols])
                if i + 2 < n:
                    load(i + 2, marks[i])
            return marks[-1]

        for dst, src in [(ident, ident_in), (triu, triu_in),
                         (onescol, onescol_in), (ones2, ones2_in),
                         (sel2, sel2_in), (sel4, sel4_in), (selg, selg_in),
                         (halfrow, halfrow_in), (pmask, pmask_in)]:
            r.dma("sync", "ds", dst[:, :], src)
        r.dma("sync", "ds", gnrow_f[:, :], gnw_in)
        c_const = r.cnt["ds"]
        r.wait("act", "ds", c_const)
        c0 = act(lambda e: e.activation(out=gnrow_b[:, :], in_=gnrow_f[:, :],
                                        func=AF.Copy))
        r.wait("pe", "act", c0)
        r.wait("pe", "ds", c_const)
        for piece in range(4):
            mm(bank(piece), halfrow[0:1, :],
               gnrow_b[0:1, piece * 512:(piece + 1) * 512], True, True)
        c1 = pe_mark()
        r.wait("dve", "pe", c1)
        c_gnw_cp = dve(lambda e: e.tensor_copy(out=gnw_h[:, :], in_=pA[:, :]))

        p1_pre = {}
        for t0 in range(min(2, NT)):
            xt_bf0 = ar(oP["xt_bf"], 8192) if t0 == 0 else None
        r.dma("pool", "pw",
              ar(oP["xt_bf"], 8192).rearrange("p (c d) -> p c d", d=2048),
              x_in[0:512, :].rearrange("(c p) d -> p c d", p=128))
        r.dma("pool", "pw",
              xf_pin[:, :].rearrange("p (j t) -> p j t", t=512),
              xT_in[:, 0:512].rearrange("(j p) t -> p j t", p=128))
        c_xpre = r.cnt["pw"]

        conv_kk = [0]

        def conv_chunk(name, j, buf):
            kk = conv_kk[0]
            srcw, dstw = w_in[name], wsc[name]
            ncol = srcw.shape[1]
            par = "A" if kk % 2 == 0 else "B"
            slot = (kk % 2) * 2048
            if kk >= 2:
                r.wait("pool", "ps" + par, 16 * (kk // 2))
            r.dma("pool", "pw" + par, buf[:, slot:slot + ncol],
                  srcw[j * 128:(j + 1) * 128, :])
            r.wait("pool", "pw" + par, r.cnt["pw" + par])
            r.dma("pool", "ps" + par, dstw[j * 128:(j + 1) * 128, :],
                  buf[:, slot:slot + ncol])
            conv_kk[0] += 1
            return (r.cnt["psA"], r.cnt["psB"])

        for j in range(NJ):
            conv_chunk("ws", j, wstr)
        c_conv_ws = (r.cnt["psA"], r.cnt["psB"])
        conv_tail = [(n, j) for n in ["wk", "wv", "wq", "wg", "wo"]
                     for j in range(NJ)]
        conv_ptr = [0]
        c_conv_kv = None
        c_conv_all = None

        r.wait("pe", "psA", c_conv_ws[0])
        r.wait("pe", "psB", c_conv_ws[1])
        g_wstr = pe_gate()

        c_acc = ar(oP["c_acc"], NJ * 1024, F32)
        dve(lambda e: e.memset(c_acc, 0.0))
        dve(lambda e: e.memset(srow_f[:, :], 0.0))

        g_p1x = 0
        g_sp1 = 0
        c_xldv = [c_xpre] + [0] * NT
        r.wait("pe", "dve", c_gnw_cp)
        for t in range(NT):
            xt_bf = ar(oP["xt_bf"], 8192)
            xf1 = xf_pin[:, :]
            p1tm = ar(oP["p1_tm"], 4096)
            s1c = ar(oP["s1clip"], 1024, F32)
            c_xld = c_xldv[t]

            def s_comp(i, slot, xf1=xf1, c_xld=c_xld):
                if i == 0:
                    r.wait("pe", "pw", c_xld)
                for jj in range(2):
                    j = i * 2 + jj
                    wsl = slot[:, jj * 1024:(jj + 1) * 1024]
                    for tc in range(NTC):
                        lh = xf1[:, j * 512 + tc * 128:j * 512 + (tc + 1) * 128]
                        for hf in range(2):
                            mm(bank(tc * 2 + hf), lh,
                               wsl[:, hf * 512:(hf + 1) * 512],
                               start=(j == 0), stop=(j == NJ - 1))
                return pe_mark()

            g_wstr = stream(
                wstr, 2048,
                [wsc["ws"][p * 256:(p + 1) * 256, :]
                 .rearrange("(c p) n -> p c n", p=128) for p in range(8)],
                s_comp, g_wstr)
            g_sp1 = g_wstr
            c_sp = r.cnt["pe"]
            for tc in range(NTC):
                r.wait("dve", "pe", c_sp)
                if tc > 0:
                    r.wait("dve", "act", r.cnt["act"])
                for hf in range(2):
                    dve(lambda e, tc=tc, hf=hf, s1c=s1c: e.tensor_scalar(
                        out=s1c[:, hf * 512:(hf + 1) * 512],
                        in0=bank(tc * 2 + hf), scalar1=32.0, scalar2=-32.0,
                        op0=ALU.min, op1=ALU.max))
                c_cl = r.cnt["dve"]
                r.wait("act", "dve", c_cl)
                act(lambda e, tc=tc, s1c=s1c, p1tm=p1tm: e.activation(
                    out=p1tm[:, tc * 1024:(tc + 1) * 1024], in_=s1c[:, :],
                    func=AF.Exp))
            c_pex = r.cnt["act"]
            r.wait("pe", "act", c_pex)
            ca_marks = [0] * NJ
            for dj in range(NJ):
                for hf in range(2):
                    for tc in range(NTC):
                        lh = xt_bf[:, tc * 2048 + dj * 128:
                                   tc * 2048 + (dj + 1) * 128]
                        mm(bank((dj % 2) * 2 + hf), lh,
                           p1tm[:, tc * 1024 + hf * 512:
                                tc * 1024 + (hf + 1) * 512],
                           start=(tc == 0), stop=(tc == NTC - 1))
                c_cp = pe_mark()
                r.wait("dve", "pe", c_cp)
                for hf in range(2):
                    dve(lambda e, dj=dj, hf=hf, bq=2 + (dj % 2) * 2:
                        e.tensor_tensor(
                            out=c_acc[:, dj * 1024 + hf * 512:
                                      dj * 1024 + (hf + 1) * 512],
                            in0=bank(bq - 2 + hf) if False else bank(
                                (dj % 2) * 2 + hf),
                            in1=c_acc[:, dj * 1024 + hf * 512:
                                      dj * 1024 + (hf + 1) * 512],
                            op=ALU.add))
                ca_marks[dj] = r.cnt["dve"]
                if dj >= 1:
                    r.wait("pe", "dve", ca_marks[dj - 1])
            r.wait("pe", "dve", ca_marks[NJ - 1])
            for tc in range(NTC):
                for hf in range(2):
                    mm(bank(2 + hf, 0, 1), onescol[:, :],
                       p1tm[:, tc * 1024 + hf * 512:tc * 1024 + (hf + 1) * 512],
                       start=(tc == 0), stop=(tc == NTC - 1))
            g_p1x = pe_mark()
            r.wait("dve", "pe", g_p1x)
            for hf in range(2):
                dve(lambda e, hf=hf: e.tensor_tensor(
                    out=srow_f[0:1, hf * 512:(hf + 1) * 512],
                    in0=bank(2 + hf, 0, 1),
                    in1=srow_f[0:1, hf * 512:(hf + 1) * 512], op=ALU.add))
            c_p1t = r.cnt["dve"]
            r.wait("pool", "dve", c_p1t)
            if t + 1 < NT:
                r.wait("pool", "pe", g_sp1)
                r.dma("pool", "pw",
                      xf_pin[:, :].rearrange("p (j t) -> p j t", t=512),
                      xT_in[:, (t + 1) * 512:(t + 2) * 512]
                      .rearrange("(j p) t -> p j t", p=128))
                r.wait("pool", "pe", g_p1x)
                r.dma("pool", "pw",
                      ar(oP["xt_bf"], 8192).rearrange("p (c d) -> p c d",
                                                      d=2048),
                      x_in[(t + 1) * 512:(t + 2) * 512, :]
                      .rearrange("(c p) d -> p c d", p=128))
                c_xldv[t + 1] = r.cnt["pw"]
            ntl = -(-len(conv_tail) // max(1, NT - 1)) if NT > 1 else 0
            while conv_ptr[0] < min((t + 1) * ntl, len(conv_tail)):
                name, j = conv_tail[conv_ptr[0]]
                cc = conv_chunk(name, j, wffn)
                conv_ptr[0] += 1
                if (name, j) == ("wv", NJ - 1):
                    c_conv_kv = cc

        while conv_ptr[0] < len(conv_tail):
            name, j = conv_tail[conv_ptr[0]]
            cc = conv_chunk(name, j, wffn)
            conv_ptr[0] += 1
            if (name, j) == ("wv", NJ - 1):
                c_conv_kv = cc
        c_conv_all = (r.cnt["psA"], r.cnt["psB"])

        c_bfv = ar(oQ["c_bf"], NJ * 1024)
        wkf = ar(oP["c_acc"], NJ * 1024)
        wvh2 = ar(oP["c_acc"] + 32768, NJ * 1024)
        r.wait("sync", "psA", c_conv_kv[0])
        r.wait("sync", "psB", c_conv_kv[1])
        r.wait("sync", "dwA", r.cnt["dwA"])
        r.wait("sync", "dwB", r.cnt["dwB"])
        for dj in range(NJ):
            dve(lambda e, dj=dj: e.tensor_copy(
                out=c_bfv[:, dj * 1024:(dj + 1) * 1024],
                in_=c_acc[:, dj * 1024:(dj + 1) * 1024]))
        c_cb = r.cnt["dve"]
        r.wait("sync", "dve", c_cb)
        r.dma("sync", "dwA",
              wkf.rearrange("p (c n) -> p c n", n=1024),
              wsc["wk"].rearrange("(c p) n -> p c n", p=128))
        c_wkf = r.cnt["dwA"]
        r.dma("sync", "dwB",
              wvh2.rearrange("p (c n) -> p c n", n=1024),
              wsc["wv"][:, 0:1024].rearrange("(c p) n -> p c n", p=128))
        c_wvf = r.cnt["dwB"]
        r.wait("pe", "dve", c_cb)
        r.wait("pe", "dwA", c_wkf)
        r.wait("pe", "dwB", c_wvf)
        c_wv2 = None
        for h in range(H):
            if h == 8:
                r.wait("pe", "dwB", c_wv2)
            for dj in range(NJ):
                mm(bank(h // 8, 0, 64)[:, (h % 8) * 64:(h % 8) * 64 + 64],
                   wkf[:, dj * 1024 + h * 64:dj * 1024 + (h + 1) * 64],
                   c_bfv[:, dj * 1024 + h * 64:dj * 1024 + (h + 1) * 64],
                   start=(dj == 0), stop=(dj == NJ - 1))
            for dj in range(NJ):
                mm(bank(2 + h // 4, 0, 64)[:, (h % 4) * 128:
                                           (h % 4) * 128 + 128],
                   c_bfv[:, dj * 1024 + h * 64:dj * 1024 + (h + 1) * 64],
                   wvh2[:, dj * 1024 + (h % 8) * 128:
                        dj * 1024 + (h % 8 + 1) * 128],
                   start=(dj == 0), stop=(dj == NJ - 1))
            if h == 7:
                r.wait("sync", "pe", pe_mark())
                r.dma("sync", "dwB",
                      wvh2.rearrange("p (c n) -> p c n", n=1024),
                      wsc["wv"][:, 1024:2048]
                      .rearrange("(c p) n -> p c n", p=128))
                c_wv2 = r.cnt["dwB"]
        c_kv = pe_mark()
        r.wait("dve", "pe", c_kv)
        for i in range(2):
            dve(lambda e, i=i: e.tensor_copy(
                out=kh_f[0:64, i * 512:(i + 1) * 512], in_=bank(i, 0, 64)))
        for i in range(4):
            dve(lambda e, i=i: e.tensor_copy(
                out=vh_f[0:64, i * 512:(i + 1) * 512], in_=bank(2 + i, 0, 64)))
        c_kvs = r.cnt["dve"]

        zbuf = ar(oP["p1_tm"], 1024, F32)
        dve(lambda e: e.memset(zbuf, 0.0))
        c_zb = r.cnt["dve"]
        r.wait("sync", "dve", max(c_kvs, c_zb))
        r.dma("sync", "ds", st_loc[1:64, 1024:2048], zbuf[1:64, :])
        r.dma("sync", "ds", st_loc[0:64, 0:1024], kh_f[0:64, :])
        r.dma("sync", "ds", st_loc[64:128, 0:2048], vh_f[0:64, :])
        r.dma("sync", "ds", st_loc[0:1, 1024:2048], srow_f[0:1, :])
        c_pack = r.cnt["ds"]
        r.wait("pool", "ds", c_pack)
        r.emit("pool", lambda e: e.collective_compute(
            "AllGather", ALU.bypass, replica_groups=[list(range(NC))],
            ins=[st_loc[:, :].opt()], outs=[st_all[:, :].opt()]), "coll", 1)
        c_coll = r.cnt["coll"]

        for engn in ("pool", "sync", "dve", "act"):
            r.wait(engn, "pe", c_kv)
        qwb = ar(oQ["qwb"], 4096)
        qaw = ar(oQ["qaw"], 4096)
        qa = ar(oQ["qa"], 4096)
        qm1 = ar(oQ["qm1"], 1024)
        qm2 = ar(oQ["qm2"], 1024)
        qmu = ar(oQ["qmu"], 1024)
        qgs = ar(oQ["qgs"], 2048, F32)[0:2, :]
        qscf = ar(oQ["qscf"], 2048, F32)[0:2, :]
        thc = ar(oQ["thc"], 4096)
        SCI = 1.0 / QTHR

        chunks = []
        for i in range(NJ):
            for rr in range(4):
                chunks.append((wfcT_in[i * 128:(i + 1) * 128,
                                       rr * 2048:(rr + 1) * 2048],
                               fcq_s[i, :, rr * 2048:(rr + 1) * 2048]))
        for cch in range(NHFC):
            chunks.append((wprojT_in[cch * 128:(cch + 1) * 128, :],
                           projq_s[cch, :, :]))
        nck = len(chunks)

        ld_m = [0] * nck
        abs_m = [0] * nck
        cast_m = [0] * nck
        sum_m = [0] * nck
        evac_m = [0] * nck
        smath_m = [0] * nck
        fill_m = [[0, 0] for _ in range(nck)]
        eff_m = [[0, 0] for _ in range(nck)]
        st_m = [0] * nck
        wf32 = ar(oQ["qpad"], 2048, F32)
        for k in range(nck):
            sl = (k % 2) * 2048
            tsl = (k % 2) * 2048
            if k % 2 == 0:
                if k >= 2:
                    r.wait("pool", "dve", eff_m[k - 2][1])
                r.dma("pool", "pwA", qwb[:, sl:sl + 2048], chunks[k][0])
                ld_m[k] = r.cnt["pwA"]
                r.wait("act", "pwA", ld_m[k])
            else:
                if k >= 2:
                    r.wait("sync", "act", cast_m[k - 2])
                r.dma("sync", "pwB", wf32[:, :], chunks[k][0])
                ld_m[k] = r.cnt["pwB"]
                r.wait("act", "pwB", ld_m[k])
                if k >= 2:
                    r.wait("act", "dve", eff_m[k - 2][1])
                act(lambda e, sl=sl: e.activation(
                    out=qwb[:, sl:sl + 2048], in_=wf32[:, :], func=AF.Copy))
                cast_m[k] = r.cnt["act"]
            if k >= 2:
                r.wait("act", "pe", sum_m[k - 2])
            act(lambda e, sl=sl: e.activation(
                out=qaw[:, sl:sl + 2048], in_=qwb[:, sl:sl + 2048],
                func=AF.Abs))
            abs_m[k] = r.cnt["act"]
            r.wait("pe", "act", abs_m[k])
            if k >= 1:
                r.wait("pe", "act", evac_m[k - 1])
            for piece in range(4):
                mm(bank(4 + piece, 0, 2), selg[:, :],
                   qaw[:, sl + piece * 512:sl + (piece + 1) * 512],
                   True, True)
            sum_m[k] = pe_mark()
            r.wait("act", "pe", sum_m[k])
            if k >= 1:
                r.wait("act", "dve", smath_m[k - 1])
            act(lambda e: e.copy(out=qgs[:, :], in_=pB[0:2, :]))
            evac_m[k] = r.cnt["act"]
            r.wait("dve", "act", evac_m[k])
            if k >= 2:
                r.wait("dve", "pe", fill_m[k - 2][1])
            dve(lambda e: e.tensor_scalar(out=qscf[:, :], in0=qgs[:, :],
                                          scalar1=1.0 / 64, scalar2=None,
                                          op0=ALU.mult))
            dve(lambda e, tsl=tsl: e.tensor_scalar(
                out=thc[0:2, tsl:tsl + 2048], in0=qscf[:, :],
                scalar1=float(bfdt(1e-8)), scalar2=None, op0=ALU.max))
            smath_m[k] = r.cnt["dve"]
            for h in range(2):
                if h == 0:
                    r.wait("pe", "dve", smath_m[k])
                if k >= 1:
                    r.wait("pe", "dve", eff_m[k - 1][h])
                for piece in range(2):
                    mm(bank(h * 2 + piece), sel2[:, :],
                       thc[0:2, tsl + h * 1024 + piece * 512:
                           tsl + h * 1024 + (piece + 1) * 512],
                       True, True)
                fill_m[k][h] = pe_mark()
                r.wait("dve", "pe", fill_m[k][h])
                if k >= 2 and h == 0:
                    r.wait("dve", "ps", st_m[k - 2])
                wslc = qwb[:, sl + h * 1024:sl + (h + 1) * 1024]
                stn = pA[:, h * 1024:(h + 1) * 1024]
                dve(lambda e, wslc=wslc, stn=stn: e.scalar_tensor_tensor(
                    out=qm1[:, :], in0=wslc, scalar=SCI,
                    in1=stn, op0=ALU.mult, op1=ALU.is_gt))
                dve(lambda e, wslc=wslc, stn=stn: e.scalar_tensor_tensor(
                    out=qm2[:, :], in0=wslc, scalar=-SCI,
                    in1=stn, op0=ALU.mult, op1=ALU.is_gt))
                dve(lambda e: e.tensor_tensor(out=qmu[:, :], in0=qm1[:, :],
                                              in1=qm2[:, :],
                                              op=ALU.subtract))
                dve(lambda e, stn=stn, o=qa[:, sl + h * 1024:
                                            sl + (h + 1) * 1024]:
                    e.tensor_tensor(out=o, in0=qmu[:, :], in1=stn,
                                    op=ALU.mult))
                eff_m[k][h] = r.cnt["dve"]
            r.wait("act", "dve", eff_m[k][1])
            r.dma("act", "ps", chunks[k][1], qa[:, sl:sl + 2048])
            st_m[k] = r.cnt["ps"]
        q_done_all = st_m[nck - 1]

        acc_ab = [x2_f[:, 0:2048], x2_f[:, 2048:4096]]
        stg = [x2_f[:, 4096:6144], x2_f[:, 6144:8192]]
        r.wait("dve", "coll", c_coll)
        r.wait("act", "coll", c_coll)
        crl = [0] * NC
        madd = [0] * NC

        def rank_load(rk):
            sem = "dwA" if rk % 2 == 0 else "dwB"
            r.dma("act", sem, stg[rk % 2], st_all[rk * 128:(rk + 1) * 128, :])
            crl[rk] = r.cnt[sem]

        rank_load(0)
        rank_load(1)
        for rk in range(NC):
            r.wait("dve", "dwA" if rk % 2 == 0 else "dwB", crl[rk])
            if rk == 0:
                dve(lambda e: e.tensor_scalar(
                    out=acc_ab[1], in0=stg[0], scalar1=pmask[:, 0:1],
                    scalar2=None, op0=ALU.mult))
            else:
                dve(lambda e, rk=rk: e.scalar_tensor_tensor(
                    out=acc_ab[(rk + 1) % 2], in0=stg[rk % 2],
                    scalar=pmask[:, rk:rk + 1], in1=acc_ab[rk % 2],
                    op0=ALU.mult, op1=ALU.add))
            madd[rk] = r.cnt["dve"]
            if rk + 2 < NC:
                r.wait("act", "dve", madd[rk])
                rank_load(rk + 2)
        fin = acc_ab[NC % 2]
        dve(lambda e: e.tensor_copy(out=kh_f[0:64, :], in_=fin[0:64, 0:1024]))
        dve(lambda e: e.tensor_copy(out=vh_f[0:64, :], in_=fin[64:128, 0:2048]))
        dve(lambda e: e.tensor_copy(out=s_st[0:1, :], in_=fin[0:1, 1024:2048]))
        for bb in (0, 64):
            dve(lambda e, bb=bb: e.tensor_copy(out=kh_b[bb:bb + 64, :],
                                               in_=kh_f[0:64, :]))
            dve(lambda e, bb=bb: e.tensor_copy(out=vh_b[bb:bb + 64, :],
                                               in_=vh_f[0:64, :]))
        dve(lambda e: e.tensor_copy(out=s_hl[0:1, :], in_=s_st[0:1, :]))
        dve(lambda e: e.tensor_tensor(out=s_lo[0:1, :], in0=s_st[0:1, :],
                                      in1=s_hl[0:1, :], op=ALU.subtract))
        c_state = r.cnt["dve"]
        r.wait("pool", "dve", c_state)
        c_slo = r.dma("pool", "slo", s_hl[1:2, :], s_lo[0:1, :])

        q_fm = ar(oA["q_fm"], 4096)
        k_fm = ar(oA["k_fm"], 4096)
        p_tm = ar(oA["p_tm"], 4096)
        v_tm = ar(oA["v_tm"], 8192)
        sgb = ar(oA["sg"], 8192)
        ogT = ar(oA["ogT"], 8192)
        xfb = xf_pin[:, :]
        ktb = ar(oA["kt"], 1024)
        p_fmb = ar(oA["p_fm"], 1024)
        pvn_tm = ar(oA["pvn_tm"], 1024)
        pvn_fm = ar(oA["pvn_fm"], 1024)
        oks = ar(oA["oks"], 1024, F32)
        sggb = ar(oA["oks"], 2048)
        pexp = ar(oA["pexp"], 1024, F32)
        ogtm = ar(oA["pexp"], 2048)
        rcs = ar(oA["rcs"], 1024, F32)
        osq = ar(oA["osq"], 512, F32)
        osq2 = ar(oA["osq2"], 512, F32)
        skq = [ar(oA["skm"] + 256 * i, 128) for i in range(2)] + \
              [ar(oA["pvm"] + 256 * i, 128) for i in range(2)]
        sclip = ar(oA["sclip"], 1024, F32)
        xrb = ar(oA["xr"], 2048, F32)
        thg = ar(oA["thg"], 1024)
        h_sr = ar(oF["h_sr"], 32768)
        x2fm = ar(oF["x2fm"], 8192)
        x2bf = ar(oF["x2bf"], 2048)
        rbuf = ar(oF["rbuf"], 1024)
        hsqb = ar(oF["hsq"], 1024)
        hsq4 = ar(oF["rbuf"], 2048)
        r4 = ar(86016, 1024)
        ssq_acc = ar(88064, 16, F32)
        xr2a = ar(oF["xr2a"], 4096, F32)
        xr2b = ar(oF["xr2b"], 1024, F32)
        x3b = ar(oF["x3"], 2048, F32)

        g_arena = q_done_all
        r.wait("pe", "ps", g_arena)
        g_tile = pe_gate()
        g_xf = g_tile
        c_dve_state = c_state
        r.wait("sync", "psA", c_conv_all[0])
        r.wait("sync", "psB", c_conv_all[1])

        NPH = {"wq": (KD, q_fm), "wk": (KD, k_fm)}

        c_ln2 = c_state
        for t in range(NT):
            r.mark(f"tile{t}_start")
            r.wait("act", "dout", r.cnt["dout"])
            r.wait("act", "dve", c_ln2)
            if t == 0:
                if isinstance(g_xf, tuple):
                    r.wait("pool", g_xf[0], g_xf[1])
                else:
                    r.wait("pool", "pe", g_xf)
                r.dma("pool", "pw",
                      xfb.rearrange("p (j t) -> p j t", t=512),
                      xT_in[:, t * 512:(t + 1) * 512]
                      .rearrange("(j p) t -> p j t", p=128))
                c_xf = r.cnt["pw"]
            else:
                c_xf = c_xf_pre
            r.wait("pe", "pw", c_xf)

            r.mark(f"tile{t}_proj")
            def proj_comp(i, slot, dest, scale, nb):
                for jj in range(2):
                    j = i * 2 + jj
                    for n in range(8):
                        mm(bank(n), slot[:, jj * 1024 + n * 128:
                                         jj * 1024 + (n + 1) * 128],
                           xfb[:, j * 512:(j + 1) * 512],
                           start=(j == 0), stop=(j == NJ - 1))
                return pe_mark()

            g_tile = stream(
                wstr, 2048,
                [wsc["wq"][p * 256:(p + 1) * 256, :]
                 .rearrange("(c p) n -> p c n", p=128) for p in range(8)],
                lambda i, slot: proj_comp(i, slot, q_fm, 0.125, 8),
                g_tile)
            c_qp = r.cnt["pe"]
            r.wait("dve", "pe", c_qp)
            for n in range(8):
                dve(lambda e, n=n: e.tensor_scalar(
                    out=q_fm[:, n * 512:(n + 1) * 512], in0=bank(n),
                    scalar1=0.125, scalar2=None, op0=ALU.mult))
            c_qe = r.cnt["dve"]
            r.wait("pe", "dve", c_qe)

            g_tile = stream(
                wstr, 2048,
                [wsc["wk"][p * 256:(p + 1) * 256, :]
                 .rearrange("(c p) n -> p c n", p=128) for p in range(8)],
                lambda i, slot: proj_comp(i, slot, k_fm, 1.0, 8),
                g_tile)
            c_kp = r.cnt["pe"]
            r.wait("dve", "pe", c_kp)
            for n in range(8):
                dve(lambda e, n=n: e.tensor_copy(
                    out=k_fm[:, n * 512:(n + 1) * 512], in_=bank(n)))
            c_ke = r.cnt["dve"]
            r.wait("pe", "dve", c_ke)

            if t == 0:
                dump("q_fm", q_fm[:, :], "dve", r.cnt["dve"])
                dump("k_fm", k_fm[:, :], "dve", r.cnt["dve"])

            def s_comp(i, slot):
                for jj in range(2):
                    j = i * 2 + jj
                    wsl = slot[:, jj * 1024:(jj + 1) * 1024]
                    for tc in range(NTC):
                        lh = xfb[:, j * 512 + tc * 128:j * 512 + (tc + 1) * 128]
                        for hf in range(2):
                            mm(bank(tc * 2 + hf), lh,
                               wsl[:, hf * 512:(hf + 1) * 512],
                               start=(j == 0), stop=(j == NJ - 1))
                return pe_mark()

            g_tile = stream(
                wstr, 2048,
                [wsc["ws"][p * 256:(p + 1) * 256, :]
                 .rearrange("(c p) n -> p c n", p=128) for p in range(8)],
                s_comp, g_tile)
            c_sp3 = r.cnt["pe"]
            for tc in range(NTC):
                r.wait("dve", "pe", c_sp3)
                if tc > 0:
                    r.wait("dve", "act", r.cnt["act"])
                for hf in range(2):
                    dve(lambda e, tc=tc, hf=hf: e.tensor_scalar(
                        out=sclip[:, hf * 512:(hf + 1) * 512],
                        in0=bank(tc * 2 + hf), scalar1=32.0, scalar2=-32.0,
                        op0=ALU.min, op1=ALU.max))
                c_cl = r.cnt["dve"]
                r.wait("act", "dve", c_cl)
                act(lambda e, tc=tc: e.activation(
                    out=p_tm[:, tc * 1024:(tc + 1) * 1024], in_=sclip[:, :],
                    func=AF.Exp))
            c_pex3 = r.cnt["act"]
            r.wait("pe", "act", c_pex3)
            if t == 0:
                dump("p_tm", p_tm[:, :], "act", c_pex3)

            for wname, dest, do_g in (("wv", v_tm, False), ("wg", None, True)):
                for half in range(2):
                    def vg_comp(i, slot, half=half):
                        for jj in range(2):
                            j = i * 2 + jj
                            wsl = slot[:, jj * 1024:(jj + 1) * 1024]
                            for tc in range(NTC):
                                lh = xfb[:, j * 512 + tc * 128:
                                         j * 512 + (tc + 1) * 128]
                                for sub in range(2):
                                    mm(bank(tc * 2 + sub), lh,
                                       wsl[:, sub * 512:(sub + 1) * 512],
                                       start=(j == 0), stop=(j == NJ - 1))
                        return pe_mark()

                    g_tile = stream(
                        wstr, 2048,
                        [wsc[wname][p * 256:(p + 1) * 256,
                                    half * 1024:(half + 1) * 1024]
                         .rearrange("(c p) n -> p c n", p=128)
                         for p in range(8)],
                        vg_comp, g_tile)
                    c_vg = r.cnt["pe"]
                    if not do_g:
                        r.wait("dve", "pe", c_vg)
                        for tc in range(NTC):
                            for sub in range(2):
                                dve(lambda e, tc=tc, sub=sub, half=half:
                                    e.tensor_copy(
                                        out=v_tm[:, tc * 2048 + half * 1024 +
                                                 sub * 512:tc * 2048 +
                                                 half * 1024 + (sub + 1) * 512],
                                        in_=bank(tc * 2 + sub)))
                        c_ve = r.cnt["dve"]
                        r.wait("pe", "dve", c_ve)
                    else:
                        r.wait("act", "pe", c_vg)
                        c_sg_dve = r.cnt["dve"]
                        for tc in range(NTC):
                            for sub in range(2):
                                act(lambda e, tc=tc, sub=sub: e.activation(
                                    out=thg[:, 0:512], in_=bank(tc * 2 + sub),
                                    func=AF.Tanh, scale=0.5))
                                act(lambda e, tc=tc, sub=sub: e.activation(
                                    out=thg[:, 512:1024],
                                    in_=bank(tc * 2 + sub), func=AF.Copy))
                                c_thg = r.cnt["act"]
                                r.wait("dve", "act", c_thg)
                                dve(lambda e, tc=tc, sub=sub, half=half:
                                    e.scalar_tensor_tensor(
                                        out=sgb[:, tc * 2048 + half * 1024 +
                                                sub * 512:tc * 2048 +
                                                half * 1024 + (sub + 1) * 512],
                                        in0=thg[:, 0:512], scalar=1.0,
                                        in1=thg[:, 512:1024], op0=ALU.add,
                                        op1=ALU.mult))
                                r.wait("act", "dve", r.cnt["dve"])
                        c_ge = r.cnt["dve"]
                        r.wait("pe", "dve", c_ge)

            if t == 0:
                dump("v_tm", v_tm[:, :], "act", r.cnt["act"])
                dump("sgb", sgb[:, :], "dve", r.cnt["dve"])

            r.mark(f"tile{t}_attn")
            fcsrc = [fcq_s[ip * 4:(ip + 1) * 4, :, hb * 512:(hb + 1) * 512]
                     .rearrange("i p h -> p i h")
                     for hb in range(16) for ip in range(4)]
            fc_pre = (slot_load(wffn, 2048, fcsrc, 0, g_tile, "pe", "dfA"),
                      slot_load(wffn, 2048, fcsrc, 1, g_tile, "pe", "dfB"))
            wosrc = [wsc["wo"][p * 128:(p + 1) * 128, :] for p in range(NJ)]
            wo_pre = (slot_load(wstr, 2048, wosrc, 0, g_tile, "pe", "dgA"),
                      slot_load(wstr, 2048, wosrc, 1, g_tile, "pe", "dgB"))
            r.wait("pool", "pw", r.cnt["pw"])
            r.wait("pool", "dve", c_ln2)
            for tc in range(NTC):
                r.dma("pool", "pw", x2_f[:, tc * 2048:(tc + 1) * 2048],
                      x_in[(t * 4 + tc) * 128:(t * 4 + tc + 1) * 128, :])
            c_xres = r.cnt["pw"]
            c_s_upd = c_dve_state
            c_kh_upd = c_dve_state
            c_vh_upd = c_dve_state
            c_kh_upd_act = 0
            c_vh_upd_act = 0
            c_og_prev = 0
            g_sqred = [0, 0]
            for tc in range(NTC):
                r.mark(f"tile{t}_ch{tc}_A")
                r.wait("pe", "dve", max(c_s_upd, c_og_prev))
                r.wait("pe", "slo", c_slo)
                for hf in range(2):
                    mm(bank(6 + hf), ones2[:, :],
                       s_hl[0:2, hf * 512:(hf + 1) * 512], True, False)
                    mm(bank(6 + hf), triu[:, :],
                       p_tm[:, tc * 1024 + hf * 512:tc * 1024 + (hf + 1) * 512],
                       False, True)
                c_sfull = pe_mark()
                r.wait("act", "pe", c_sfull)
                act(lambda e: e.copy(out=oks[:, :], in_=pB[:, 1024:2048]))
                c_sfc = r.cnt["act"]
                r.wait("dve", "act", c_sfc)
                dve(lambda e: e.reciprocal(out=rcs[:, :], in_=oks[:, :]))
                r.wait("pool", "act", c_sfc)
                c_row = r.dma("pool", "slo", s_st[0:1, :], oks[127:128, :])
                r.wait("dve", "slo", c_row)
                dve(lambda e: e.tensor_copy(out=s_hl[0:1, :], in_=s_st[0:1, :]))
                dve(lambda e: e.tensor_tensor(out=s_lo[0:1, :],
                                              in0=s_st[0:1, :],
                                              in1=s_hl[0:1, :],
                                              op=ALU.subtract))
                c_s_upd = r.cnt["dve"]
                r.wait("pool", "dve", c_s_upd)
                c_slo = r.dma("pool", "slo", s_hl[1:2, :], s_lo[0:1, :])

                r.wait("pe", "dve", c_vh_upd)
                for n in range(8):
                    tr(pA[:, 1024 + n * 64:1024 + (n + 1) * 64].bitcast(BF16),
                       k_fm[:, n * 512 + tc * 128:n * 512 + (tc + 1) * 128])
                c_trk = pe_mark()
                r.wait("act", "pe", c_trk)
                act(lambda e: e.copy(
                    out=ktb[:, :], in_=pA[:, 1024:1536].bitcast(BF16)))
                c_kt = r.cnt["act"]
                r.wait("pe", "act", c_kt)
                for n in range(8):
                    tr(pA[:, 1024 + n * 64:1024 + (n + 1) * 64].bitcast(BF16),
                       p_tm[:, tc * 1024 + n * 128:tc * 1024 + (n + 1) * 128])
                c_trp = pe_mark()
                r.wait("act", "pe", c_trp)
                act(lambda e: e.copy(
                    out=p_fmb[:, :], in_=pA[:, 1024:1536].bitcast(BF16)))
                c_pfm = r.cnt["act"]
                r.wait("pe", "act", c_pfm)

                sk_m = [0] * H
                msk_m = [0] * H
                ok_m = [0] * H

                def emit_ok(h):
                    b0 = (h % 2) * 64
                    tn = h // 2
                    r.wait("pe", "dve", msk_m[h])
                    if h == 0:
                        r.wait("pe", "dve", c_kh_upd)
                        r.wait("pe", "act", c_kh_upd_act)
                    okb = bank(6 + h // 8)[:, (h % 8) * 64:(h % 8) * 64 + 64]
                    mm(okb, q_fm[b0:b0 + 64, tn * 512 + tc * 128:
                                 tn * 512 + (tc + 1) * 128],
                       kh_b[b0:b0 + 64, h * 64:(h + 1) * 64], True, False)
                    mm(okb, skq[h % 4][:, :],
                       p_tm[:, tc * 1024 + h * 64:tc * 1024 + (h + 1) * 64],
                       False, True)
                    ok_m[h] = r.cnt["pe"]

                r.wait("pe", "dve", c_kh_upd)
                for h in range(H):
                    b0 = (h % 2) * 64
                    tn = h // 2
                    mm(bank(h % 4)[:, 0:128],
                       k_fm[b0:b0 + 64, tn * 512 + tc * 128:
                            tn * 512 + (tc + 1) * 128],
                       q_fm[b0:b0 + 64, tn * 512 + tc * 128:
                            tn * 512 + (tc + 1) * 128],
                       True, True)
                    sk_m[h] = r.cnt["pe"]
                    r.wait("dve", "pe", sk_m[h])
                    if h >= 4:
                        r.wait("dve", "pe", ok_m[h - 4])
                    dve(lambda e, h=h: e.tensor_tensor(
                        out=skq[h % 4][:, :], in0=bank(h % 4)[:, 0:128],
                        in1=triu[:, :], op=ALU.mult))
                    msk_m[h] = r.cnt["dve"]
                    if h >= 3:
                        emit_ok(h - 3)
                for h in range(H - 3, H):
                    emit_ok(h)
                c_ok = pe_mark()

                r.wait("pe", "dve", msk_m[H - 1])
                for h in range(H):
                    mm(bank(h // 8, 0, 64)[:, (h % 8) * 64:(h % 8) * 64 + 64],
                       ktb[:, h * 64:(h + 1) * 64],
                       p_tm[:, tc * 1024 + h * 64:tc * 1024 + (h + 1) * 64],
                       True, True)
                c_kst = pe_mark()

                r.wait("dve", "pe", c_ok)
                r.wait("dve", "slo", c_row)
                dve(lambda e: e.tensor_tensor(out=oks[:, :],
                                              in0=pB[:, 1024:2048],
                                              in1=rcs[:, :], op=ALU.mult))
                dve(lambda e: e.reduce_max(
                    out=sm[0][:, 0:16],
                    in_=oks[:, :].rearrange("p (h m) -> p h m", m=64),
                    axis=AX.X))
                dve(lambda e: e.tensor_scalar(out=sm[1][:, 0:16],
                                              in0=sm[0][:, 0:16], scalar1=-1.0,
                                              scalar2=None, op0=ALU.mult))
                for h in range(H):
                    dve(lambda e, h=h: e.tensor_scalar(
                        out=pexp[:, h * 64:(h + 1) * 64],
                        in0=oks[:, h * 64:(h + 1) * 64],
                        scalar1=sm[1][:, h:h + 1], scalar2=None, op0=ALU.add))
                c_msub = r.cnt["dve"]
                r.wait("act", "dve", c_msub)
                act(lambda e: e.activation(out=oks[:, :], in_=pexp[:, :],
                                           func=AF.Exp))
                c_exp = r.cnt["act"]
                r.wait("dve", "act", c_exp)
                dve(lambda e: e.reduce_sum(
                    out=sm[2][:, 0:16],
                    in_=oks[:, :].rearrange("p (h m) -> p h m", m=64),
                    axis=AX.X))
                dve(lambda e: e.reciprocal(out=sm[3][:, 0:16],
                                           in_=sm[2][:, 0:16]))
                for h in range(H):
                    dve(lambda e, h=h: e.tensor_scalar(
                        out=pexp[:, h * 64:(h + 1) * 64],
                        in0=rcs[:, h * 64:(h + 1) * 64],
                        scalar1=sm[3][:, h:h + 1], scalar2=None, op0=ALU.mult))
                dve(lambda e: e.tensor_tensor(out=pvn_tm[:, :], in0=oks[:, :],
                                              in1=pexp[:, :], op=ALU.mult))
                c_pvn = r.cnt["dve"]
                if t == 0 and tc == 0:
                    dump("rcs0", rcs[:, :], "dve", c_pvn)
                    dump("pvn0", pvn_tm[:, :], "dve", c_pvn)
                r.wait("dve", "pe", c_kst)
                for i in range(2):
                    dve(lambda e, i=i: e.tensor_tensor(
                        out=kh_f[0:64, i * 512:(i + 1) * 512],
                        in0=bank(i, 0, 64),
                        in1=kh_f[0:64, i * 512:(i + 1) * 512], op=ALU.add))
                c_khf = r.cnt["dve"]
                c_kh_upd = c_khf
                r.wait("act", "dve", c_khf)
                for bb in (0, 64):
                    act(lambda e, bb=bb: e.copy(
                        out=kh_b[bb:bb + 64, :], in_=kh_f[0:64, :]))
                c_kh_upd_act = r.cnt["act"]

                r.wait("pe", "dve", c_pvn)
                for n in range(8):
                    tr(pA[:, 1024 + n * 64:1024 + (n + 1) * 64].bitcast(BF16),
                       pvn_tm[:, n * 128:(n + 1) * 128])
                c_trpv = pe_mark()
                r.wait("act", "pe", c_trpv)
                act(lambda e: e.copy(
                    out=pvn_fm[:, :], in_=pA[:, 1024:1536].bitcast(BF16)))
                c_pvf = r.cnt["act"]
                r.wait("pe", "act", c_pvf)

                pv_m = [0] * H
                mpv_m = [0] * H
                ovm = [0] * H

                def emit_ov(h):
                    b0 = (h % 2) * 64
                    r.wait("pe", "dve", mpv_m[h])
                    if h == 0:
                        r.wait("pe", "dve", c_vh_upd)
                        r.wait("pe", "act", c_vh_upd_act)
                    ovb = bank(4 + h // 4)[:, (h % 4) * 128:(h % 4) * 128 + 128]
                    mm(ovb, pvn_fm[b0:b0 + 64,
                                   (h // 2) * 128:(h // 2) * 128 + 128],
                       vh_b[b0:b0 + 64, h * 128:(h + 1) * 128], True, False)
                    mm(ovb, skq[h % 4][:, :],
                       v_tm[:, tc * 2048 + h * 128:tc * 2048 + (h + 1) * 128],
                       False, True)
                    ovm[h] = r.cnt["pe"]

                r.wait("pe", "dve", c_kh_upd)
                for h in range(H):
                    b0 = (h % 2) * 64
                    mm(bank(h % 4)[:, 0:128],
                       p_fmb[b0:b0 + 64, (h // 2) * 128:(h // 2) * 128 + 128],
                       pvn_fm[b0:b0 + 64, (h // 2) * 128:(h // 2) * 128 + 128],
                       True, True)
                    pv_m[h] = r.cnt["pe"]
                    r.wait("dve", "pe", pv_m[h])
                    if h >= 4:
                        r.wait("dve", "pe", ovm[h - 4])
                    dve(lambda e, h=h: e.tensor_tensor(
                        out=skq[h % 4][:, :], in0=bank(h % 4)[:, 0:128],
                        in1=triu[:, :], op=ALU.mult))
                    mpv_m[h] = r.cnt["dve"]
                    if h >= 3:
                        emit_ov(h - 3)
                for h in range(H - 3, H):
                    emit_ov(h)
                c_ov = pe_mark()

                r.wait("act", "pe", c_ov)
                for bq in range(4):
                    r.wait("act", "dve", g_sqred[bq % 2])
                    act(lambda e, bq=bq: e.activation(
                        out=osq[:, :] if bq % 2 == 0 else osq2[:, :],
                        in_=bank(4 + bq), func=AF.Square))
                    c_sqa = r.cnt["act"]
                    r.wait("dve", "act", c_sqa)
                    dve(lambda e, bq=bq: e.reduce_sum(
                        out=sm[4][:, bq * 4:(bq + 1) * 4],
                        in_=(osq if bq % 2 == 0 else osq2)[:, :]
                        .rearrange("p (h v) -> p h v", v=128),
                        axis=AX.X))
                    g_sqred[bq % 2] = r.cnt["dve"]
                dve(lambda e: e.tensor_scalar(
                    out=sm[5][:, 0:16], in0=sm[4][:, 0:16], scalar1=1.0 / DV,
                    scalar2=EPS, op0=ALU.mult, op1=ALU.add))
                rsqrt_rec(sm[5][:, 0:16], sm[6][:, 0:16], sm[0][:, 0:16],
                          sm[1][:, 0:16])
                dve(lambda e, tc=tc: e.tensor_tensor(
                    out=sggb[:, :], in0=sgb[:, tc * 2048:(tc + 1) * 2048],
                    in1=gnw_h[:, :], op=ALU.mult))
                for h in range(H):
                    dve(lambda e, h=h: e.scalar_tensor_tensor(
                        out=ogtm[:, h * 128:(h + 1) * 128],
                        in0=bank(4 + h // 4)[:, (h % 4) * 128:
                                             (h % 4) * 128 + 128],
                        scalar=sm[6][:, h:h + 1],
                        in1=sggb[:, h * 128:(h + 1) * 128],
                        op0=ALU.mult, op1=ALU.mult))
                c_og = r.cnt["dve"]
                c_og_prev = c_og
                if t == 0 and tc == 0:
                    dump("ogtm0", ogtm[:, :], "dve", c_og)
                r.wait("pe", "dve", c_og)
                for grp in range(2):
                    for n in range(8):
                        j = grp * 8 + n
                        tr(pA[:, 1024 + n * 64:1024 + (n + 1) * 64]
                           .bitcast(BF16), ogtm[:, j * 128:(j + 1) * 128])
                    c_trog = pe_mark()
                    r.wait("act", "pe", c_trog)
                    for n in range(8):
                        j = grp * 8 + n
                        act(lambda e, j=j, n=n, tc=tc: e.copy(
                            out=ogT[:, j * 512 + tc * 128:
                                    j * 512 + (tc + 1) * 128],
                            in_=pA[:, 1024 + n * 64:1024 + (n + 1) * 64]
                            .bitcast(BF16)))
                    if grp == 0:
                        r.wait("pe", "act", r.cnt["act"])
                c_ogT = r.cnt["act"]
                r.wait("pe", "act", c_ogT)

                for h in range(H):
                    mm(bank(2 + h // 4, 0, 64)
                       [:, (h % 4) * 128:(h % 4) * 128 + 128],
                       p_tm[:, tc * 1024 + h * 64:tc * 1024 + (h + 1) * 64],
                       v_tm[:, tc * 2048 + h * 128:tc * 2048 + (h + 1) * 128],
                       True, True)
                c_su = pe_mark()
                r.wait("dve", "pe", c_su)
                for i in range(4):
                    dve(lambda e, i=i: e.tensor_tensor(
                        out=vh_f[0:64, i * 512:(i + 1) * 512],
                        in0=bank(2 + i, 0, 64),
                        in1=vh_f[0:64, i * 512:(i + 1) * 512], op=ALU.add))
                c_vhf = r.cnt["dve"]
                r.wait("act", "dve", c_vhf)
                for bb in (0, 64):
                    act(lambda e, bb=bb: e.copy(
                        out=vh_b[bb:bb + 64, :], in_=vh_f[0:64, :]))
                c_vh_upd_act = r.cnt["act"]
                c_vh_upd = r.cnt["dve"]
                c_dve_state = r.cnt["dve"]

            r.mark(f"tile{t}_wo")
            r.wait("pe", "dve", c_vh_upd)
            r.wait("pe", "pw", c_xres)
            g_xf = pe_gate()
            if t + 1 < NT:
                r.wait("pool", g_xf[0], g_xf[1])
                r.dma("pool", "pw",
                      xfb.rearrange("p (j t) -> p j t", t=512),
                      xT_in[:, (t + 1) * 512:(t + 2) * 512]
                      .rearrange("(j p) t -> p j t", p=128))
                c_xf_pre = r.cnt["pw"]
            c_x2 = [0] * NTC
            for hp in range(2):
                def wo_comp(i, slot, hp=hp):
                    for tcl in range(2):
                        tc = hp * 2 + tcl
                        for piece in range(4):
                            mm(bank(tcl * 4 + piece),
                               ogT[:, i * 512 + tc * 128:
                                   i * 512 + (tc + 1) * 128],
                               slot[:, piece * 512:(piece + 1) * 512],
                               start=(i == 0), stop=(i == NJ - 1))
                    return pe_mark()

                g_tile2 = stream(
                    wstr, 2048, wosrc,
                    wo_comp, g_tile if hp == 0 else g_tile2,
                    pre=wo_pre if hp == 0 else None)
                c_wo = r.cnt["pe"]
                for tcl in range(2):
                    tc = hp * 2 + tcl
                    r.wait("dve", "pe", c_wo)
                    for piece in range(4):
                        dve(lambda e, tc=tc, tcl=tcl, piece=piece:
                            e.tensor_tensor(
                                out=xrb[:, piece * 512:(piece + 1) * 512],
                                in0=bank(tcl * 4 + piece),
                                in1=x2_f[:, tc * 2048 + piece * 512:
                                         tc * 2048 + (piece + 1) * 512],
                                op=ALU.add))
                    if t == 0 and tc == 0:
                        dump("xrb0", xrb[:, :], "dve", r.cnt["dve"])
                    dve(lambda e: e.reduce_sum(out=sm[0][:, 0:1],
                                               in_=xrb[:, :], axis=AX.X))
                    dve(lambda e: e.tensor_scalar(
                        out=sm[1][:, 0:1], in0=sm[0][:, 0:1],
                        scalar1=-1.0 / D, scalar2=None, op0=ALU.mult))
                    dve(lambda e, tc=tc: e.tensor_scalar(
                        out=x2_f[:, tc * 2048:(tc + 1) * 2048], in0=xrb[:, :],
                        scalar1=sm[1][:, 0:1], scalar2=None, op0=ALU.add))
                    dve(lambda e, tc=tc: e.tensor_tensor(
                        out=xrb[:, :], in0=x2_f[:, tc * 2048:(tc + 1) * 2048],
                        in1=x2_f[:, tc * 2048:(tc + 1) * 2048], op=ALU.mult))
                    dve(lambda e: e.reduce_sum(out=sm[2][:, 0:1],
                                               in_=xrb[:, :], axis=AX.X))
                    dve(lambda e: e.tensor_scalar(
                        out=sm[3][:, 0:1], in0=sm[2][:, 0:1], scalar1=1.0 / D,
                        scalar2=EPS, op0=ALU.mult, op1=ALU.add))
                    rsqrt_rec(sm[3][:, 0:1], sm[4][:, 0:1], sm[0][:, 0:1],
                              sm[1][:, 0:1])
                    dve(lambda e, tc=tc: e.tensor_scalar(
                        out=x2_f[:, tc * 2048:(tc + 1) * 2048],
                        in0=x2_f[:, tc * 2048:(tc + 1) * 2048],
                        scalar1=sm[4][:, 0:1], scalar2=None, op0=ALU.mult))
                    c_x2[tc] = r.cnt["dve"]
                    r.wait("sync", "dve", c_x2[tc])

            r.mark(f"tile{t}_x2cast")
            c_trx = 0
            for tc in range(NTC):
                if tc > 0:
                    r.wait("dve", "pe", c_trx)
                dve(lambda e, tc=tc: e.tensor_copy(
                    out=x2bf[:, :], in_=x2_f[:, tc * 2048:(tc + 1) * 2048]))
                c_cast = r.cnt["dve"]
                r.wait("pe", "dve", c_cast)
                for grp in range(2):
                    for n in range(8):
                        j = grp * 8 + n
                        tr(pA[:, 1024 + n * 64:1024 + (n + 1) * 64]
                           .bitcast(BF16), x2bf[:, j * 128:(j + 1) * 128])
                    c_trx = pe_mark()
                    r.wait("act", "pe", c_trx)
                    for n in range(8):
                        j = grp * 8 + n
                        act(lambda e, j=j, n=n, tc=tc: e.copy(
                            out=x2fm[:, j * 512 + tc * 128:
                                     j * 512 + (tc + 1) * 128],
                            in_=pA[:, 1024 + n * 64:1024 + (n + 1) * 64]
                            .bitcast(BF16)))
                    if grp == 0:
                        r.wait("pe", "act", r.cnt["act"])
                c_x2f = r.cnt["act"]
                r.wait("pe", "act", c_x2f)

            r.mark(f"tile{t}_fc")

            fcb_act = [0] * 16
            fcb_dve = [0] * 16
            fcb_ssqm = [0] * 16
            fcb_acc = [0] * 16

            def emit_fc_tail(hb):
                B0 = (hb % 2) * 4
                m = pe_mark()
                r.wait("act", "pe", m)
                r.wait("dve", "pe", m)
                if hb >= 1:
                    r.wait("dve", "pe", fcb_ssqm[hb - 1])
                    r.wait("act", "dve", fcb_dve[hb - 1])
                for hs in range(4):
                    ht = hb * 4 + hs
                    rsl = r4[:, (hs % 2) * 512:(hs % 2) * 512 + 512]
                    if hs % 2 == 0:
                        if hs == 2:
                            r.wait("act", "dve", r.cnt["dve"])
                        act(lambda e, B0=B0, hs=hs, rsl=rsl: e.activation(
                            out=rsl, in_=bank(B0 + hs), func=AF.Relu))
                        r.wait("dve", "act", r.cnt["act"])
                    else:
                        dve(lambda e, B0=B0, hs=hs, rsl=rsl: e.tensor_scalar(
                            out=rsl, in0=bank(B0 + hs), scalar1=0.0,
                            scalar2=None, op0=ALU.max))
                    dve(lambda e, ht=ht, rsl=rsl: e.tensor_tensor(
                        out=h_sr[:, ht * 512:(ht + 1) * 512],
                        in0=rsl, in1=rsl, op=ALU.mult))
                    dve(lambda e, ht=ht, hs=hs: e.tensor_tensor(
                        out=hsq4[:, hs * 512:(hs + 1) * 512],
                        in0=h_sr[:, ht * 512:(ht + 1) * 512],
                        in1=h_sr[:, ht * 512:(ht + 1) * 512], op=ALU.mult))
                fcb_dve[hb] = r.cnt["dve"]
                fcb_act[hb] = r.cnt["act"]

            def emit_fc_ssq(hb):
                B0 = (hb % 2) * 4
                r.wait("pe", "dve", fcb_dve[hb])
                for tcl in range(4):
                    for hs in range(4):
                        mm(bank(B0 + tcl)[:, 0:1],
                           hsq4[:, hs * 512 + tcl * 128:
                                hs * 512 + (tcl + 1) * 128],
                           onescol[:, :], start=(hs == 0), stop=(hs == 3))
                fcb_ssqm[hb] = pe_mark()
                r.wait("dve", "pe", fcb_ssqm[hb])
                for tcl in range(4):
                    if hb == 0:
                        dve(lambda e, B0=B0, tcl=tcl: e.tensor_copy(
                            out=ssq_acc[:, tcl:tcl + 1],
                            in_=bank(B0 + tcl)[:, 0:1]))
                    else:
                        dve(lambda e, B0=B0, tcl=tcl: e.tensor_tensor(
                            out=ssq_acc[:, tcl:tcl + 1],
                            in0=bank(B0 + tcl)[:, 0:1],
                            in1=ssq_acc[:, tcl:tcl + 1], op=ALU.add))
                fcb_acc[hb] = r.cnt["dve"]

            def fc_comp(n, slot):
                hb, ip = divmod(n, 4)
                B0 = (hb % 2) * 4
                if ip == 0 and hb >= 2:
                    r.wait("pe", "act", fcb_act[hb - 2])
                    r.wait("pe", "dve", fcb_dve[hb - 2])
                    r.wait("pe", "dve", fcb_acc[hb - 2])
                for jj in range(4):
                    j = ip * 4 + jj
                    for hs in range(4):
                        mm(bank(B0 + hs),
                           slot[:, jj * 512 + hs * 128:
                                jj * 512 + (hs + 1) * 128],
                           x2fm[:, j * 512:(j + 1) * 512],
                           start=(j == 0), stop=(j == NJ - 1))
                if ip == 1 and hb >= 1:
                    emit_fc_ssq(hb - 1)
                if ip == 3:
                    emit_fc_tail(hb)
                return pe_mark()

            g_tile = stream(wffn, 2048, fcsrc, fc_comp, g_tile,
                            pre=fc_pre, dual=True)
            emit_fc_ssq(15)
            c_fc = r.cnt["pe"]
            r.wait("dve", "pe", c_fc)
            for tcl in range(4):
                dve(lambda e, tcl=tcl: e.tensor_scalar(
                    out=sm[5][:, tcl:tcl + 1], in0=ssq_acc[:, tcl:tcl + 1],
                    scalar1=1.0 / HID, scalar2=EPS_FFN, op0=ALU.mult,
                    op1=ALU.add))
            rsqrt_rec(sm[5][:, 0:4], sm[7][:, 0:4], sm[0][:, 0:4],
                      sm[1][:, 0:4])
            c_rsf = r.cnt["dve"]
            r.wait("pe", "dve", c_rsf)

            r.mark(f"tile{t}_proj2")
            r.wait("pe", "act", fcb_act[15])
            r.wait("pe", "act", fcb_act[14])
            for dh in range(2):
                pjsrc = [projq_s[hh * 2:hh * 2 + 2, :,
                                 dh * 1024:(dh + 1) * 1024]
                         .rearrange("c p n -> p c n") for hh in range(32)]

                def pj_comp(i, slot, dh=dh):
                    for hl in range(2):
                        h2 = i * 2 + hl
                        for tc in range(NTC):
                            for piece in range(2):
                                mm(bank(tc * 2 + piece),
                                   h_sr[:, h2 * 512 + tc * 128:
                                        h2 * 512 + (tc + 1) * 128],
                                   slot[:, hl * 1024 + piece * 512:
                                        hl * 1024 + (piece + 1) * 512],
                                   start=(h2 == 0), stop=(h2 == 63))
                    return pe_mark()

                g_tile = stream(wffn, 2048, pjsrc, pj_comp, g_tile,
                                dual=True)
                c_pj = r.cnt["pe"]
                r.wait("dve", "pe", c_pj)
                if dh == 0:
                    for tc in range(NTC):
                        for piece in range(2):
                            dve(lambda e, tc=tc, piece=piece:
                                e.scalar_tensor_tensor(
                                    out=xr2a[:, tc * 1024 + piece * 512:
                                             tc * 1024 + (piece + 1) * 512],
                                    in0=bank(tc * 2 + piece),
                                    scalar=sm[7][:, tc:tc + 1],
                                    in1=x2_f[:, tc * 2048 + piece * 512:
                                             tc * 2048 + (piece + 1) * 512],
                                    op0=ALU.mult, op1=ALU.add))
                    c_eva = r.cnt["dve"]
                    r.wait("pe", "dve", c_eva)
                else:
                    xr2b4 = lambda tc: x2_f[:, tc * 2048:tc * 2048 + 1024]
                    for tc in range(NTC):
                        for piece in range(2):
                            dve(lambda e, tc=tc, piece=piece:
                                e.scalar_tensor_tensor(
                                    out=xr2b4(tc)[:, piece * 512:
                                                  (piece + 1) * 512],
                                    in0=bank(tc * 2 + piece),
                                    scalar=sm[7][:, tc:tc + 1],
                                    in1=x2_f[:, tc * 2048 + 1024 + piece * 512:
                                             tc * 2048 + 1024 +
                                             (piece + 1) * 512],
                                    op0=ALU.mult, op1=ALU.add))
                    c_evb = r.cnt["dve"]
                    r.wait("pe", "dve", c_evb)
                    for tc in range(NTC):
                        dve(lambda e, tc=tc: e.reduce_sum(
                            out=sm[0][:, 0:1],
                            in_=xr2a[:, tc * 1024:(tc + 1) * 1024], axis=AX.X))
                        dve(lambda e, tc=tc: e.reduce_sum(
                            out=sm[1][:, 0:1], in_=xr2b4(tc), axis=AX.X))
                        dve(lambda e: e.tensor_tensor(
                            out=sm[0][:, 0:1], in0=sm[0][:, 0:1],
                            in1=sm[1][:, 0:1], op=ALU.add))
                        dve(lambda e: e.tensor_scalar(
                            out=sm[1][:, 0:1], in0=sm[0][:, 0:1],
                            scalar1=-1.0 / D, scalar2=None, op0=ALU.mult))
                        dve(lambda e, tc=tc: e.tensor_scalar(
                            out=x3b[:, 0:1024],
                            in0=xr2a[:, tc * 1024:(tc + 1) * 1024],
                            scalar1=sm[1][:, 0:1], scalar2=None, op0=ALU.add))
                        dve(lambda e, tc=tc: e.tensor_scalar(
                            out=x3b[:, 1024:2048], in0=xr2b4(tc),
                            scalar1=sm[1][:, 0:1], scalar2=None, op0=ALU.add))
                        dve(lambda e, tc=tc: e.tensor_tensor(
                            out=xr2a[:, tc * 1024:(tc + 1) * 1024],
                            in0=x3b[:, 0:1024], in1=x3b[:, 0:1024],
                            op=ALU.mult))
                        dve(lambda e, tc=tc: e.tensor_tensor(
                            out=xr2b4(tc), in0=x3b[:, 1024:2048],
                            in1=x3b[:, 1024:2048], op=ALU.mult))
                        dve(lambda e, tc=tc: e.reduce_sum(
                            out=sm[0][:, 0:1],
                            in_=xr2a[:, tc * 1024:(tc + 1) * 1024], axis=AX.X))
                        dve(lambda e, tc=tc: e.reduce_sum(
                            out=sm[1][:, 0:1], in_=xr2b4(tc), axis=AX.X))
                        dve(lambda e: e.tensor_tensor(
                            out=sm[0][:, 0:1], in0=sm[0][:, 0:1],
                            in1=sm[1][:, 0:1], op=ALU.add))
                        dve(lambda e: e.tensor_scalar(
                            out=sm[2][:, 0:1], in0=sm[0][:, 0:1],
                            scalar1=1.0 / D, scalar2=EPS, op0=ALU.mult,
                            op1=ALU.add))
                        rsqrt_rec(sm[2][:, 0:1], sm[3][:, 0:1], sm[0][:, 0:1],
                                  sm[1][:, 0:1])
                        dve(lambda e: e.tensor_scalar(
                            out=x3b[:, 0:1024], in0=x3b[:, 0:1024],
                            scalar1=sm[3][:, 0:1], scalar2=None, op0=ALU.mult))
                        dve(lambda e: e.tensor_scalar(
                            out=x3b[:, 1024:2048], in0=x3b[:, 1024:2048],
                            scalar1=sm[3][:, 0:1], scalar2=None, op0=ALU.mult))
                        c_x3 = r.cnt["dve"]
                        r.wait("act", "dve", c_x3)
                        r.streams["act"].append(
                            ("i", lambda e, t=t, tc=tc: e.dma_start(
                                out=out_dram[(t * 4 + tc) * 128:
                                             (t * 4 + tc + 1) * 128, :],
                                in_=x3b[:, :]), "dout", 16))
                        r.cnt["dout"] += 16
                        r.wait("dve", "dout", r.cnt["dout"])
                    c_ln2 = r.cnt["dve"]

        r.wait("sync", "dout", 16 * 4 * NT)

        def runner(eng_name):
            selfsync = eng_name in ("dve", "act")

            def f(eng):
                own = 0
                last_wait = -1
                for it in r.streams[eng_name]:
                    if it[0] == "w":
                        eng.wait_ge(sems[it[1]], it[2])
                    else:
                        if selfsync and own > 0 and last_wait < own:
                            eng.wait_ge(sems[eng_name], own)
                            last_wait = own
                        ins = it[1](eng)
                        if it[2] is not None:
                            ins.then_inc(sems[it[2]], it[3])
                            if it[2] == eng_name:
                                own += it[3]
            return f

        block.tensor(runner("pe"))
        block.vector(runner("dve"))
        block.scalar(runner("act"))
        block.gpsimd(runner("pool"))
        block.sync(runner("sync"))

    return nc


def _consts():
    ident = np.eye(128, dtype=bfdt)
    triu = np.triu(np.ones((128, 128), dtype=np.float32)).astype(bfdt)
    onescol = np.ones((128, 1), dtype=bfdt)
    ones2 = np.ones((2, 128), dtype=bfdt)
    sel2 = np.zeros((2, 128), dtype=bfdt)
    sel2[0, :64] = 1
    sel2[1, 64:] = 1
    sel4 = np.zeros((4, 128), dtype=bfdt)
    sel4[0, :64] = 1
    sel4[1, 64:] = 1
    sel4[2, :64] = 1
    sel4[3, 64:] = 1
    selg = np.zeros((128, 2), dtype=bfdt)
    selg[:64, 0] = 1
    selg[64:, 1] = 1
    halfrow = np.full((1, 128), 0.5, dtype=bfdt)
    return ident, triu, onescol, ones2, sel2, sel4, selg, halfrow


def kernel(L0_attn_hidden_states, L0_attn_Wq, L0_attn_Wk, L0_attn_Wv,
           L0_attn_Wg, L0_attn_Ws, L0_attn_Wo, L0_attn_g_norm_weight,
           L0_ffn_W_fc, L0_ffn_W_proj, _nt=8):
    x = np.ascontiguousarray(np.asarray(L0_attn_hidden_states,
                                        dtype=np.float32).reshape(B * T, D))
    tcore = _nt * 512
    ident, triu, onescol, ones2, sel2, sel4, selg, halfrow = _consts()
    wfcT = np.ascontiguousarray(np.asarray(L0_ffn_W_fc, np.float32).T)
    wprojT = np.ascontiguousarray(np.asarray(L0_ffn_W_proj, np.float32).T)
    gnw = np.ascontiguousarray(
        np.tile(np.asarray(L0_attn_g_norm_weight, np.float32), H)[None, :])

    nc_prog = build(_nt)
    in_maps = []
    for c in range(NC):
        seq = c // (NC // B)
        t0 = c * tcore if _nt == 8 else (seq * (T if _nt == 8 else 0))
        if _nt == 8:
            rows = slice(c * 4096, (c + 1) * 4096)
        else:
            per_seq = NC // B
            within = c % per_seq
            base = seq * T + within * tcore
            rows = slice(base, base + tcore)
        xs = np.ascontiguousarray(x[rows])
        xTs = np.ascontiguousarray(xs.T)
        pm = np.zeros((128, 8), np.float32)
        per_seq = NC // B
        for rk in range((c // per_seq) * per_seq, c):
            pm[:, rk] = 1.0
        in_maps.append(dict(
            x=xs, xT=xTs,
            wq=np.asarray(L0_attn_Wq, np.float32),
            wk=np.asarray(L0_attn_Wk, np.float32),
            ws=np.asarray(L0_attn_Ws, np.float32),
            wv=np.asarray(L0_attn_Wv, np.float32),
            wg=np.asarray(L0_attn_Wg, np.float32),
            wo=np.asarray(L0_attn_Wo, np.float32),
            wfcT=wfcT, wprojT=wprojT, gnw=gnw,
            ident=ident, triu=triu, onescol=onescol, ones2=ones2,
            sel2=sel2, sel4=sel4, selg=selg, halfrow=halfrow, pmask=pm))
    res = run_bass_kernel_spmd(nc_prog, in_maps, core_ids=list(range(NC)))
    outs = [np.asarray(res.results[c]["out"]) for c in range(NC)]
    if _nt == 8:
        full = np.concatenate(outs, axis=0)
        return full.reshape(B, T, D)
    return outs


# version 12 (final)
# speedup vs baseline: 1.1223x; 1.1223x over previous
import numpy as np
import ml_dtypes
import concourse.bass as bass
import concourse.mybir as mybir
from concourse.bass_utils import run_bass_kernel_spmd
from contextlib import ExitStack

F32 = mybir.dt.float32
BF16 = mybir.dt.bfloat16
I32 = mybir.dt.int32
AF = mybir.ActivationFunctionType
ALU = mybir.AluOpType
AX = mybir.AxisListType
bfdt = ml_dtypes.bfloat16

B, T, D = 2, 16384, 2048
H, DK, DV, M = 16, 64, 128, 64
KD, HM, VD, HID = 1024, 1024, 2048, 8192
NC = 8
EPS = 1e-5
EPS_FFN = float(np.finfo(np.float32).eps)
MAGIC = 0x5F3759DF
QTHR = 0.5 + 2.0 ** -10

ENGS = ("pe", "dve", "act", "pool", "sync")
SEMS = ("dw", "dwA", "dwB", "dfA", "dfB", "dgA", "dgB", "dx", "ds", "pw", "pwA", "pwB", "ps", "psA", "psB", "slo", "dbg", "pe", "gate", "dve", "act", "pool", "dout", "coll")


PHASELOG = []


class Rec:
    def __init__(self):
        self.streams = {e: [] for e in ENGS}
        self.cnt = {s: 0 for s in SEMS}

    def mark(self, label):
        PHASELOG.append((label, dict(self.cnt),
                         {e: len(s) for e, s in self.streams.items()}))

    def wait(self, eng, sem, val):
        if val > 0:
            self.streams[eng].append(("w", sem, val))

    def emit(self, eng, fn, sem=None, n=1):
        if sem is not None:
            self.cnt[sem] += n
        self.streams[eng].append(("i", fn, sem, n))
        return self.cnt[sem] if sem else None

    def dma(self, eng, sem, out, in_):
        self.cnt[sem] += 16
        self.streams[eng].append(
            ("i", lambda e, out=out, in_=in_: e.dma_start(out=out, in_=in_),
             sem, 16))
        return self.cnt[sem]


def build(NT=8, dbg=False):
    TILE, NTC, NJ = 512, 4, 16
    TCORE = NT * TILE
    NHFC = HID // 128

    nc = bass.Bass("TRN2", target_bir_lowering=False, debug=False)
    dr = lambda n, s, d, k="ExternalInput": nc.dram_tensor(n, s, d, kind=k).ap()
    x_in = dr("x", [TCORE, D], F32)
    xT_in = dr("xT", [D, TCORE], F32)
    w_in = {"wq": dr("wq", [D, KD], F32), "wk": dr("wk", [D, KD], F32),
            "ws": dr("ws", [D, HM], F32), "wv": dr("wv", [D, VD], F32),
            "wg": dr("wg", [D, VD], F32), "wo": dr("wo", [VD, D], F32)}
    wfcT_in = dr("wfcT", [D, HID], F32)
    wprojT_in = dr("wprojT", [HID, D], F32)
    gnw_in = dr("gnw", [1, VD], F32)
    ident_in = dr("ident", [128, 128], BF16)
    triu_in = dr("triu", [128, 128], BF16)
    onescol_in = dr("onescol", [128, 1], BF16)
    ones2_in = dr("ones2", [2, 128], BF16)
    sel2_in = dr("sel2", [2, 128], BF16)
    sel4_in = dr("sel4", [4, 128], BF16)
    selg_in = dr("selg", [128, 2], BF16)
    halfrow_in = dr("halfrow", [1, 128], BF16)
    pmask_in = dr("pmask", [128, 8], F32)
    out_dram = dr("out", [TCORE, D], F32, "ExternalOutput")

    wsc = {n: dr(n + "_s", list(w_in[n].shape), BF16, "Internal") for n in w_in}
    fcq_s = dr("fcq_s", [NJ, 128, HID], BF16, "Internal")
    projq_s = dr("projq_s", [NHFC, 128, D], BF16, "Internal")
    st_loc = dr("st_loc", [128, 2048], F32, "Internal")
    st_all = dr("st_all", [128 * NC, 2048], F32, "Internal")
    dbg_t = {}
    if dbg:
        for n, shp, dt_ in [("q_fm", [128, 4096], BF16),
                            ("k_fm", [128, 4096], BF16),
                            ("p_tm", [128, 4096], BF16),
                            ("v_tm", [128, 8192], BF16),
                            ("sgb", [128, 8192], BF16),
                            ("rcs0", [128, 1024], F32),
                            ("oks0", [128, 1024], F32),
                            ("pvn0", [128, 1024], BF16),
                            ("ogtm0", [128, 2048], BF16),
                            ("xrb0", [128, 2048], F32)]:
            dbg_t[n] = dr("dbg_" + n, shp, dt_, "Internal")

    with ExitStack() as top:
        E = top.enter_context
        sb = lambda n, s, d=F32: E(nc.sbuf_tensor(n, s, d))

        ident = sb("sb_ident", [128, 128], BF16)
        triu = sb("sb_triu", [128, 128], BF16)
        onescol = sb("sb_onescol", [128, 1], BF16)
        ones2 = sb("sb_ones2", [2, 128], BF16)
        sel2 = sb("sb_sel2", [2, 128], BF16)
        sel4 = sb("sb_sel4", [4, 128], BF16)
        selg = sb("sb_selg", [128, 2], BF16)
        halfrow = sb("sb_halfrow", [1, 128], BF16)
        pmask = sb("sb_pmask", [128, 8])
        gnw_h = sb("gnw_h", [128, VD], BF16)
        kh_f = sb("kh_f", [64, HM])
        vh_f = sb("vh_f", [64, VD])
        kh_b = sb("kh_b", [128, HM], BF16)
        vh_b = sb("vh_b", [128, VD], BF16)
        s_st = sb("s_st", [1, HM])
        s_hl = sb("s_hl", [2, HM], BF16)
        s_lo = sb("s_lo", [1, HM], BF16)
        xt_f = sb("xt_f", [128, D])
        xf_pin = sb("xf_pin", [128, 8192], BF16)
        x2_f = sb("x2_f", [128, NTC * D])
        wstr = sb("wstr", [128, 2 * 2048], BF16)
        wffn = sb("wffn", [128, 2 * 2048], BF16)
        sm = [sb(f"sm{i}", [128, 32]) for i in range(8)]

        AR_BYTES = 100 * 1024
        arena = sb("arena", [128, AR_BYTES // 2], BF16)

        def ar(off_b, nelem, dt=BF16):
            esz = 4 if dt in (F32, I32) else 2
            assert off_b % 4 == 0 and off_b + nelem * esz <= AR_BYTES
            v = arena[:, off_b // 2:(off_b + nelem * esz) // 2]
            return v.bitcast(dt) if dt != BF16 else v

        def mkmap(spec):
            mp, off = {}, 0
            for name, nbytes in spec:
                mp[name] = off
                off += (nbytes + 63) // 64 * 64
            assert off <= AR_BYTES, ("arena overflow", off)
            return mp

        oA = mkmap([("q_fm", 8192), ("k_fm", 8192), ("p_tm", 8192),
                    ("v_tm", 16384), ("sg", 16384), ("ogT", 16384),
                    ("kt", 2048), ("p_fm", 2048),
                    ("pvn_tm", 2048), ("pvn_fm", 2048), ("oks", 4096),
                    ("pexp", 4096), ("rcs", 4096), ("skm", 512),
                    ("pvm", 512)])
        oA["xr"] = oA["q_fm"]
        oA["sclip"] = oA["oks"]
        oA["thg"] = oA["pvn_tm"]
        oA["osq"] = oA["rcs"]
        oA["osq2"] = oA["rcs"] + 2048
        oF = mkmap([("h_sr", 65536), ("x2fm", 16384), ("rbuf", 2048),
                    ("hsq", 2048)])
        oF["x2bf"] = oF["rbuf"]
        oF["xr2a"] = oF["x2fm"]
        oF["xr2b"] = oF["rbuf"]
        oF["x3"] = 0
        oP = mkmap([("c_acc", 65536), ("xt_bf", 16384),
                    ("p1_tm", 8192), ("s1clip", 4096), ("pad_", 4096),
                    ("srow", 4096)])
        assert oP["srow"] >= 98304
        oQ = mkmap([("qwb", 8192), ("qaw", 8192), ("qa", 8192),
                    ("qm1", 2048), ("qm2", 2048), ("qmu", 2048),
                    ("qgs", 8192), ("qscf", 8192), ("thc", 8192),
                    ("qpad", 10240), ("c_bf", 32768)])
        assert oQ["c_bf"] == 65536
        srow_f = ar(oP["srow"], 1024, F32)[0:1, :]
        gnrow_f = ar(0, 2048, F32)[0:1, :]
        gnrow_b = ar(8192, 2048)[0:1, :]

        pA = E(nc.psum_tensor("pA", [128, 2048], F32))
        pB = E(nc.psum_tensor("pB", [128, 2048], F32))

        def bank(i, p0=0, np_=128):
            t = pA if i < 4 else pB
            return t[p0:p0 + np_, (i % 4) * 512:(i % 4 + 1) * 512]

        def bank_bf(i):
            t = pA if i < 4 else pB
            return t[:, (i % 4) * 512:(i % 4 + 1) * 512].bitcast(BF16)

        sems = {s: E(nc.semaphore(s)) for s in SEMS}
        block = E(nc.Block())
        r = Rec()

        def dump(name, ap, sem, val):
            if dbg and name in dbg_t:
                r.wait("act", sem, val)
                r.dma("act", "dbg", dbg_t[name], ap)
                for engn in ("dve", "pe", "pool", "sync", "act"):
                    r.wait(engn, "dbg", r.cnt["dbg"])

        def mm(out, lhsT, rhs, start, stop):
            r.emit("pe", lambda e, a=out, b=lhsT, c=rhs, st=start, sp=stop:
                   e.matmul(a, b, c, start=st, stop=sp), "pe", 1)

        def tr(out_psum, in_sb):
            r.emit("pe", lambda e, a=out_psum, b=in_sb:
                   e.transpose(a, b, ident[:, :]), "pe", 1)

        def pe_mark():
            for idx in range(len(r.streams["pe"]) - 1, -1, -1):
                it = r.streams["pe"][idx]
                if it[0] == "i":
                    if it[2] == "pe":
                        return r.cnt["pe"]
                    assert it[2] is None
                    r.cnt["pe"] += 1
                    r.streams["pe"][idx] = ("i", it[1], "pe", 1)
                    return r.cnt["pe"]
            raise AssertionError("pe_mark with no prior op")

        def pe_gate():
            r.emit("pe", lambda e: e.nop(), "gate")
            return ("gate", r.cnt["gate"])

        def dve(fn):
            return r.emit("dve", fn, "dve")

        def act(fn):
            return r.emit("act", fn, "act")

        def rsqrt_rec(v_ap, out_ap, t1, t2, iters=2):
            dve(lambda e, a=v_ap, b=t1: e.tensor_scalar(
                out=b.bitcast(I32), in0=a.bitcast(I32), scalar1=1,
                scalar2=None, op0=ALU.logical_shift_right))
            dve(lambda e, a=t1, b=out_ap: e.tensor_scalar(
                out=b.bitcast(I32), in0=a.bitcast(I32), scalar1=-1,
                scalar2=MAGIC, op0=ALU.mult, op1=ALU.add))
            for _ in range(iters):
                dve(lambda e, a=v_ap, b=out_ap, c=t1: e.tensor_tensor(
                    out=c, in0=a, in1=b, op=ALU.mult))
                dve(lambda e, a=t1, b=out_ap, c=t2: e.tensor_tensor(
                    out=c, in0=a, in1=b, op=ALU.mult))
                dve(lambda e, a=t2, b=t1: e.tensor_scalar(
                    out=b, in0=a, scalar1=-0.5, scalar2=1.5, op0=ALU.mult,
                    op1=ALU.add))
                dve(lambda e, a=t1, b=out_ap: e.tensor_tensor(
                    out=b, in0=b, in1=a, op=ALU.mult))
            return r.cnt["dve"]

        def slot_load(buf, slot_cols, srcs, i, guard, guard_sem, sem=None,
                      queue="sync"):
            if isinstance(guard, tuple):
                r.wait(queue, guard[0], guard[1])
            else:
                r.wait(queue, guard_sem, guard)
            sl = (i % 2) * slot_cols
            out = buf[:, sl:sl + slot_cols]
            s = srcs[i]
            if len(s.shape) == 3:
                out = out.rearrange("p (a b) -> p a b", b=s.shape[2])
            if sem is None:
                sem = "dwA" if i % 2 == 0 else "dwB"
            r.dma(queue, sem, out, s)
            return sem, r.cnt[sem]

        def stream(buf, slot_cols, srcs, compute, guard0, guard_sem="pe",
                   pre=None, dual=False):
            n = len(srcs)
            cwl, marks = [0] * n, [0] * n

            def load(i, guard):
                if dual and i % 2 == 1:
                    cwl[i] = slot_load(buf, slot_cols, srcs, i, guard,
                                       guard_sem, sem="pwB", queue="pool")
                else:
                    cwl[i] = slot_load(buf, slot_cols, srcs, i, guard,
                                       guard_sem)

            if pre is not None:
                cwl[0], cwl[1] = pre
            else:
                load(0, guard0)
                if n > 1:
                    load(1, guard0)
            for i in range(n):
                r.wait("pe", cwl[i][0], cwl[i][1])
                marks[i] = compute(i, buf[:, (i % 2) * slot_cols:
                                          (i % 2) * slot_cols + slot_cols])
                if i + 2 < n:
                    load(i + 2, marks[i])
            return marks[-1]

        for dst, src in [(ident, ident_in), (triu, triu_in),
                         (onescol, onescol_in), (ones2, ones2_in),
                         (sel2, sel2_in), (sel4, sel4_in), (selg, selg_in),
                         (halfrow, halfrow_in), (pmask, pmask_in)]:
            r.dma("sync", "ds", dst[:, :], src)
        r.dma("sync", "ds", gnrow_f[:, :], gnw_in)
        c_const = r.cnt["ds"]
        r.wait("act", "ds", c_const)
        c0 = act(lambda e: e.activation(out=gnrow_b[:, :], in_=gnrow_f[:, :],
                                        func=AF.Copy))
        r.wait("pe", "act", c0)
        r.wait("pe", "ds", c_const)
        for piece in range(4):
            mm(bank(piece), halfrow[0:1, :],
               gnrow_b[0:1, piece * 512:(piece + 1) * 512], True, True)
        c1 = pe_mark()
        r.wait("dve", "pe", c1)
        c_gnw_cp = dve(lambda e: e.tensor_copy(out=gnw_h[:, :], in_=pA[:, :]))

        p1_pre = {}
        for t0 in range(min(2, NT)):
            xt_bf0 = ar(oP["xt_bf"], 8192) if t0 == 0 else None
        r.dma("pool", "pw",
              ar(oP["xt_bf"], 8192).rearrange("p (c d) -> p c d", d=2048),
              x_in[0:512, :].rearrange("(c p) d -> p c d", p=128))
        r.dma("pool", "pw",
              xf_pin[:, :].rearrange("p (j t) -> p j t", t=512),
              xT_in[:, 0:512].rearrange("(j p) t -> p j t", p=128))
        c_xpre = r.cnt["pw"]

        conv_kk = [0]

        def conv_chunk(name, j, buf):
            kk = conv_kk[0]
            srcw, dstw = w_in[name], wsc[name]
            ncol = srcw.shape[1]
            par = "A" if kk % 2 == 0 else "B"
            slot = (kk % 2) * 2048
            if kk >= 2:
                r.wait("pool", "ps" + par, 16 * (kk // 2))
            r.dma("pool", "pw" + par, buf[:, slot:slot + ncol],
                  srcw[j * 128:(j + 1) * 128, :])
            r.wait("pool", "pw" + par, r.cnt["pw" + par])
            r.dma("pool", "ps" + par, dstw[j * 128:(j + 1) * 128, :],
                  buf[:, slot:slot + ncol])
            conv_kk[0] += 1
            return (r.cnt["psA"], r.cnt["psB"])

        for j in range(NJ):
            conv_chunk("ws", j, wstr)
        c_conv_ws = (r.cnt["psA"], r.cnt["psB"])
        conv_tail = [(n, j) for n in ["wk", "wv", "wq", "wg", "wo"]
                     for j in range(NJ)]
        conv_ptr = [0]
        c_conv_kv = None
        c_conv_all = None

        r.wait("pe", "psA", c_conv_ws[0])
        r.wait("pe", "psB", c_conv_ws[1])
        g_wstr = pe_gate()

        c_acc = ar(oP["c_acc"], NJ * 1024, F32)
        dve(lambda e: e.memset(c_acc, 0.0))
        dve(lambda e: e.memset(srow_f[:, :], 0.0))

        g_p1x = 0
        g_sp1 = 0
        c_xldv = [c_xpre] + [0] * NT
        r.wait("pe", "dve", c_gnw_cp)
        for t in range(NT):
            xt_bf = ar(oP["xt_bf"], 8192)
            xf1 = xf_pin[:, :]
            p1tm = ar(oP["p1_tm"], 4096)
            s1c = ar(oP["s1clip"], 1024, F32)
            c_xld = c_xldv[t]

            def s_comp(i, slot, xf1=xf1, c_xld=c_xld):
                if i == 0:
                    r.wait("pe", "pw", c_xld)
                for jj in range(2):
                    j = i * 2 + jj
                    wsl = slot[:, jj * 1024:(jj + 1) * 1024]
                    for tc in range(NTC):
                        lh = xf1[:, j * 512 + tc * 128:j * 512 + (tc + 1) * 128]
                        for hf in range(2):
                            mm(bank(tc * 2 + hf), lh,
                               wsl[:, hf * 512:(hf + 1) * 512],
                               start=(j == 0), stop=(j == NJ - 1))
                return pe_mark()

            g_wstr = stream(
                wstr, 2048,
                [wsc["ws"][p * 256:(p + 1) * 256, :]
                 .rearrange("(c p) n -> p c n", p=128) for p in range(8)],
                s_comp, g_wstr)
            g_sp1 = g_wstr
            c_sp = r.cnt["pe"]
            for tc in range(NTC):
                r.wait("dve", "pe", c_sp)
                if tc > 0:
                    r.wait("dve", "act", r.cnt["act"])
                for hf in range(2):
                    dve(lambda e, tc=tc, hf=hf, s1c=s1c: e.tensor_scalar(
                        out=s1c[:, hf * 512:(hf + 1) * 512],
                        in0=bank(tc * 2 + hf), scalar1=32.0, scalar2=-32.0,
                        op0=ALU.min, op1=ALU.max))
                c_cl = r.cnt["dve"]
                r.wait("act", "dve", c_cl)
                act(lambda e, tc=tc, s1c=s1c, p1tm=p1tm: e.activation(
                    out=p1tm[:, tc * 1024:(tc + 1) * 1024], in_=s1c[:, :],
                    func=AF.Exp))
            c_pex = r.cnt["act"]
            r.wait("pe", "act", c_pex)
            ca_marks = [0] * NJ
            for dj in range(NJ):
                for hf in range(2):
                    for tc in range(NTC):
                        lh = xt_bf[:, tc * 2048 + dj * 128:
                                   tc * 2048 + (dj + 1) * 128]
                        mm(bank((dj % 2) * 2 + hf), lh,
                           p1tm[:, tc * 1024 + hf * 512:
                                tc * 1024 + (hf + 1) * 512],
                           start=(tc == 0), stop=(tc == NTC - 1))
                c_cp = pe_mark()
                r.wait("dve", "pe", c_cp)
                for hf in range(2):
                    dve(lambda e, dj=dj, hf=hf, bq=2 + (dj % 2) * 2:
                        e.tensor_tensor(
                            out=c_acc[:, dj * 1024 + hf * 512:
                                      dj * 1024 + (hf + 1) * 512],
                            in0=bank(bq - 2 + hf) if False else bank(
                                (dj % 2) * 2 + hf),
                            in1=c_acc[:, dj * 1024 + hf * 512:
                                      dj * 1024 + (hf + 1) * 512],
                            op=ALU.add))
                ca_marks[dj] = r.cnt["dve"]
                if dj >= 1:
                    r.wait("pe", "dve", ca_marks[dj - 1])
            r.wait("pe", "dve", ca_marks[NJ - 1])
            for tc in range(NTC):
                for hf in range(2):
                    mm(bank(2 + hf, 0, 1), onescol[:, :],
                       p1tm[:, tc * 1024 + hf * 512:tc * 1024 + (hf + 1) * 512],
                       start=(tc == 0), stop=(tc == NTC - 1))
            g_p1x = pe_mark()
            r.wait("dve", "pe", g_p1x)
            for hf in range(2):
                dve(lambda e, hf=hf: e.tensor_tensor(
                    out=srow_f[0:1, hf * 512:(hf + 1) * 512],
                    in0=bank(2 + hf, 0, 1),
                    in1=srow_f[0:1, hf * 512:(hf + 1) * 512], op=ALU.add))
            c_p1t = r.cnt["dve"]
            r.wait("pool", "dve", c_p1t)
            if t + 1 < NT:
                r.wait("pool", "pe", g_sp1)
                r.dma("pool", "pw",
                      xf_pin[:, :].rearrange("p (j t) -> p j t", t=512),
                      xT_in[:, (t + 1) * 512:(t + 2) * 512]
                      .rearrange("(j p) t -> p j t", p=128))
                r.wait("pool", "pe", g_p1x)
                r.dma("pool", "pw",
                      ar(oP["xt_bf"], 8192).rearrange("p (c d) -> p c d",
                                                      d=2048),
                      x_in[(t + 1) * 512:(t + 2) * 512, :]
                      .rearrange("(c p) d -> p c d", p=128))
                c_xldv[t + 1] = r.cnt["pw"]
            ntl = -(-len(conv_tail) // max(1, NT - 1)) if NT > 1 else 0
            while conv_ptr[0] < min((t + 1) * ntl, len(conv_tail)):
                name, j = conv_tail[conv_ptr[0]]
                cc = conv_chunk(name, j, wffn)
                conv_ptr[0] += 1
                if (name, j) == ("wv", NJ - 1):
                    c_conv_kv = cc

        while conv_ptr[0] < len(conv_tail):
            name, j = conv_tail[conv_ptr[0]]
            cc = conv_chunk(name, j, wffn)
            conv_ptr[0] += 1
            if (name, j) == ("wv", NJ - 1):
                c_conv_kv = cc
        c_conv_all = (r.cnt["psA"], r.cnt["psB"])

        c_bfv = ar(oQ["c_bf"], NJ * 1024)
        wkf = ar(oP["c_acc"], NJ * 1024)
        wvh2 = ar(oP["c_acc"] + 32768, NJ * 1024)
        r.wait("sync", "psA", c_conv_kv[0])
        r.wait("sync", "psB", c_conv_kv[1])
        r.wait("sync", "dwA", r.cnt["dwA"])
        r.wait("sync", "dwB", r.cnt["dwB"])
        for dj in range(NJ):
            dve(lambda e, dj=dj: e.tensor_copy(
                out=c_bfv[:, dj * 1024:(dj + 1) * 1024],
                in_=c_acc[:, dj * 1024:(dj + 1) * 1024]))
        c_cb = r.cnt["dve"]
        r.wait("sync", "dve", c_cb)
        r.dma("sync", "dwA",
              wkf.rearrange("p (c n) -> p c n", n=1024),
              wsc["wk"].rearrange("(c p) n -> p c n", p=128))
        c_wkf = r.cnt["dwA"]
        r.dma("sync", "dwB",
              wvh2.rearrange("p (c n) -> p c n", n=1024),
              wsc["wv"][:, 0:1024].rearrange("(c p) n -> p c n", p=128))
        c_wvf = r.cnt["dwB"]
        r.wait("pe", "dve", c_cb)
        r.wait("pe", "dwA", c_wkf)
        r.wait("pe", "dwB", c_wvf)
        c_wv2 = None
        for h in range(H):
            if h == 8:
                r.wait("pe", "dwB", c_wv2)
            for dj in range(NJ):
                mm(bank(h // 8, 0, 64)[:, (h % 8) * 64:(h % 8) * 64 + 64],
                   wkf[:, dj * 1024 + h * 64:dj * 1024 + (h + 1) * 64],
                   c_bfv[:, dj * 1024 + h * 64:dj * 1024 + (h + 1) * 64],
                   start=(dj == 0), stop=(dj == NJ - 1))
            for dj in range(NJ):
                mm(bank(2 + h // 4, 0, 64)[:, (h % 4) * 128:
                                           (h % 4) * 128 + 128],
                   c_bfv[:, dj * 1024 + h * 64:dj * 1024 + (h + 1) * 64],
                   wvh2[:, dj * 1024 + (h % 8) * 128:
                        dj * 1024 + (h % 8 + 1) * 128],
                   start=(dj == 0), stop=(dj == NJ - 1))
            if h == 7:
                r.wait("sync", "pe", pe_mark())
                r.dma("sync", "dwB",
                      wvh2.rearrange("p (c n) -> p c n", n=1024),
                      wsc["wv"][:, 1024:2048]
                      .rearrange("(c p) n -> p c n", p=128))
                c_wv2 = r.cnt["dwB"]
        c_kv = pe_mark()
        r.wait("dve", "pe", c_kv)
        for i in range(2):
            dve(lambda e, i=i: e.tensor_copy(
                out=kh_f[0:64, i * 512:(i + 1) * 512], in_=bank(i, 0, 64)))
        for i in range(4):
            dve(lambda e, i=i: e.tensor_copy(
                out=vh_f[0:64, i * 512:(i + 1) * 512], in_=bank(2 + i, 0, 64)))
        c_kvs = r.cnt["dve"]

        zbuf = ar(oP["p1_tm"], 1024, F32)
        dve(lambda e: e.memset(zbuf, 0.0))
        c_zb = r.cnt["dve"]
        r.wait("sync", "dve", max(c_kvs, c_zb))
        r.dma("sync", "ds", st_loc[1:64, 1024:2048], zbuf[1:64, :])
        r.dma("sync", "ds", st_loc[0:64, 0:1024], kh_f[0:64, :])
        r.dma("sync", "ds", st_loc[64:128, 0:2048], vh_f[0:64, :])
        r.dma("sync", "ds", st_loc[0:1, 1024:2048], srow_f[0:1, :])
        c_pack = r.cnt["ds"]
        r.wait("pool", "ds", c_pack)
        r.emit("pool", lambda e: e.collective_compute(
            "AllGather", ALU.bypass, replica_groups=[list(range(NC))],
            ins=[st_loc[:, :].opt()], outs=[st_all[:, :].opt()]), "coll", 1)
        c_coll = r.cnt["coll"]

        for engn in ("pool", "sync", "dve", "act"):
            r.wait(engn, "pe", c_kv)
        qwb = ar(oQ["qwb"], 4096)
        qaw = ar(oQ["qaw"], 4096)
        qa = ar(oQ["qa"], 4096)
        qm1 = ar(oQ["qm1"], 1024)
        qm2 = ar(oQ["qm2"], 1024)
        qmu = ar(oQ["qmu"], 1024)
        qgs = ar(oQ["qgs"], 2048, F32)[0:2, :]
        qscf = ar(oQ["qscf"], 2048, F32)[0:2, :]
        thc = ar(oQ["thc"], 4096)
        SCI = 1.0 / QTHR

        chunks = []
        for i in range(NJ):
            for rr in range(4):
                chunks.append((wfcT_in[i * 128:(i + 1) * 128,
                                       rr * 2048:(rr + 1) * 2048],
                               fcq_s[i, :, rr * 2048:(rr + 1) * 2048]))
        for cch in range(NHFC):
            chunks.append((wprojT_in[cch * 128:(cch + 1) * 128, :],
                           projq_s[cch, :, :]))
        nck = len(chunks)

        ld_m = [0] * nck
        abs_m = [0] * nck
        cast_m = [0] * nck
        sum_m = [0] * nck
        evac_m = [0] * nck
        smath_m = [0] * nck
        fill_m = [[0, 0] for _ in range(nck)]
        eff_m = [[0, 0] for _ in range(nck)]
        st_m = [0] * nck
        wf32 = ar(oQ["qpad"], 2048, F32)
        for k in range(nck):
            sl = (k % 2) * 2048
            tsl = (k % 2) * 2048
            if k % 2 == 0:
                if k >= 2:
                    r.wait("pool", "dve", eff_m[k - 2][1])
                r.dma("pool", "pwA", qwb[:, sl:sl + 2048], chunks[k][0])
                ld_m[k] = r.cnt["pwA"]
                r.wait("act", "pwA", ld_m[k])
            else:
                if k >= 2:
                    r.wait("sync", "act", cast_m[k - 2])
                r.dma("sync", "pwB", wf32[:, :], chunks[k][0])
                ld_m[k] = r.cnt["pwB"]
                r.wait("act", "pwB", ld_m[k])
                if k >= 2:
                    r.wait("act", "dve", eff_m[k - 2][1])
                act(lambda e, sl=sl: e.activation(
                    out=qwb[:, sl:sl + 2048], in_=wf32[:, :], func=AF.Copy))
                cast_m[k] = r.cnt["act"]
            if k >= 2:
                r.wait("act", "pe", sum_m[k - 2])
            act(lambda e, sl=sl: e.activation(
                out=qaw[:, sl:sl + 2048], in_=qwb[:, sl:sl + 2048],
                func=AF.Abs))
            abs_m[k] = r.cnt["act"]
            r.wait("pe", "act", abs_m[k])
            if k >= 1:
                r.wait("pe", "act", evac_m[k - 1])
            for piece in range(4):
                mm(bank(4 + piece, 0, 2), selg[:, :],
                   qaw[:, sl + piece * 512:sl + (piece + 1) * 512],
                   True, True)
            sum_m[k] = pe_mark()
            r.wait("act", "pe", sum_m[k])
            if k >= 1:
                r.wait("act", "dve", smath_m[k - 1])
            act(lambda e: e.copy(out=qgs[:, :], in_=pB[0:2, :]))
            evac_m[k] = r.cnt["act"]
            r.wait("dve", "act", evac_m[k])
            if k >= 2:
                r.wait("dve", "pe", fill_m[k - 2][1])
            dve(lambda e: e.tensor_scalar(out=qscf[:, :], in0=qgs[:, :],
                                          scalar1=1.0 / 64, scalar2=None,
                                          op0=ALU.mult))
            dve(lambda e, tsl=tsl: e.tensor_scalar(
                out=thc[0:2, tsl:tsl + 2048], in0=qscf[:, :],
                scalar1=float(bfdt(1e-8)), scalar2=None, op0=ALU.max))
            smath_m[k] = r.cnt["dve"]
            for h in range(2):
                if h == 0:
                    r.wait("pe", "dve", smath_m[k])
                if k >= 1:
                    r.wait("pe", "dve", eff_m[k - 1][h])
                for piece in range(2):
                    mm(bank(h * 2 + piece), sel2[:, :],
                       thc[0:2, tsl + h * 1024 + piece * 512:
                           tsl + h * 1024 + (piece + 1) * 512],
                       True, True)
                fill_m[k][h] = pe_mark()
                r.wait("dve", "pe", fill_m[k][h])
                if k >= 2 and h == 0:
                    r.wait("dve", "ps", st_m[k - 2])
                wslc = qwb[:, sl + h * 1024:sl + (h + 1) * 1024]
                stn = pA[:, h * 1024:(h + 1) * 1024]
                dve(lambda e, wslc=wslc, stn=stn: e.scalar_tensor_tensor(
                    out=qm1[:, :], in0=wslc, scalar=SCI,
                    in1=stn, op0=ALU.mult, op1=ALU.is_gt))
                dve(lambda e, wslc=wslc, stn=stn: e.scalar_tensor_tensor(
                    out=qm2[:, :], in0=wslc, scalar=-SCI,
                    in1=stn, op0=ALU.mult, op1=ALU.is_gt))
                dve(lambda e: e.tensor_tensor(out=qmu[:, :], in0=qm1[:, :],
                                              in1=qm2[:, :],
                                              op=ALU.subtract))
                dve(lambda e, stn=stn, o=qa[:, sl + h * 1024:
                                            sl + (h + 1) * 1024]:
                    e.tensor_tensor(out=o, in0=qmu[:, :], in1=stn,
                                    op=ALU.mult))
                eff_m[k][h] = r.cnt["dve"]
            r.wait("act", "dve", eff_m[k][1])
            r.dma("act", "ps", chunks[k][1], qa[:, sl:sl + 2048])
            st_m[k] = r.cnt["ps"]
        q_done_all = st_m[nck - 1]

        acc_ab = [x2_f[:, 0:2048], x2_f[:, 2048:4096]]
        stg = [x2_f[:, 4096:6144], x2_f[:, 6144:8192]]
        r.wait("dve", "coll", c_coll)
        r.wait("act", "coll", c_coll)
        crl = [0] * NC
        madd = [0] * NC

        def rank_load(rk):
            sem = "dwA" if rk % 2 == 0 else "dwB"
            r.dma("act", sem, stg[rk % 2], st_all[rk * 128:(rk + 1) * 128, :])
            crl[rk] = r.cnt[sem]

        rank_load(0)
        rank_load(1)
        for rk in range(NC):
            r.wait("dve", "dwA" if rk % 2 == 0 else "dwB", crl[rk])
            if rk == 0:
                dve(lambda e: e.tensor_scalar(
                    out=acc_ab[1], in0=stg[0], scalar1=pmask[:, 0:1],
                    scalar2=None, op0=ALU.mult))
            else:
                dve(lambda e, rk=rk: e.scalar_tensor_tensor(
                    out=acc_ab[(rk + 1) % 2], in0=stg[rk % 2],
                    scalar=pmask[:, rk:rk + 1], in1=acc_ab[rk % 2],
                    op0=ALU.mult, op1=ALU.add))
            madd[rk] = r.cnt["dve"]
            if rk + 2 < NC:
                r.wait("act", "dve", madd[rk])
                rank_load(rk + 2)
        fin = acc_ab[NC % 2]
        dve(lambda e: e.tensor_copy(out=kh_f[0:64, :], in_=fin[0:64, 0:1024]))
        dve(lambda e: e.tensor_copy(out=vh_f[0:64, :], in_=fin[64:128, 0:2048]))
        dve(lambda e: e.tensor_copy(out=s_st[0:1, :], in_=fin[0:1, 1024:2048]))
        for bb in (0, 64):
            dve(lambda e, bb=bb: e.tensor_copy(out=kh_b[bb:bb + 64, :],
                                               in_=kh_f[0:64, :]))
            dve(lambda e, bb=bb: e.tensor_copy(out=vh_b[bb:bb + 64, :],
                                               in_=vh_f[0:64, :]))
        dve(lambda e: e.tensor_copy(out=s_hl[0:1, :], in_=s_st[0:1, :]))
        dve(lambda e: e.tensor_tensor(out=s_lo[0:1, :], in0=s_st[0:1, :],
                                      in1=s_hl[0:1, :], op=ALU.subtract))
        c_state = r.cnt["dve"]
        r.wait("pool", "dve", c_state)
        c_slo = r.dma("pool", "slo", s_hl[1:2, :], s_lo[0:1, :])

        q_fm = ar(oA["q_fm"], 4096)
        k_fm = ar(oA["k_fm"], 4096)
        p_tm = ar(oA["p_tm"], 4096)
        v_tm = ar(oA["v_tm"], 8192)
        sgb = ar(oA["sg"], 8192)
        ogT = ar(oA["ogT"], 8192)
        xfb = xf_pin[:, :]
        ktb = ar(oA["kt"], 1024)
        p_fmb = ar(oA["p_fm"], 1024)
        pvn_tm = ar(oA["pvn_tm"], 1024)
        pvn_fm = ar(oA["pvn_fm"], 1024)
        oks = ar(oA["oks"], 1024, F32)
        sggb = ar(oA["oks"], 2048)
        pexp = ar(oA["pexp"], 1024, F32)
        ogtm = ar(oA["pexp"], 2048)
        rcs = ar(oA["rcs"], 1024, F32)
        osq = ar(oA["osq"], 512, F32)
        osq2 = ar(oA["osq2"], 512, F32)
        skq = [ar(oA["skm"] + 256 * i, 128) for i in range(2)] + \
              [ar(oA["pvm"] + 256 * i, 128) for i in range(2)]
        sclip = ar(oA["sclip"], 1024, F32)
        xrb = ar(oA["xr"], 2048, F32)
        thg = ar(oA["thg"], 1024)
        h_sr = ar(oF["h_sr"], 32768)
        x2fm = ar(oF["x2fm"], 8192)
        x2bf = ar(oF["x2bf"], 2048)
        rbuf = ar(oF["rbuf"], 1024)
        hsqb = ar(oF["hsq"], 1024)
        hsq4 = ar(oF["rbuf"], 2048)
        r4 = ar(86016, 1024)
        ssq_acc = ar(88064, 16, F32)
        xr2a = ar(oF["xr2a"], 4096, F32)
        xr2b = ar(oF["xr2b"], 1024, F32)
        x3b = ar(oF["x3"], 2048, F32)

        g_arena = q_done_all
        r.wait("pe", "ps", g_arena)
        g_tile = pe_gate()
        g_xf = g_tile
        c_dve_state = c_state
        r.wait("sync", "psA", c_conv_all[0])
        r.wait("sync", "psB", c_conv_all[1])

        NPH = {"wq": (KD, q_fm), "wk": (KD, k_fm)}

        c_ln2 = c_state
        for t in range(NT):
            r.mark(f"tile{t}_start")
            r.wait("act", "dout", r.cnt["dout"])
            r.wait("act", "dve", c_ln2)
            if t == 0:
                if isinstance(g_xf, tuple):
                    r.wait("pool", g_xf[0], g_xf[1])
                else:
                    r.wait("pool", "pe", g_xf)
                r.dma("pool", "pw",
                      xfb.rearrange("p (j t) -> p j t", t=512),
                      xT_in[:, t * 512:(t + 1) * 512]
                      .rearrange("(j p) t -> p j t", p=128))
                c_xf = r.cnt["pw"]
            else:
                c_xf = c_xf_pre
            r.wait("pe", "pw", c_xf)

            r.mark(f"tile{t}_proj")
            def proj_comp(i, slot, dest, scale, nb):
                for jj in range(2):
                    j = i * 2 + jj
                    for n in range(8):
                        mm(bank(n), slot[:, jj * 1024 + n * 128:
                                         jj * 1024 + (n + 1) * 128],
                           xfb[:, j * 512:(j + 1) * 512],
                           start=(j == 0), stop=(j == NJ - 1))
                return pe_mark()

            g_tile = stream(
                wstr, 2048,
                [wsc["wq"][p * 256:(p + 1) * 256, :]
                 .rearrange("(c p) n -> p c n", p=128) for p in range(8)],
                lambda i, slot: proj_comp(i, slot, q_fm, 0.125, 8),
                g_tile)
            c_qp = r.cnt["pe"]
            r.wait("dve", "pe", c_qp)
            for n in range(8):
                dve(lambda e, n=n: e.tensor_scalar(
                    out=q_fm[:, n * 512:(n + 1) * 512], in0=bank(n),
                    scalar1=0.125, scalar2=None, op0=ALU.mult))
            c_qe = r.cnt["dve"]
            r.wait("pe", "dve", c_qe)

            g_tile = stream(
                wstr, 2048,
                [wsc["wk"][p * 256:(p + 1) * 256, :]
                 .rearrange("(c p) n -> p c n", p=128) for p in range(8)],
                lambda i, slot: proj_comp(i, slot, k_fm, 1.0, 8),
                g_tile)
            c_kp = r.cnt["pe"]
            r.wait("dve", "pe", c_kp)
            for n in range(8):
                dve(lambda e, n=n: e.tensor_copy(
                    out=k_fm[:, n * 512:(n + 1) * 512], in_=bank(n)))
            c_ke = r.cnt["dve"]
            r.wait("pe", "dve", c_ke)

            if t == 0:
                dump("q_fm", q_fm[:, :], "dve", r.cnt["dve"])
                dump("k_fm", k_fm[:, :], "dve", r.cnt["dve"])

            def s_comp(i, slot):
                for jj in range(2):
                    j = i * 2 + jj
                    wsl = slot[:, jj * 1024:(jj + 1) * 1024]
                    for tc in range(NTC):
                        lh = xfb[:, j * 512 + tc * 128:j * 512 + (tc + 1) * 128]
                        for hf in range(2):
                            mm(bank(tc * 2 + hf), lh,
                               wsl[:, hf * 512:(hf + 1) * 512],
                               start=(j == 0), stop=(j == NJ - 1))
                return pe_mark()

            g_tile = stream(
                wstr, 2048,
                [wsc["ws"][p * 256:(p + 1) * 256, :]
                 .rearrange("(c p) n -> p c n", p=128) for p in range(8)],
                s_comp, g_tile)
            c_sp3 = r.cnt["pe"]
            for tc in range(NTC):
                r.wait("dve", "pe", c_sp3)
                if tc > 0:
                    r.wait("dve", "act", r.cnt["act"])
                for hf in range(2):
                    dve(lambda e, tc=tc, hf=hf: e.tensor_scalar(
                        out=sclip[:, hf * 512:(hf + 1) * 512],
                        in0=bank(tc * 2 + hf), scalar1=32.0, scalar2=-32.0,
                        op0=ALU.min, op1=ALU.max))
                c_cl = r.cnt["dve"]
                r.wait("act", "dve", c_cl)
                act(lambda e, tc=tc: e.activation(
                    out=p_tm[:, tc * 1024:(tc + 1) * 1024], in_=sclip[:, :],
                    func=AF.Exp))
            c_pex3 = r.cnt["act"]
            r.wait("pe", "act", c_pex3)
            if t == 0:
                dump("p_tm", p_tm[:, :], "act", c_pex3)

            for wname, dest, do_g in (("wv", v_tm, False), ("wg", None, True)):
                for half in range(2):
                    def vg_comp(i, slot, half=half):
                        for jj in range(2):
                            j = i * 2 + jj
                            wsl = slot[:, jj * 1024:(jj + 1) * 1024]
                            for tc in range(NTC):
                                lh = xfb[:, j * 512 + tc * 128:
                                         j * 512 + (tc + 1) * 128]
                                for sub in range(2):
                                    mm(bank(tc * 2 + sub), lh,
                                       wsl[:, sub * 512:(sub + 1) * 512],
                                       start=(j == 0), stop=(j == NJ - 1))
                        return pe_mark()

                    g_tile = stream(
                        wstr, 2048,
                        [wsc[wname][p * 256:(p + 1) * 256,
                                    half * 1024:(half + 1) * 1024]
                         .rearrange("(c p) n -> p c n", p=128)
                         for p in range(8)],
                        vg_comp, g_tile)
                    c_vg = r.cnt["pe"]
                    if not do_g:
                        r.wait("dve", "pe", c_vg)
                        for tc in range(NTC):
                            for sub in range(2):
                                dve(lambda e, tc=tc, sub=sub, half=half:
                                    e.tensor_copy(
                                        out=v_tm[:, tc * 2048 + half * 1024 +
                                                 sub * 512:tc * 2048 +
                                                 half * 1024 + (sub + 1) * 512],
                                        in_=bank(tc * 2 + sub)))
                        c_ve = r.cnt["dve"]
                        r.wait("pe", "dve", c_ve)
                    else:
                        r.wait("act", "pe", c_vg)
                        c_sg_dve = r.cnt["dve"]
                        for tc in range(NTC):
                            for sub in range(2):
                                act(lambda e, tc=tc, sub=sub: e.activation(
                                    out=thg[:, 0:512], in_=bank(tc * 2 + sub),
                                    func=AF.Tanh, scale=0.5))
                                act(lambda e, tc=tc, sub=sub: e.activation(
                                    out=thg[:, 512:1024],
                                    in_=bank(tc * 2 + sub), func=AF.Copy))
                                c_thg = r.cnt["act"]
                                r.wait("dve", "act", c_thg)
                                dve(lambda e, tc=tc, sub=sub, half=half:
                                    e.scalar_tensor_tensor(
                                        out=sgb[:, tc * 2048 + half * 1024 +
                                                sub * 512:tc * 2048 +
                                                half * 1024 + (sub + 1) * 512],
                                        in0=thg[:, 0:512], scalar=1.0,
                                        in1=thg[:, 512:1024], op0=ALU.add,
                                        op1=ALU.mult))
                                r.wait("act", "dve", r.cnt["dve"])
                        c_ge = r.cnt["dve"]
                        r.wait("pe", "dve", c_ge)

            c_xf_free = r.cnt["pe"]

            if t == 0:
                dump("v_tm", v_tm[:, :], "act", r.cnt["act"])
                dump("sgb", sgb[:, :], "dve", r.cnt["dve"])

            r.mark(f"tile{t}_attn")
            fcsrc = [fcq_s[ip * 4:(ip + 1) * 4, :, hb * 512:(hb + 1) * 512]
                     .rearrange("i p h -> p i h")
                     for hb in range(16) for ip in range(4)]
            fc_pre = (slot_load(wffn, 2048, fcsrc, 0, g_tile, "pe", "dfA"),
                      slot_load(wffn, 2048, fcsrc, 1, g_tile, "pe", "dfB"))
            wosrc = [wsc["wo"][p * 128:(p + 1) * 128, :] for p in range(NJ)]
            wo_pre = (slot_load(wstr, 2048, wosrc, 0, g_tile, "pe", "dgA"),
                      slot_load(wstr, 2048, wosrc, 1, g_tile, "pe", "dgB"))
            r.wait("pool", "pw", r.cnt["pw"])
            r.wait("pool", "dve", c_ln2)
            for tc in range(NTC):
                r.dma("pool", "pw", x2_f[:, tc * 2048:(tc + 1) * 2048],
                      x_in[(t * 4 + tc) * 128:(t * 4 + tc + 1) * 128, :])
            c_xres = r.cnt["pw"]
            c_s_upd = c_dve_state
            c_kh_upd = c_dve_state
            c_vh_upd = c_dve_state
            c_kh_upd_act = 0
            c_vh_upd_act = 0
            c_og_prev = 0
            g_sqred = [0, 0]
            for tc in range(NTC):
                r.mark(f"tile{t}_ch{tc}_A")
                r.wait("pe", "dve", max(c_s_upd, c_og_prev))
                r.wait("pe", "slo", c_slo)
                for hf in range(2):
                    mm(bank(6 + hf), ones2[:, :],
                       s_hl[0:2, hf * 512:(hf + 1) * 512], True, False)
                    mm(bank(6 + hf), triu[:, :],
                       p_tm[:, tc * 1024 + hf * 512:tc * 1024 + (hf + 1) * 512],
                       False, True)
                c_sfull = pe_mark()
                r.wait("act", "pe", c_sfull)
                act(lambda e: e.copy(out=oks[:, :], in_=pB[:, 1024:2048]))
                c_sfc = r.cnt["act"]
                r.wait("dve", "act", c_sfc)
                dve(lambda e: e.reciprocal(out=rcs[:, :], in_=oks[:, :]))
                r.wait("pool", "act", c_sfc)
                c_row = r.dma("pool", "slo", s_st[0:1, :], oks[127:128, :])
                r.wait("dve", "slo", c_row)
                dve(lambda e: e.tensor_copy(out=s_hl[0:1, :], in_=s_st[0:1, :]))
                dve(lambda e: e.tensor_tensor(out=s_lo[0:1, :],
                                              in0=s_st[0:1, :],
                                              in1=s_hl[0:1, :],
                                              op=ALU.subtract))
                c_s_upd = r.cnt["dve"]
                r.wait("pool", "dve", c_s_upd)
                c_slo = r.dma("pool", "slo", s_hl[1:2, :], s_lo[0:1, :])

                r.wait("pe", "dve", c_vh_upd)
                for n in range(8):
                    tr(pA[:, 1024 + n * 64:1024 + (n + 1) * 64].bitcast(BF16),
                       k_fm[:, n * 512 + tc * 128:n * 512 + (tc + 1) * 128])
                c_trk = pe_mark()
                r.wait("act", "pe", c_trk)
                act(lambda e: e.copy(
                    out=ktb[:, :], in_=pA[:, 1024:1536].bitcast(BF16)))
                c_kt = r.cnt["act"]
                r.wait("pe", "act", c_kt)
                for n in range(8):
                    tr(pA[:, 1024 + n * 64:1024 + (n + 1) * 64].bitcast(BF16),
                       p_tm[:, tc * 1024 + n * 128:tc * 1024 + (n + 1) * 128])
                c_trp = pe_mark()
                r.wait("act", "pe", c_trp)
                act(lambda e: e.copy(
                    out=p_fmb[:, :], in_=pA[:, 1024:1536].bitcast(BF16)))
                c_pfm = r.cnt["act"]
                r.wait("pe", "act", c_pfm)

                sk_m = [0] * H
                msk_m = [0] * H
                ok_m = [0] * H

                def emit_ok(h):
                    b0 = (h % 2) * 64
                    tn = h // 2
                    r.wait("pe", "dve", msk_m[h])
                    if h == 0:
                        r.wait("pe", "dve", c_kh_upd)
                        r.wait("pe", "act", c_kh_upd_act)
                    okb = bank(6 + h // 8)[:, (h % 8) * 64:(h % 8) * 64 + 64]
                    mm(okb, q_fm[b0:b0 + 64, tn * 512 + tc * 128:
                                 tn * 512 + (tc + 1) * 128],
                       kh_b[b0:b0 + 64, h * 64:(h + 1) * 64], True, False)
                    mm(okb, skq[h % 4][:, :],
                       p_tm[:, tc * 1024 + h * 64:tc * 1024 + (h + 1) * 64],
                       False, True)
                    ok_m[h] = r.cnt["pe"]

                r.wait("pe", "dve", c_kh_upd)
                for h in range(H):
                    b0 = (h % 2) * 64
                    tn = h // 2
                    mm(bank(h % 4)[:, 0:128],
                       k_fm[b0:b0 + 64, tn * 512 + tc * 128:
                            tn * 512 + (tc + 1) * 128],
                       q_fm[b0:b0 + 64, tn * 512 + tc * 128:
                            tn * 512 + (tc + 1) * 128],
                       True, True)
                    sk_m[h] = r.cnt["pe"]
                    r.wait("dve", "pe", sk_m[h])
                    if h >= 4:
                        r.wait("dve", "pe", ok_m[h - 4])
                    dve(lambda e, h=h: e.tensor_tensor(
                        out=skq[h % 4][:, :], in0=bank(h % 4)[:, 0:128],
                        in1=triu[:, :], op=ALU.mult))
                    msk_m[h] = r.cnt["dve"]
                    if h >= 3:
                        emit_ok(h - 3)
                for h in range(H - 3, H):
                    emit_ok(h)
                c_ok = pe_mark()

                r.wait("pe", "dve", msk_m[H - 1])
                for h in range(H):
                    mm(bank(h // 8, 0, 64)[:, (h % 8) * 64:(h % 8) * 64 + 64],
                       ktb[:, h * 64:(h + 1) * 64],
                       p_tm[:, tc * 1024 + h * 64:tc * 1024 + (h + 1) * 64],
                       True, True)
                c_kst = pe_mark()

                r.wait("dve", "pe", c_ok)
                r.wait("dve", "slo", c_row)
                dve(lambda e: e.tensor_tensor(out=oks[:, :],
                                              in0=pB[:, 1024:2048],
                                              in1=rcs[:, :], op=ALU.mult))
                dve(lambda e: e.reduce_max(
                    out=sm[0][:, 0:16],
                    in_=oks[:, :].rearrange("p (h m) -> p h m", m=64),
                    axis=AX.X))
                dve(lambda e: e.tensor_scalar(out=sm[1][:, 0:16],
                                              in0=sm[0][:, 0:16], scalar1=-1.0,
                                              scalar2=None, op0=ALU.mult))
                for h in range(H):
                    dve(lambda e, h=h: e.tensor_scalar(
                        out=pexp[:, h * 64:(h + 1) * 64],
                        in0=oks[:, h * 64:(h + 1) * 64],
                        scalar1=sm[1][:, h:h + 1], scalar2=None, op0=ALU.add))
                c_msub = r.cnt["dve"]
                r.wait("act", "dve", c_msub)
                act(lambda e: e.activation(out=oks[:, :], in_=pexp[:, :],
                                           func=AF.Exp))
                c_exp = r.cnt["act"]
                r.wait("dve", "act", c_exp)
                dve(lambda e: e.reduce_sum(
                    out=sm[2][:, 0:16],
                    in_=oks[:, :].rearrange("p (h m) -> p h m", m=64),
                    axis=AX.X))
                dve(lambda e: e.reciprocal(out=sm[3][:, 0:16],
                                           in_=sm[2][:, 0:16]))
                for h in range(H):
                    dve(lambda e, h=h: e.tensor_scalar(
                        out=pexp[:, h * 64:(h + 1) * 64],
                        in0=rcs[:, h * 64:(h + 1) * 64],
                        scalar1=sm[3][:, h:h + 1], scalar2=None, op0=ALU.mult))
                dve(lambda e: e.tensor_tensor(out=pvn_tm[:, :], in0=oks[:, :],
                                              in1=pexp[:, :], op=ALU.mult))
                c_pvn = r.cnt["dve"]
                if t == 0 and tc == 0:
                    dump("rcs0", rcs[:, :], "dve", c_pvn)
                    dump("pvn0", pvn_tm[:, :], "dve", c_pvn)
                r.wait("dve", "pe", c_kst)
                for i in range(2):
                    dve(lambda e, i=i: e.tensor_tensor(
                        out=kh_f[0:64, i * 512:(i + 1) * 512],
                        in0=bank(i, 0, 64),
                        in1=kh_f[0:64, i * 512:(i + 1) * 512], op=ALU.add))
                c_khf = r.cnt["dve"]
                c_kh_upd = c_khf
                r.wait("act", "dve", c_khf)
                for bb in (0, 64):
                    act(lambda e, bb=bb: e.copy(
                        out=kh_b[bb:bb + 64, :], in_=kh_f[0:64, :]))
                c_kh_upd_act = r.cnt["act"]

                r.wait("pe", "dve", c_pvn)
                for n in range(8):
                    tr(pA[:, 1024 + n * 64:1024 + (n + 1) * 64].bitcast(BF16),
                       pvn_tm[:, n * 128:(n + 1) * 128])
                c_trpv = pe_mark()
                r.wait("act", "pe", c_trpv)
                act(lambda e: e.copy(
                    out=pvn_fm[:, :], in_=pA[:, 1024:1536].bitcast(BF16)))
                c_pvf = r.cnt["act"]
                r.wait("pe", "act", c_pvf)

                pv_m = [0] * H
                mpv_m = [0] * H
                ovm = [0] * H

                def emit_ov(h):
                    b0 = (h % 2) * 64
                    r.wait("pe", "dve", mpv_m[h])
                    if h == 0:
                        r.wait("pe", "dve", c_vh_upd)
                        r.wait("pe", "act", c_vh_upd_act)
                    ovb = bank(4 + h // 4)[:, (h % 4) * 128:(h % 4) * 128 + 128]
                    mm(ovb, pvn_fm[b0:b0 + 64,
                                   (h // 2) * 128:(h // 2) * 128 + 128],
                       vh_b[b0:b0 + 64, h * 128:(h + 1) * 128], True, False)
                    mm(ovb, skq[h % 4][:, :],
                       v_tm[:, tc * 2048 + h * 128:tc * 2048 + (h + 1) * 128],
                       False, True)
                    ovm[h] = r.cnt["pe"]

                r.wait("pe", "dve", c_kh_upd)
                for h in range(H):
                    b0 = (h % 2) * 64
                    mm(bank(h % 4)[:, 0:128],
                       p_fmb[b0:b0 + 64, (h // 2) * 128:(h // 2) * 128 + 128],
                       pvn_fm[b0:b0 + 64, (h // 2) * 128:(h // 2) * 128 + 128],
                       True, True)
                    pv_m[h] = r.cnt["pe"]
                    r.wait("dve", "pe", pv_m[h])
                    if h >= 4:
                        r.wait("dve", "pe", ovm[h - 4])
                    dve(lambda e, h=h: e.tensor_tensor(
                        out=skq[h % 4][:, :], in0=bank(h % 4)[:, 0:128],
                        in1=triu[:, :], op=ALU.mult))
                    mpv_m[h] = r.cnt["dve"]
                    if h >= 3:
                        emit_ov(h - 3)
                for h in range(H - 3, H):
                    emit_ov(h)
                c_ov = pe_mark()

                r.wait("act", "pe", c_ov)
                for bq in range(4):
                    r.wait("act", "dve", g_sqred[bq % 2])
                    act(lambda e, bq=bq: e.activation(
                        out=osq[:, :] if bq % 2 == 0 else osq2[:, :],
                        in_=bank(4 + bq), func=AF.Square))
                    c_sqa = r.cnt["act"]
                    r.wait("dve", "act", c_sqa)
                    dve(lambda e, bq=bq: e.reduce_sum(
                        out=sm[4][:, bq * 4:(bq + 1) * 4],
                        in_=(osq if bq % 2 == 0 else osq2)[:, :]
                        .rearrange("p (h v) -> p h v", v=128),
                        axis=AX.X))
                    g_sqred[bq % 2] = r.cnt["dve"]
                dve(lambda e: e.tensor_scalar(
                    out=sm[5][:, 0:16], in0=sm[4][:, 0:16], scalar1=1.0 / DV,
                    scalar2=EPS, op0=ALU.mult, op1=ALU.add))
                rsqrt_rec(sm[5][:, 0:16], sm[6][:, 0:16], sm[0][:, 0:16],
                          sm[1][:, 0:16])
                dve(lambda e, tc=tc: e.tensor_tensor(
                    out=sggb[:, :], in0=sgb[:, tc * 2048:(tc + 1) * 2048],
                    in1=gnw_h[:, :], op=ALU.mult))
                for h in range(H):
                    dve(lambda e, h=h: e.scalar_tensor_tensor(
                        out=ogtm[:, h * 128:(h + 1) * 128],
                        in0=bank(4 + h // 4)[:, (h % 4) * 128:
                                             (h % 4) * 128 + 128],
                        scalar=sm[6][:, h:h + 1],
                        in1=sggb[:, h * 128:(h + 1) * 128],
                        op0=ALU.mult, op1=ALU.mult))
                c_og = r.cnt["dve"]
                c_og_prev = c_og
                if t == 0 and tc == 0:
                    dump("ogtm0", ogtm[:, :], "dve", c_og)
                r.wait("pe", "dve", c_og)
                for grp in range(2):
                    for n in range(8):
                        j = grp * 8 + n
                        tr(pA[:, 1024 + n * 64:1024 + (n + 1) * 64]
                           .bitcast(BF16), ogtm[:, j * 128:(j + 1) * 128])
                    c_trog = pe_mark()
                    r.wait("act", "pe", c_trog)
                    for n in range(8):
                        j = grp * 8 + n
                        act(lambda e, j=j, n=n, tc=tc: e.copy(
                            out=ogT[:, j * 512 + tc * 128:
                                    j * 512 + (tc + 1) * 128],
                            in_=pA[:, 1024 + n * 64:1024 + (n + 1) * 64]
                            .bitcast(BF16)))
                    if grp == 0:
                        r.wait("pe", "act", r.cnt["act"])
                c_ogT = r.cnt["act"]
                r.wait("pe", "act", c_ogT)

                for h in range(H):
                    mm(bank(2 + h // 4, 0, 64)
                       [:, (h % 4) * 128:(h % 4) * 128 + 128],
                       p_tm[:, tc * 1024 + h * 64:tc * 1024 + (h + 1) * 64],
                       v_tm[:, tc * 2048 + h * 128:tc * 2048 + (h + 1) * 128],
                       True, True)
                c_su = pe_mark()
                r.wait("dve", "pe", c_su)
                for i in range(4):
                    dve(lambda e, i=i: e.tensor_tensor(
                        out=vh_f[0:64, i * 512:(i + 1) * 512],
                        in0=bank(2 + i, 0, 64),
                        in1=vh_f[0:64, i * 512:(i + 1) * 512], op=ALU.add))
                c_vhf = r.cnt["dve"]
                r.wait("act", "dve", c_vhf)
                for bb in (0, 64):
                    act(lambda e, bb=bb: e.copy(
                        out=vh_b[bb:bb + 64, :], in_=vh_f[0:64, :]))
                c_vh_upd_act = r.cnt["act"]
                c_vh_upd = r.cnt["dve"]
                c_dve_state = r.cnt["dve"]

            r.mark(f"tile{t}_wo")
            r.wait("pe", "dve", c_vh_upd)
            r.wait("pe", "pw", c_xres)
            g_xf = pe_gate()
            if t + 1 < NT:
                r.wait("pool", "pe", c_xf_free)
                r.dma("pool", "pw",
                      xfb.rearrange("p (j t) -> p j t", t=512),
                      xT_in[:, (t + 1) * 512:(t + 2) * 512]
                      .rearrange("(j p) t -> p j t", p=128))
                c_xf_pre = r.cnt["pw"]
            c_x2 = [0] * NTC
            for hp in range(2):
                def wo_comp(i, slot, hp=hp):
                    for tcl in range(2):
                        tc = hp * 2 + tcl
                        for piece in range(4):
                            mm(bank(tcl * 4 + piece),
                               ogT[:, i * 512 + tc * 128:
                                   i * 512 + (tc + 1) * 128],
                               slot[:, piece * 512:(piece + 1) * 512],
                               start=(i == 0), stop=(i == NJ - 1))
                    return pe_mark()

                g_tile2 = stream(
                    wstr, 2048, wosrc,
                    wo_comp, g_tile if hp == 0 else g_tile2,
                    pre=wo_pre if hp == 0 else None)
                c_wo = r.cnt["pe"]
                for tcl in range(2):
                    tc = hp * 2 + tcl
                    r.wait("dve", "pe", c_wo)
                    for piece in range(4):
                        dve(lambda e, tc=tc, tcl=tcl, piece=piece:
                            e.tensor_tensor(
                                out=xrb[:, piece * 512:(piece + 1) * 512],
                                in0=bank(tcl * 4 + piece),
                                in1=x2_f[:, tc * 2048 + piece * 512:
                                         tc * 2048 + (piece + 1) * 512],
                                op=ALU.add))
                    if t == 0 and tc == 0:
                        dump("xrb0", xrb[:, :], "dve", r.cnt["dve"])
                    dve(lambda e: e.reduce_sum(out=sm[0][:, 0:1],
                                               in_=xrb[:, :], axis=AX.X))
                    dve(lambda e: e.tensor_scalar(
                        out=sm[1][:, 0:1], in0=sm[0][:, 0:1],
                        scalar1=-1.0 / D, scalar2=None, op0=ALU.mult))
                    dve(lambda e, tc=tc: e.tensor_scalar(
                        out=x2_f[:, tc * 2048:(tc + 1) * 2048], in0=xrb[:, :],
                        scalar1=sm[1][:, 0:1], scalar2=None, op0=ALU.add))
                    dve(lambda e, tc=tc: e.tensor_tensor(
                        out=xrb[:, :], in0=x2_f[:, tc * 2048:(tc + 1) * 2048],
                        in1=x2_f[:, tc * 2048:(tc + 1) * 2048], op=ALU.mult))
                    dve(lambda e: e.reduce_sum(out=sm[2][:, 0:1],
                                               in_=xrb[:, :], axis=AX.X))
                    dve(lambda e: e.tensor_scalar(
                        out=sm[3][:, 0:1], in0=sm[2][:, 0:1], scalar1=1.0 / D,
                        scalar2=EPS, op0=ALU.mult, op1=ALU.add))
                    rsqrt_rec(sm[3][:, 0:1], sm[4][:, 0:1], sm[0][:, 0:1],
                              sm[1][:, 0:1])
                    dve(lambda e, tc=tc: e.tensor_scalar(
                        out=x2_f[:, tc * 2048:(tc + 1) * 2048],
                        in0=x2_f[:, tc * 2048:(tc + 1) * 2048],
                        scalar1=sm[4][:, 0:1], scalar2=None, op0=ALU.mult))
                    c_x2[tc] = r.cnt["dve"]
                    r.wait("sync", "dve", c_x2[tc])

            r.mark(f"tile{t}_x2cast")
            c_trx = 0
            for tc in range(NTC):
                if tc > 0:
                    r.wait("dve", "pe", c_trx)
                dve(lambda e, tc=tc: e.tensor_copy(
                    out=x2bf[:, :], in_=x2_f[:, tc * 2048:(tc + 1) * 2048]))
                c_cast = r.cnt["dve"]
                r.wait("pe", "dve", c_cast)
                for grp in range(2):
                    for n in range(8):
                        j = grp * 8 + n
                        tr(pA[:, 1024 + n * 64:1024 + (n + 1) * 64]
                           .bitcast(BF16), x2bf[:, j * 128:(j + 1) * 128])
                    c_trx = pe_mark()
                    r.wait("act", "pe", c_trx)
                    for n in range(8):
                        j = grp * 8 + n
                        act(lambda e, j=j, n=n, tc=tc: e.copy(
                            out=x2fm[:, j * 512 + tc * 128:
                                     j * 512 + (tc + 1) * 128],
                            in_=pA[:, 1024 + n * 64:1024 + (n + 1) * 64]
                            .bitcast(BF16)))
                    if grp == 0:
                        r.wait("pe", "act", r.cnt["act"])
                c_x2f = r.cnt["act"]
                r.wait("pe", "act", c_x2f)

            r.mark(f"tile{t}_fc")

            fcb_act = [0] * 16
            fcb_dve = [0] * 16
            fcb_ssqm = [0] * 16
            fcb_acc = [0] * 16

            def emit_fc_tail(hb):
                B0 = (hb % 2) * 4
                m = pe_mark()
                r.wait("act", "pe", m)
                r.wait("dve", "pe", m)
                if hb >= 1:
                    r.wait("dve", "pe", fcb_ssqm[hb - 1])
                    r.wait("act", "dve", fcb_dve[hb - 1])
                for hs in range(4):
                    ht = hb * 4 + hs
                    rsl = r4[:, (hs % 2) * 512:(hs % 2) * 512 + 512]
                    if hs % 2 == 0:
                        if hs == 2:
                            r.wait("act", "dve", r.cnt["dve"])
                        act(lambda e, B0=B0, hs=hs, rsl=rsl: e.activation(
                            out=rsl, in_=bank(B0 + hs), func=AF.Relu))
                        r.wait("dve", "act", r.cnt["act"])
                    else:
                        dve(lambda e, B0=B0, hs=hs, rsl=rsl: e.tensor_scalar(
                            out=rsl, in0=bank(B0 + hs), scalar1=0.0,
                            scalar2=None, op0=ALU.max))
                    dve(lambda e, ht=ht, rsl=rsl: e.tensor_tensor(
                        out=h_sr[:, ht * 512:(ht + 1) * 512],
                        in0=rsl, in1=rsl, op=ALU.mult))
                    dve(lambda e, ht=ht, hs=hs: e.tensor_tensor(
                        out=hsq4[:, hs * 512:(hs + 1) * 512],
                        in0=h_sr[:, ht * 512:(ht + 1) * 512],
                        in1=h_sr[:, ht * 512:(ht + 1) * 512], op=ALU.mult))
                fcb_dve[hb] = r.cnt["dve"]
                fcb_act[hb] = r.cnt["act"]

            def emit_fc_ssq(hb):
                B0 = (hb % 2) * 4
                r.wait("pe", "dve", fcb_dve[hb])
                for tcl in range(4):
                    for hs in range(4):
                        mm(bank(B0 + tcl)[:, 0:1],
                           hsq4[:, hs * 512 + tcl * 128:
                                hs * 512 + (tcl + 1) * 128],
                           onescol[:, :], start=(hs == 0), stop=(hs == 3))
                fcb_ssqm[hb] = pe_mark()
                r.wait("dve", "pe", fcb_ssqm[hb])
                for tcl in range(4):
                    if hb == 0:
                        dve(lambda e, B0=B0, tcl=tcl: e.tensor_copy(
                            out=ssq_acc[:, tcl:tcl + 1],
                            in_=bank(B0 + tcl)[:, 0:1]))
                    else:
                        dve(lambda e, B0=B0, tcl=tcl: e.tensor_tensor(
                            out=ssq_acc[:, tcl:tcl + 1],
                            in0=bank(B0 + tcl)[:, 0:1],
                            in1=ssq_acc[:, tcl:tcl + 1], op=ALU.add))
                fcb_acc[hb] = r.cnt["dve"]

            def fc_comp(n, slot):
                hb, ip = divmod(n, 4)
                B0 = (hb % 2) * 4
                if ip == 0 and hb >= 2:
                    r.wait("pe", "act", fcb_act[hb - 2])
                    r.wait("pe", "dve", fcb_dve[hb - 2])
                    r.wait("pe", "dve", fcb_acc[hb - 2])
                for jj in range(4):
                    j = ip * 4 + jj
                    for hs in range(4):
                        mm(bank(B0 + hs),
                           slot[:, jj * 512 + hs * 128:
                                jj * 512 + (hs + 1) * 128],
                           x2fm[:, j * 512:(j + 1) * 512],
                           start=(j == 0), stop=(j == NJ - 1))
                if ip == 1 and hb >= 1:
                    emit_fc_ssq(hb - 1)
                if ip == 3:
                    emit_fc_tail(hb)
                return pe_mark()

            g_tile = stream(wffn, 2048, fcsrc, fc_comp, g_tile,
                            pre=fc_pre, dual=True)
            emit_fc_ssq(15)
            c_fc = r.cnt["pe"]
            r.wait("dve", "pe", c_fc)
            for tcl in range(4):
                dve(lambda e, tcl=tcl: e.tensor_scalar(
                    out=sm[5][:, tcl:tcl + 1], in0=ssq_acc[:, tcl:tcl + 1],
                    scalar1=1.0 / HID, scalar2=EPS_FFN, op0=ALU.mult,
                    op1=ALU.add))
            rsqrt_rec(sm[5][:, 0:4], sm[7][:, 0:4], sm[0][:, 0:4],
                      sm[1][:, 0:4])
            c_rsf = r.cnt["dve"]
            r.wait("pe", "dve", c_rsf)

            r.mark(f"tile{t}_proj2")
            r.wait("pe", "act", fcb_act[15])
            r.wait("pe", "act", fcb_act[14])
            for dh in range(2):
                pjsrc = [projq_s[hh * 2:hh * 2 + 2, :,
                                 dh * 1024:(dh + 1) * 1024]
                         .rearrange("c p n -> p c n") for hh in range(32)]

                def pj_comp(i, slot, dh=dh):
                    for hl in range(2):
                        h2 = i * 2 + hl
                        for tc in range(NTC):
                            for piece in range(2):
                                mm(bank(tc * 2 + piece),
                                   h_sr[:, h2 * 512 + tc * 128:
                                        h2 * 512 + (tc + 1) * 128],
                                   slot[:, hl * 1024 + piece * 512:
                                        hl * 1024 + (piece + 1) * 512],
                                   start=(h2 == 0), stop=(h2 == 63))
                    return pe_mark()

                g_tile = stream(wffn, 2048, pjsrc, pj_comp, g_tile,
                                dual=True)
                c_pj = r.cnt["pe"]
                r.wait("dve", "pe", c_pj)
                if dh == 0:
                    for tc in range(NTC):
                        for piece in range(2):
                            dve(lambda e, tc=tc, piece=piece:
                                e.scalar_tensor_tensor(
                                    out=xr2a[:, tc * 1024 + piece * 512:
                                             tc * 1024 + (piece + 1) * 512],
                                    in0=bank(tc * 2 + piece),
                                    scalar=sm[7][:, tc:tc + 1],
                                    in1=x2_f[:, tc * 2048 + piece * 512:
                                             tc * 2048 + (piece + 1) * 512],
                                    op0=ALU.mult, op1=ALU.add))
                    c_eva = r.cnt["dve"]
                    r.wait("pe", "dve", c_eva)
                else:
                    xr2b4 = lambda tc: x2_f[:, tc * 2048:tc * 2048 + 1024]
                    for tc in range(NTC):
                        for piece in range(2):
                            dve(lambda e, tc=tc, piece=piece:
                                e.scalar_tensor_tensor(
                                    out=xr2b4(tc)[:, piece * 512:
                                                  (piece + 1) * 512],
                                    in0=bank(tc * 2 + piece),
                                    scalar=sm[7][:, tc:tc + 1],
                                    in1=x2_f[:, tc * 2048 + 1024 + piece * 512:
                                             tc * 2048 + 1024 +
                                             (piece + 1) * 512],
                                    op0=ALU.mult, op1=ALU.add))
                    c_evb = r.cnt["dve"]
                    r.wait("pe", "dve", c_evb)
                    for tc in range(NTC):
                        dve(lambda e, tc=tc: e.reduce_sum(
                            out=sm[0][:, 0:1],
                            in_=xr2a[:, tc * 1024:(tc + 1) * 1024], axis=AX.X))
                        dve(lambda e, tc=tc: e.reduce_sum(
                            out=sm[1][:, 0:1], in_=xr2b4(tc), axis=AX.X))
                        dve(lambda e: e.tensor_tensor(
                            out=sm[0][:, 0:1], in0=sm[0][:, 0:1],
                            in1=sm[1][:, 0:1], op=ALU.add))
                        dve(lambda e: e.tensor_scalar(
                            out=sm[1][:, 0:1], in0=sm[0][:, 0:1],
                            scalar1=-1.0 / D, scalar2=None, op0=ALU.mult))
                        dve(lambda e, tc=tc: e.tensor_scalar(
                            out=x3b[:, 0:1024],
                            in0=xr2a[:, tc * 1024:(tc + 1) * 1024],
                            scalar1=sm[1][:, 0:1], scalar2=None, op0=ALU.add))
                        dve(lambda e, tc=tc: e.tensor_scalar(
                            out=x3b[:, 1024:2048], in0=xr2b4(tc),
                            scalar1=sm[1][:, 0:1], scalar2=None, op0=ALU.add))
                        dve(lambda e, tc=tc: e.tensor_tensor(
                            out=xr2a[:, tc * 1024:(tc + 1) * 1024],
                            in0=x3b[:, 0:1024], in1=x3b[:, 0:1024],
                            op=ALU.mult))
                        dve(lambda e, tc=tc: e.tensor_tensor(
                            out=xr2b4(tc), in0=x3b[:, 1024:2048],
                            in1=x3b[:, 1024:2048], op=ALU.mult))
                        dve(lambda e, tc=tc: e.reduce_sum(
                            out=sm[0][:, 0:1],
                            in_=xr2a[:, tc * 1024:(tc + 1) * 1024], axis=AX.X))
                        dve(lambda e, tc=tc: e.reduce_sum(
                            out=sm[1][:, 0:1], in_=xr2b4(tc), axis=AX.X))
                        dve(lambda e: e.tensor_tensor(
                            out=sm[0][:, 0:1], in0=sm[0][:, 0:1],
                            in1=sm[1][:, 0:1], op=ALU.add))
                        dve(lambda e: e.tensor_scalar(
                            out=sm[2][:, 0:1], in0=sm[0][:, 0:1],
                            scalar1=1.0 / D, scalar2=EPS, op0=ALU.mult,
                            op1=ALU.add))
                        rsqrt_rec(sm[2][:, 0:1], sm[3][:, 0:1], sm[0][:, 0:1],
                                  sm[1][:, 0:1])
                        dve(lambda e: e.tensor_scalar(
                            out=x3b[:, 0:1024], in0=x3b[:, 0:1024],
                            scalar1=sm[3][:, 0:1], scalar2=None, op0=ALU.mult))
                        dve(lambda e: e.tensor_scalar(
                            out=x3b[:, 1024:2048], in0=x3b[:, 1024:2048],
                            scalar1=sm[3][:, 0:1], scalar2=None, op0=ALU.mult))
                        c_x3 = r.cnt["dve"]
                        r.wait("act", "dve", c_x3)
                        r.streams["act"].append(
                            ("i", lambda e, t=t, tc=tc: e.dma_start(
                                out=out_dram[(t * 4 + tc) * 128:
                                             (t * 4 + tc + 1) * 128, :],
                                in_=x3b[:, :]), "dout", 16))
                        r.cnt["dout"] += 16
                        r.wait("dve", "dout", r.cnt["dout"])
                    c_ln2 = r.cnt["dve"]

        r.wait("sync", "dout", 16 * 4 * NT)

        def runner(eng_name):
            selfsync = eng_name in ("dve", "act")

            def f(eng):
                own = 0
                last_wait = -1
                for it in r.streams[eng_name]:
                    if it[0] == "w":
                        eng.wait_ge(sems[it[1]], it[2])
                    else:
                        if selfsync and own > 0 and last_wait < own:
                            eng.wait_ge(sems[eng_name], own)
                            last_wait = own
                        ins = it[1](eng)
                        if it[2] is not None:
                            ins.then_inc(sems[it[2]], it[3])
                            if it[2] == eng_name:
                                own += it[3]
            return f

        block.tensor(runner("pe"))
        block.vector(runner("dve"))
        block.scalar(runner("act"))
        block.gpsimd(runner("pool"))
        block.sync(runner("sync"))

    return nc


def _consts():
    ident = np.eye(128, dtype=bfdt)
    triu = np.triu(np.ones((128, 128), dtype=np.float32)).astype(bfdt)
    onescol = np.ones((128, 1), dtype=bfdt)
    ones2 = np.ones((2, 128), dtype=bfdt)
    sel2 = np.zeros((2, 128), dtype=bfdt)
    sel2[0, :64] = 1
    sel2[1, 64:] = 1
    sel4 = np.zeros((4, 128), dtype=bfdt)
    sel4[0, :64] = 1
    sel4[1, 64:] = 1
    sel4[2, :64] = 1
    sel4[3, 64:] = 1
    selg = np.zeros((128, 2), dtype=bfdt)
    selg[:64, 0] = 1
    selg[64:, 1] = 1
    halfrow = np.full((1, 128), 0.5, dtype=bfdt)
    return ident, triu, onescol, ones2, sel2, sel4, selg, halfrow


def kernel(L0_attn_hidden_states, L0_attn_Wq, L0_attn_Wk, L0_attn_Wv,
           L0_attn_Wg, L0_attn_Ws, L0_attn_Wo, L0_attn_g_norm_weight,
           L0_ffn_W_fc, L0_ffn_W_proj, _nt=8):
    x = np.ascontiguousarray(np.asarray(L0_attn_hidden_states,
                                        dtype=np.float32).reshape(B * T, D))
    tcore = _nt * 512
    ident, triu, onescol, ones2, sel2, sel4, selg, halfrow = _consts()
    wfcT = np.ascontiguousarray(np.asarray(L0_ffn_W_fc, np.float32).T)
    wprojT = np.ascontiguousarray(np.asarray(L0_ffn_W_proj, np.float32).T)
    gnw = np.ascontiguousarray(
        np.tile(np.asarray(L0_attn_g_norm_weight, np.float32), H)[None, :])

    nc_prog = build(_nt)
    in_maps = []
    for c in range(NC):
        seq = c // (NC // B)
        t0 = c * tcore if _nt == 8 else (seq * (T if _nt == 8 else 0))
        if _nt == 8:
            rows = slice(c * 4096, (c + 1) * 4096)
        else:
            per_seq = NC // B
            within = c % per_seq
            base = seq * T + within * tcore
            rows = slice(base, base + tcore)
        xs = np.ascontiguousarray(x[rows])
        xTs = np.ascontiguousarray(xs.T)
        pm = np.zeros((128, 8), np.float32)
        per_seq = NC // B
        for rk in range((c // per_seq) * per_seq, c):
            pm[:, rk] = 1.0
        in_maps.append(dict(
            x=xs, xT=xTs,
            wq=np.asarray(L0_attn_Wq, np.float32),
            wk=np.asarray(L0_attn_Wk, np.float32),
            ws=np.asarray(L0_attn_Ws, np.float32),
            wv=np.asarray(L0_attn_Wv, np.float32),
            wg=np.asarray(L0_attn_Wg, np.float32),
            wo=np.asarray(L0_attn_Wo, np.float32),
            wfcT=wfcT, wprojT=wprojT, gnw=gnw,
            ident=ident, triu=triu, onescol=onescol, ones2=ones2,
            sel2=sel2, sel4=sel4, selg=selg, halfrow=halfrow, pmask=pm))
    res = run_bass_kernel_spmd(nc_prog, in_maps, core_ids=list(range(NC)))
    outs = [np.asarray(res.results[c]["out"]) for c in range(NC)]
    if _nt == 8:
        full = np.concatenate(outs, axis=0)
        return full.reshape(B, T, D)
    return outs


# version 14
# speedup vs baseline: 1.2090x; 1.0773x over previous
import numpy as np
import ml_dtypes
import concourse.bass as bass
import concourse.mybir as mybir
from concourse.bass_utils import run_bass_kernel_spmd
from contextlib import ExitStack

F32 = mybir.dt.float32
BF16 = mybir.dt.bfloat16
I32 = mybir.dt.int32
AF = mybir.ActivationFunctionType
ALU = mybir.AluOpType
AX = mybir.AxisListType
bfdt = ml_dtypes.bfloat16

B, T, D = 2, 16384, 2048
H, DK, DV, M = 16, 64, 128, 64
KD, HM, VD, HID = 1024, 1024, 2048, 8192
NC = 8
EPS = 1e-5
EPS_FFN = float(np.finfo(np.float32).eps)
MAGIC = 0x5F3759DF
QTHR = 0.5 + 2.0 ** -10

ENGS = ("pe", "dve", "act", "pool", "sync")
SEMS = ("dw", "dwA", "dwB", "dfA", "dfB", "dgA", "dgB", "dx", "ds", "pw", "pwA", "pwB", "ps", "psA", "psB", "slo", "dbg", "pe", "gate", "dve", "act", "pool", "dout", "coll")


PHASELOG = []


class Rec:
    def __init__(self):
        self.streams = {e: [] for e in ENGS}
        self.cnt = {s: 0 for s in SEMS}

    def mark(self, label):
        PHASELOG.append((label, dict(self.cnt),
                         {e: len(s) for e, s in self.streams.items()}))

    def wait(self, eng, sem, val):
        if val > 0:
            self.streams[eng].append(("w", sem, val))

    def emit(self, eng, fn, sem=None, n=1):
        if sem is not None:
            self.cnt[sem] += n
        self.streams[eng].append(("i", fn, sem, n))
        return self.cnt[sem] if sem else None

    def dma(self, eng, sem, out, in_):
        self.cnt[sem] += 16
        self.streams[eng].append(
            ("i", lambda e, out=out, in_=in_: e.dma_start(out=out, in_=in_),
             sem, 16))
        return self.cnt[sem]


def build(NT=8, dbg=False):
    TILE, NTC, NJ = 512, 4, 16
    TCORE = NT * TILE
    NHFC = HID // 128

    nc = bass.Bass("TRN2", target_bir_lowering=False, debug=False)
    dr = lambda n, s, d, k="ExternalInput": nc.dram_tensor(n, s, d, kind=k).ap()
    x_in = dr("x", [TCORE, D], F32)
    xT_in = dr("xT", [D, TCORE], F32)
    w_in = {"wq": dr("wq", [D, KD], F32), "wk": dr("wk", [D, KD], F32),
            "ws": dr("ws", [D, HM], F32), "wv": dr("wv", [D, VD], F32),
            "wg": dr("wg", [D, VD], F32), "wo": dr("wo", [VD, D], F32)}
    wfcT_in = dr("wfcT", [D, HID], F32)
    wprojT_in = dr("wprojT", [HID, D], F32)
    gnw_in = dr("gnw", [1, VD], F32)
    ident_in = dr("ident", [128, 128], BF16)
    triu_in = dr("triu", [128, 128], BF16)
    onescol_in = dr("onescol", [128, 1], BF16)
    ones2_in = dr("ones2", [2, 128], BF16)
    sel2_in = dr("sel2", [2, 128], BF16)
    sel4_in = dr("sel4", [4, 128], BF16)
    selg_in = dr("selg", [128, 2], BF16)
    halfrow_in = dr("halfrow", [1, 128], BF16)
    pmask_in = dr("pmask", [128, 8], F32)
    out_dram = dr("out", [TCORE, D], F32, "ExternalOutput")

    wsc = {n: dr(n + "_s", list(w_in[n].shape), BF16, "Internal") for n in w_in}
    fcq_s = dr("fcq_s", [NJ, 128, HID], BF16, "Internal")
    projq_s = dr("projq_s", [NHFC, 128, D], BF16, "Internal")
    st_loc = dr("st_loc", [128, 2048], F32, "Internal")
    st_all = dr("st_all", [128 * NC, 2048], F32, "Internal")
    dbg_t = {}
    if dbg:
        for n, shp, dt_ in [("q_fm", [128, 4096], BF16),
                            ("k_fm", [128, 4096], BF16),
                            ("p_tm", [128, 4096], BF16),
                            ("v_tm", [128, 8192], BF16),
                            ("sgb", [128, 8192], BF16),
                            ("rcs0", [128, 1024], F32),
                            ("oks0", [128, 1024], F32),
                            ("pvn0", [128, 1024], BF16),
                            ("ogtm0", [128, 2048], BF16),
                            ("xrb0", [128, 2048], F32)]:
            dbg_t[n] = dr("dbg_" + n, shp, dt_, "Internal")

    with ExitStack() as top:
        E = top.enter_context
        sb = lambda n, s, d=F32: E(nc.sbuf_tensor(n, s, d))

        ident = sb("sb_ident", [128, 128], BF16)
        triu = sb("sb_triu", [128, 128], BF16)
        onescol = sb("sb_onescol", [128, 1], BF16)
        ones2 = sb("sb_ones2", [2, 128], BF16)
        sel2 = sb("sb_sel2", [2, 128], BF16)
        sel4 = sb("sb_sel4", [4, 128], BF16)
        selg = sb("sb_selg", [128, 2], BF16)
        halfrow = sb("sb_halfrow", [1, 128], BF16)
        pmask = sb("sb_pmask", [128, 8])
        gnw_h = sb("gnw_h", [128, VD], BF16)
        kh_f = sb("kh_f", [64, HM])
        vh_f = sb("vh_f", [64, VD])
        kh_b = sb("kh_b", [128, HM], BF16)
        vh_b = sb("vh_b", [128, VD], BF16)
        s_st = sb("s_st", [1, HM])
        s_hl = sb("s_hl", [2, HM], BF16)
        s_lo = sb("s_lo", [1, HM], BF16)
        xt_f = sb("xt_f", [128, D])
        xf_pin = sb("xf_pin", [128, 8192], BF16)
        x2_f = sb("x2_f", [128, NTC * D])
        wstr = sb("wstr", [128, 2 * 2048], BF16)
        wffn = sb("wffn", [128, 2 * 2048], BF16)
        sm = [sb(f"sm{i}", [128, 32]) for i in range(8)]

        AR_BYTES = 100 * 1024
        arena = sb("arena", [128, AR_BYTES // 2], BF16)

        def ar(off_b, nelem, dt=BF16):
            esz = 4 if dt in (F32, I32) else 2
            assert off_b % 4 == 0 and off_b + nelem * esz <= AR_BYTES
            v = arena[:, off_b // 2:(off_b + nelem * esz) // 2]
            return v.bitcast(dt) if dt != BF16 else v

        def mkmap(spec):
            mp, off = {}, 0
            for name, nbytes in spec:
                mp[name] = off
                off += (nbytes + 63) // 64 * 64
            assert off <= AR_BYTES, ("arena overflow", off)
            return mp

        oA = mkmap([("q_fm", 8192), ("k_fm", 8192), ("p_tm", 8192),
                    ("v_tm", 16384), ("sg", 16384), ("ogT", 16384),
                    ("kt", 2048), ("p_fm", 2048),
                    ("pvn_tm", 2048), ("pvn_fm", 2048), ("oks", 4096),
                    ("pexp", 4096), ("rcs", 4096), ("skm", 512),
                    ("pvm", 512)])
        oA["xr"] = oA["q_fm"]
        oA["sclip"] = oA["oks"]
        oA["thg"] = oA["pvn_tm"]
        oA["osq"] = oA["rcs"]
        oA["osq2"] = oA["rcs"] + 2048
        oF = mkmap([("h_sr", 65536), ("x2fm", 16384), ("rbuf", 2048),
                    ("hsq", 2048)])
        oF["x2bf"] = oF["rbuf"]
        oF["xr2a"] = oF["x2fm"]
        oF["xr2b"] = oF["rbuf"]
        oF["x3"] = 0
        oP = mkmap([("c_acc", 65536), ("xt_bf", 16384),
                    ("p1_tm", 8192), ("s1clip", 4096), ("pad_", 4096),
                    ("srow", 4096)])
        assert oP["srow"] >= 98304
        oQ = mkmap([("qwb", 16384), ("qaw", 8192), ("qa", 16384),
                    ("qm1", 2048), ("qm2", 2048), ("qmu", 2048),
                    ("qgs", 8192), ("qscf", 8192), ("thc", 8192),
                    ("wf32", 16384)])
        oQ["c_bf"] = 65536
        srow_f = ar(oP["srow"], 1024, F32)[0:1, :]
        gnrow_f = ar(0, 2048, F32)[0:1, :]
        gnrow_b = ar(8192, 2048)[0:1, :]

        pA = E(nc.psum_tensor("pA", [128, 2048], F32))
        pB = E(nc.psum_tensor("pB", [128, 2048], F32))

        def bank(i, p0=0, np_=128):
            t = pA if i < 4 else pB
            return t[p0:p0 + np_, (i % 4) * 512:(i % 4 + 1) * 512]

        def bank_bf(i):
            t = pA if i < 4 else pB
            return t[:, (i % 4) * 512:(i % 4 + 1) * 512].bitcast(BF16)

        sems = {s: E(nc.semaphore(s)) for s in SEMS}
        block = E(nc.Block())
        r = Rec()

        def dump(name, ap, sem, val):
            if dbg and name in dbg_t:
                r.wait("act", sem, val)
                r.dma("act", "dbg", dbg_t[name], ap)
                for engn in ("dve", "pe", "pool", "sync", "act"):
                    r.wait(engn, "dbg", r.cnt["dbg"])

        def mm(out, lhsT, rhs, start, stop):
            r.emit("pe", lambda e, a=out, b=lhsT, c=rhs, st=start, sp=stop:
                   e.matmul(a, b, c, start=st, stop=sp), "pe", 1)

        def tr(out_psum, in_sb):
            r.emit("pe", lambda e, a=out_psum, b=in_sb:
                   e.transpose(a, b, ident[:, :]), "pe", 1)

        def pe_mark():
            for idx in range(len(r.streams["pe"]) - 1, -1, -1):
                it = r.streams["pe"][idx]
                if it[0] == "i":
                    if it[2] == "pe":
                        return r.cnt["pe"]
                    assert it[2] is None
                    r.cnt["pe"] += 1
                    r.streams["pe"][idx] = ("i", it[1], "pe", 1)
                    return r.cnt["pe"]
            raise AssertionError("pe_mark with no prior op")

        def pe_gate():
            r.emit("pe", lambda e: e.nop(), "gate")
            return ("gate", r.cnt["gate"])

        def dve(fn):
            return r.emit("dve", fn, "dve")

        def act(fn):
            return r.emit("act", fn, "act")

        def rsqrt_rec(v_ap, out_ap, t1, t2, iters=2):
            dve(lambda e, a=v_ap, b=t1: e.tensor_scalar(
                out=b.bitcast(I32), in0=a.bitcast(I32), scalar1=1,
                scalar2=None, op0=ALU.logical_shift_right))
            dve(lambda e, a=t1, b=out_ap: e.tensor_scalar(
                out=b.bitcast(I32), in0=a.bitcast(I32), scalar1=-1,
                scalar2=MAGIC, op0=ALU.mult, op1=ALU.add))
            for _ in range(iters):
                dve(lambda e, a=v_ap, b=out_ap, c=t1: e.tensor_tensor(
                    out=c, in0=a, in1=b, op=ALU.mult))
                dve(lambda e, a=t1, b=out_ap, c=t2: e.tensor_tensor(
                    out=c, in0=a, in1=b, op=ALU.mult))
                dve(lambda e, a=t2, b=t1: e.tensor_scalar(
                    out=b, in0=a, scalar1=-0.5, scalar2=1.5, op0=ALU.mult,
                    op1=ALU.add))
                dve(lambda e, a=t1, b=out_ap: e.tensor_tensor(
                    out=b, in0=b, in1=a, op=ALU.mult))
            return r.cnt["dve"]

        def slot_load(buf, slot_cols, srcs, i, guard, guard_sem, sem=None,
                      queue="sync"):
            if isinstance(guard, tuple):
                r.wait(queue, guard[0], guard[1])
            else:
                r.wait(queue, guard_sem, guard)
            sl = (i % 2) * slot_cols
            out = buf[:, sl:sl + slot_cols]
            s = srcs[i]
            if len(s.shape) == 3:
                out = out.rearrange("p (a b) -> p a b", b=s.shape[2])
            if sem is None:
                sem = "dwA" if i % 2 == 0 else "dwB"
            r.dma(queue, sem, out, s)
            return sem, r.cnt[sem]

        def stream(buf, slot_cols, srcs, compute, guard0, guard_sem="pe",
                   pre=None, dual=False):
            n = len(srcs)
            cwl, marks = [0] * n, [0] * n

            def load(i, guard):
                if dual and i % 2 == 1:
                    cwl[i] = slot_load(buf, slot_cols, srcs, i, guard,
                                       guard_sem, sem="pwB", queue="pool")
                else:
                    cwl[i] = slot_load(buf, slot_cols, srcs, i, guard,
                                       guard_sem)

            if pre is not None:
                cwl[0], cwl[1] = pre
            else:
                load(0, guard0)
                if n > 1:
                    load(1, guard0)
            for i in range(n):
                r.wait("pe", cwl[i][0], cwl[i][1])
                marks[i] = compute(i, buf[:, (i % 2) * slot_cols:
                                          (i % 2) * slot_cols + slot_cols])
                if i + 2 < n:
                    load(i + 2, marks[i])
            return marks[-1]

        for dst, src in [(ident, ident_in), (triu, triu_in),
                         (onescol, onescol_in), (ones2, ones2_in),
                         (sel2, sel2_in), (sel4, sel4_in), (selg, selg_in),
                         (halfrow, halfrow_in), (pmask, pmask_in)]:
            r.dma("sync", "ds", dst[:, :], src)
        r.dma("sync", "ds", gnrow_f[:, :], gnw_in)
        c_const = r.cnt["ds"]
        r.wait("act", "ds", c_const)
        c0 = act(lambda e: e.activation(out=gnrow_b[:, :], in_=gnrow_f[:, :],
                                        func=AF.Copy))
        r.wait("pe", "act", c0)
        r.wait("pe", "ds", c_const)
        for piece in range(4):
            mm(bank(piece), halfrow[0:1, :],
               gnrow_b[0:1, piece * 512:(piece + 1) * 512], True, True)
        c1 = pe_mark()
        r.wait("dve", "pe", c1)
        c_gnw_cp = dve(lambda e: e.tensor_copy(out=gnw_h[:, :], in_=pA[:, :]))

        p1_pre = {}
        for t0 in range(min(2, NT)):
            xt_bf0 = ar(oP["xt_bf"], 8192) if t0 == 0 else None
        r.dma("pool", "pw",
              ar(oP["xt_bf"], 8192).rearrange("p (c d) -> p c d", d=2048),
              x_in[0:512, :].rearrange("(c p) d -> p c d", p=128))
        r.dma("pool", "pw",
              xf_pin[:, :].rearrange("p (j t) -> p j t", t=512),
              xT_in[:, 0:512].rearrange("(j p) t -> p j t", p=128))
        c_xpre = r.cnt["pw"]

        conv_kk = [0]

        def conv_chunk(name, j, buf):
            kk = conv_kk[0]
            srcw, dstw = w_in[name], wsc[name]
            ncol = srcw.shape[1]
            par = "A" if kk % 2 == 0 else "B"
            slot = (kk % 2) * 2048
            if kk >= 2:
                r.wait("pool", "ps" + par, 16 * (kk // 2))
            r.dma("pool", "pw" + par, buf[:, slot:slot + ncol],
                  srcw[j * 128:(j + 1) * 128, :])
            r.wait("pool", "pw" + par, r.cnt["pw" + par])
            r.dma("pool", "ps" + par, dstw[j * 128:(j + 1) * 128, :],
                  buf[:, slot:slot + ncol])
            conv_kk[0] += 1
            return (r.cnt["psA"], r.cnt["psB"])

        for j in range(NJ):
            conv_chunk("ws", j, wstr)
        c_conv_ws = (r.cnt["psA"], r.cnt["psB"])
        conv_tail = [(n, j) for n in ["wk", "wv", "wq", "wg", "wo"]
                     for j in range(NJ)]
        conv_ptr = [0]
        c_conv_kv = None
        c_conv_all = None

        r.wait("pe", "psA", c_conv_ws[0])
        r.wait("pe", "psB", c_conv_ws[1])
        g_wstr = pe_gate()

        c_acc = ar(oP["c_acc"], NJ * 1024, F32)
        dve(lambda e: e.memset(c_acc, 0.0))
        dve(lambda e: e.memset(srow_f[:, :], 0.0))

        g_p1x = 0
        g_sp1 = 0
        c_xldv = [c_xpre] + [0] * NT
        r.wait("pe", "dve", c_gnw_cp)
        for t in range(NT):
            xt_bf = ar(oP["xt_bf"], 8192)
            xf1 = xf_pin[:, :]
            p1tm = ar(oP["p1_tm"], 4096)
            s1c = ar(oP["s1clip"], 1024, F32)
            c_xld = c_xldv[t]

            def s_comp(i, slot, xf1=xf1, c_xld=c_xld):
                if i == 0:
                    r.wait("pe", "pw", c_xld)
                for jj in range(2):
                    j = i * 2 + jj
                    wsl = slot[:, jj * 1024:(jj + 1) * 1024]
                    for tc in range(NTC):
                        lh = xf1[:, j * 512 + tc * 128:j * 512 + (tc + 1) * 128]
                        for hf in range(2):
                            mm(bank(tc * 2 + hf), lh,
                               wsl[:, hf * 512:(hf + 1) * 512],
                               start=(j == 0), stop=(j == NJ - 1))
                return pe_mark()

            g_wstr = stream(
                wstr, 2048,
                [wsc["ws"][p * 256:(p + 1) * 256, :]
                 .rearrange("(c p) n -> p c n", p=128) for p in range(8)],
                s_comp, g_wstr)
            g_sp1 = g_wstr
            c_sp = r.cnt["pe"]
            for tc in range(NTC):
                r.wait("dve", "pe", c_sp)
                if tc > 0:
                    r.wait("dve", "act", r.cnt["act"])
                for hf in range(2):
                    dve(lambda e, tc=tc, hf=hf, s1c=s1c: e.tensor_scalar(
                        out=s1c[:, hf * 512:(hf + 1) * 512],
                        in0=bank(tc * 2 + hf), scalar1=32.0, scalar2=-32.0,
                        op0=ALU.min, op1=ALU.max))
                c_cl = r.cnt["dve"]
                r.wait("act", "dve", c_cl)
                act(lambda e, tc=tc, s1c=s1c, p1tm=p1tm: e.activation(
                    out=p1tm[:, tc * 1024:(tc + 1) * 1024], in_=s1c[:, :],
                    func=AF.Exp))
            c_pex = r.cnt["act"]
            r.wait("pe", "act", c_pex)
            ca_marks = [0] * NJ
            for dj in range(NJ):
                for hf in range(2):
                    for tc in range(NTC):
                        lh = xt_bf[:, tc * 2048 + dj * 128:
                                   tc * 2048 + (dj + 1) * 128]
                        mm(bank((dj % 2) * 2 + hf), lh,
                           p1tm[:, tc * 1024 + hf * 512:
                                tc * 1024 + (hf + 1) * 512],
                           start=(tc == 0), stop=(tc == NTC - 1))
                c_cp = pe_mark()
                r.wait("dve", "pe", c_cp)
                for hf in range(2):
                    dve(lambda e, dj=dj, hf=hf, bq=2 + (dj % 2) * 2:
                        e.tensor_tensor(
                            out=c_acc[:, dj * 1024 + hf * 512:
                                      dj * 1024 + (hf + 1) * 512],
                            in0=bank(bq - 2 + hf) if False else bank(
                                (dj % 2) * 2 + hf),
                            in1=c_acc[:, dj * 1024 + hf * 512:
                                      dj * 1024 + (hf + 1) * 512],
                            op=ALU.add))
                ca_marks[dj] = r.cnt["dve"]
                if dj >= 1:
                    r.wait("pe", "dve", ca_marks[dj - 1])
            r.wait("pe", "dve", ca_marks[NJ - 1])
            for tc in range(NTC):
                for hf in range(2):
                    mm(bank(2 + hf, 0, 1), onescol[:, :],
                       p1tm[:, tc * 1024 + hf * 512:tc * 1024 + (hf + 1) * 512],
                       start=(tc == 0), stop=(tc == NTC - 1))
            g_p1x = pe_mark()
            r.wait("dve", "pe", g_p1x)
            for hf in range(2):
                dve(lambda e, hf=hf: e.tensor_tensor(
                    out=srow_f[0:1, hf * 512:(hf + 1) * 512],
                    in0=bank(2 + hf, 0, 1),
                    in1=srow_f[0:1, hf * 512:(hf + 1) * 512], op=ALU.add))
            c_p1t = r.cnt["dve"]
            r.wait("pool", "dve", c_p1t)
            if t + 1 < NT:
                r.wait("pool", "pe", g_sp1)
                r.dma("pool", "pw",
                      xf_pin[:, :].rearrange("p (j t) -> p j t", t=512),
                      xT_in[:, (t + 1) * 512:(t + 2) * 512]
                      .rearrange("(j p) t -> p j t", p=128))
                r.wait("pool", "pe", g_p1x)
                r.dma("pool", "pw",
                      ar(oP["xt_bf"], 8192).rearrange("p (c d) -> p c d",
                                                      d=2048),
                      x_in[(t + 1) * 512:(t + 2) * 512, :]
                      .rearrange("(c p) d -> p c d", p=128))
                c_xldv[t + 1] = r.cnt["pw"]
            ntl = -(-len(conv_tail) // max(1, NT - 1)) if NT > 1 else 0
            while conv_ptr[0] < min((t + 1) * ntl, len(conv_tail)):
                name, j = conv_tail[conv_ptr[0]]
                cc = conv_chunk(name, j, wffn)
                conv_ptr[0] += 1
                if (name, j) == ("wv", NJ - 1):
                    c_conv_kv = cc

        while conv_ptr[0] < len(conv_tail):
            name, j = conv_tail[conv_ptr[0]]
            cc = conv_chunk(name, j, wffn)
            conv_ptr[0] += 1
            if (name, j) == ("wv", NJ - 1):
                c_conv_kv = cc
        c_conv_all = (r.cnt["psA"], r.cnt["psB"])

        c_bfv = ar(oQ["c_bf"], NJ * 1024)
        wkf = ar(oP["c_acc"], NJ * 1024)
        wvh2 = ar(oP["c_acc"] + 32768, NJ * 1024)
        r.wait("sync", "psA", c_conv_kv[0])
        r.wait("sync", "psB", c_conv_kv[1])
        r.wait("sync", "dwA", r.cnt["dwA"])
        r.wait("sync", "dwB", r.cnt["dwB"])
        for dj in range(NJ):
            dve(lambda e, dj=dj: e.tensor_copy(
                out=c_bfv[:, dj * 1024:(dj + 1) * 1024],
                in_=c_acc[:, dj * 1024:(dj + 1) * 1024]))
        c_cb = r.cnt["dve"]
        r.wait("sync", "dve", c_cb)
        r.dma("sync", "dwA",
              wkf.rearrange("p (c n) -> p c n", n=1024),
              wsc["wk"].rearrange("(c p) n -> p c n", p=128))
        c_wkf = r.cnt["dwA"]
        r.dma("sync", "dwB",
              wvh2.rearrange("p (c n) -> p c n", n=1024),
              wsc["wv"][:, 0:1024].rearrange("(c p) n -> p c n", p=128))
        c_wvf = r.cnt["dwB"]
        r.wait("pe", "dve", c_cb)
        r.wait("pe", "dwA", c_wkf)
        r.wait("pe", "dwB", c_wvf)
        c_wv2 = None
        for h in range(H):
            if h == 8:
                r.wait("pe", "dwB", c_wv2)
            for dj in range(NJ):
                mm(bank(h // 8, 0, 64)[:, (h % 8) * 64:(h % 8) * 64 + 64],
                   wkf[:, dj * 1024 + h * 64:dj * 1024 + (h + 1) * 64],
                   c_bfv[:, dj * 1024 + h * 64:dj * 1024 + (h + 1) * 64],
                   start=(dj == 0), stop=(dj == NJ - 1))
            for dj in range(NJ):
                mm(bank(2 + h // 4, 0, 64)[:, (h % 4) * 128:
                                           (h % 4) * 128 + 128],
                   c_bfv[:, dj * 1024 + h * 64:dj * 1024 + (h + 1) * 64],
                   wvh2[:, dj * 1024 + (h % 8) * 128:
                        dj * 1024 + (h % 8 + 1) * 128],
                   start=(dj == 0), stop=(dj == NJ - 1))
            if h == 7:
                r.wait("sync", "pe", pe_mark())
                r.dma("sync", "dwB",
                      wvh2.rearrange("p (c n) -> p c n", n=1024),
                      wsc["wv"][:, 1024:2048]
                      .rearrange("(c p) n -> p c n", p=128))
                c_wv2 = r.cnt["dwB"]
        c_kv = pe_mark()
        r.wait("dve", "pe", c_kv)
        for i in range(2):
            dve(lambda e, i=i: e.tensor_copy(
                out=kh_f[0:64, i * 512:(i + 1) * 512], in_=bank(i, 0, 64)))
        for i in range(4):
            dve(lambda e, i=i: e.tensor_copy(
                out=vh_f[0:64, i * 512:(i + 1) * 512], in_=bank(2 + i, 0, 64)))
        c_kvs = r.cnt["dve"]

        zbuf = ar(oP["p1_tm"], 1024, F32)
        dve(lambda e: e.memset(zbuf, 0.0))
        c_zb = r.cnt["dve"]
        r.wait("sync", "dve", max(c_kvs, c_zb))
        r.dma("sync", "ds", st_loc[1:64, 1024:2048], zbuf[1:64, :])
        r.dma("sync", "ds", st_loc[0:64, 0:1024], kh_f[0:64, :])
        r.dma("sync", "ds", st_loc[64:128, 0:2048], vh_f[0:64, :])
        r.dma("sync", "ds", st_loc[0:1, 1024:2048], srow_f[0:1, :])
        c_pack = r.cnt["ds"]
        r.wait("pool", "ds", c_pack)
        r.emit("pool", lambda e: e.collective_compute(
            "AllGather", ALU.bypass, replica_groups=[list(range(NC))],
            ins=[st_loc[:, :].opt()], outs=[st_all[:, :].opt()]), "coll", 1)
        c_coll = r.cnt["coll"]

        for engn in ("pool", "sync", "dve", "act"):
            r.wait(engn, "pe", c_kv)
        qwb = ar(oQ["qwb"], 8192)
        qaw = ar(oQ["qaw"], 4096)
        qa = ar(oQ["qa"], 8192)
        qm1 = ar(oQ["qm1"], 1024)
        qm2 = ar(oQ["qm2"], 1024)
        qmu = ar(oQ["qmu"], 1024)
        qgs = ar(oQ["qgs"], 2048, F32)[0:2, :]
        qscf = ar(oQ["qscf"], 2048, F32)[0:2, :]
        thc = ar(oQ["thc"], 4096)
        wf32 = ar(oQ["wf32"], 4096, F32)
        SCI = 1.0 / QTHR

        chunks = []
        for i in range(NJ):
            for rr in range(4):
                chunks.append((wfcT_in[i * 128:(i + 1) * 128,
                                       rr * 2048:(rr + 1) * 2048],
                               fcq_s[i, :, rr * 2048:(rr + 1) * 2048]))
        for cch in range(NHFC):
            chunks.append((wprojT_in[cch * 128:(cch + 1) * 128, :],
                           projq_s[cch, :, :]))
        nck = len(chunks)

        ld_m = [0] * nck
        abs_m = [0] * nck
        cast_m = [0] * nck
        sum_m = [0] * nck
        evac_m = [0] * nck
        smath_m = [0] * (nck + 1)
        fill_m = [[0, 0] for _ in range(nck)]
        eff_m = [[0, 0] for _ in range(nck)]
        st_m = [0] * nck

        def q_smath(k):
            tsl = (k % 2) * 2048
            r.wait("dve", "act", evac_m[k])
            if k >= 2:
                r.wait("dve", "pe", fill_m[k - 2][1])
            dve(lambda e: e.tensor_scalar(out=qscf[:, :], in0=qgs[:, :],
                                          scalar1=1.0 / 64, scalar2=None,
                                          op0=ALU.mult))
            dve(lambda e, tsl=tsl: e.tensor_scalar(
                out=thc[0:2, tsl:tsl + 2048], in0=qscf[:, :],
                scalar1=float(bfdt(1e-8)), scalar2=None, op0=ALU.max))
            smath_m[k] = r.cnt["dve"]

        def q_cmp(k, h):
            sl = (k % 4) * 2048
            r.wait("dve", "pe", fill_m[k][h])
            wslc = qwb[:, sl + h * 1024:sl + (h + 1) * 1024]
            stn = pA[:, h * 1024:(h + 1) * 1024]
            dve(lambda e, wslc=wslc, stn=stn: e.scalar_tensor_tensor(
                out=qm1[:, :], in0=wslc, scalar=SCI,
                in1=stn, op0=ALU.mult, op1=ALU.is_gt))
            dve(lambda e, wslc=wslc, stn=stn: e.scalar_tensor_tensor(
                out=qm2[:, :], in0=wslc, scalar=-SCI,
                in1=stn, op0=ALU.mult, op1=ALU.is_gt))
            dve(lambda e: e.tensor_tensor(out=qmu[:, :], in0=qm1[:, :],
                                          in1=qm2[:, :], op=ALU.subtract))
            if k >= 4 and h == 0:
                r.wait("dve", "ps", st_m[k - 4])
            dve(lambda e, stn=stn, o=qa[:, sl + h * 1024:
                                        sl + (h + 1) * 1024]:
                e.tensor_tensor(out=o, in0=qmu[:, :], in1=stn,
                                op=ALU.mult))
            eff_m[k][h] = r.cnt["dve"]

        def q_store(k):
            sl = (k % 4) * 2048
            r.wait("act", "dve", eff_m[k][1])
            r.dma("act", "ps", chunks[k][1], qa[:, sl:sl + 2048])
            st_m[k] = r.cnt["ps"]

        for k in range(nck):
            sl = (k % 4) * 2048
            tsl = (k % 2) * 2048
            if k % 2 == 0:
                if k >= 4:
                    r.wait("pool", "dve", eff_m[k - 4][1])
                r.dma("pool", "pwA", qwb[:, sl:sl + 2048], chunks[k][0])
                ld_m[k] = r.cnt["pwA"]
                r.wait("act", "pwA", ld_m[k])
            else:
                wsl32 = wf32[:, ((k // 2) % 2) * 2048:
                             ((k // 2) % 2) * 2048 + 2048]
                if k >= 4:
                    r.wait("sync", "act", cast_m[k - 4])
                r.dma("sync", "pwB", wsl32, chunks[k][0])
                ld_m[k] = r.cnt["pwB"]
                r.wait("act", "pwB", ld_m[k])
                if k >= 4:
                    r.wait("act", "dve", eff_m[k - 4][1])
                act(lambda e, sl=sl, wsl32=wsl32: e.activation(
                    out=qwb[:, sl:sl + 2048], in_=wsl32, func=AF.Copy))
                cast_m[k] = r.cnt["act"]
            if k >= 2:
                r.wait("act", "pe", sum_m[k - 2])
            act(lambda e, sl=sl, asl=(k % 2) * 2048: e.activation(
                out=qaw[:, asl:asl + 2048], in_=qwb[:, sl:sl + 2048],
                func=AF.Abs))
            abs_m[k] = r.cnt["act"]
            r.wait("pe", "act", abs_m[k])
            if k >= 1:
                r.wait("pe", "act", evac_m[k - 1])
            for piece in range(4):
                mm(bank(4 + piece, 0, 2), selg[:, :],
                   qaw[:, (k % 2) * 2048 + piece * 512:
                       (k % 2) * 2048 + (piece + 1) * 512],
                   True, True)
            sum_m[k] = pe_mark()
            r.wait("act", "pe", sum_m[k])
            if k >= 1:
                r.wait("act", "dve", smath_m[k - 1])
            act(lambda e: e.copy(out=qgs[:, :], in_=pB[0:2, :]))
            evac_m[k] = r.cnt["act"]
            if k >= 2:
                q_store(k - 2)
            q_smath(k)
            if k >= 1:
                q_cmp(k - 1, 1)
            for h in range(2):
                if h == 0:
                    r.wait("pe", "dve", smath_m[k])
                if k >= 1:
                    r.wait("pe", "dve", eff_m[k - 1][h])
                for piece in range(2):
                    mm(bank(h * 2 + piece), sel2[:, :],
                       thc[0:2, tsl + h * 1024 + piece * 512:
                           tsl + h * 1024 + (piece + 1) * 512],
                       True, True)
                fill_m[k][h] = pe_mark()
            q_cmp(k, 0)
        q_cmp(nck - 1, 1)
        q_store(nck - 2)
        q_store(nck - 1)
        q_done_all = st_m[nck - 1]

        acc_ab = [x2_f[:, 0:2048], x2_f[:, 2048:4096]]
        stg = [x2_f[:, 4096:6144], x2_f[:, 6144:8192]]
        r.wait("dve", "coll", c_coll)
        r.wait("act", "coll", c_coll)
        crl = [0] * NC
        madd = [0] * NC

        def rank_load(rk):
            sem = "dwA" if rk % 2 == 0 else "dwB"
            r.dma("act", sem, stg[rk % 2], st_all[rk * 128:(rk + 1) * 128, :])
            crl[rk] = r.cnt[sem]

        rank_load(0)
        rank_load(1)
        for rk in range(NC):
            r.wait("dve", "dwA" if rk % 2 == 0 else "dwB", crl[rk])
            if rk == 0:
                dve(lambda e: e.tensor_scalar(
                    out=acc_ab[1], in0=stg[0], scalar1=pmask[:, 0:1],
                    scalar2=None, op0=ALU.mult))
            else:
                dve(lambda e, rk=rk: e.scalar_tensor_tensor(
                    out=acc_ab[(rk + 1) % 2], in0=stg[rk % 2],
                    scalar=pmask[:, rk:rk + 1], in1=acc_ab[rk % 2],
                    op0=ALU.mult, op1=ALU.add))
            madd[rk] = r.cnt["dve"]
            if rk + 2 < NC:
                r.wait("act", "dve", madd[rk])
                rank_load(rk + 2)
        fin = acc_ab[NC % 2]
        dve(lambda e: e.tensor_copy(out=kh_f[0:64, :], in_=fin[0:64, 0:1024]))
        dve(lambda e: e.tensor_copy(out=vh_f[0:64, :], in_=fin[64:128, 0:2048]))
        dve(lambda e: e.tensor_copy(out=s_st[0:1, :], in_=fin[0:1, 1024:2048]))
        for bb in (0, 64):
            dve(lambda e, bb=bb: e.tensor_copy(out=kh_b[bb:bb + 64, :],
                                               in_=kh_f[0:64, :]))
            dve(lambda e, bb=bb: e.tensor_copy(out=vh_b[bb:bb + 64, :],
                                               in_=vh_f[0:64, :]))
        dve(lambda e: e.tensor_copy(out=s_hl[0:1, :], in_=s_st[0:1, :]))
        dve(lambda e: e.tensor_tensor(out=s_lo[0:1, :], in0=s_st[0:1, :],
                                      in1=s_hl[0:1, :], op=ALU.subtract))
        c_state = r.cnt["dve"]
        r.wait("pool", "dve", c_state)
        c_slo = r.dma("pool", "slo", s_hl[1:2, :], s_lo[0:1, :])

        q_fm = ar(oA["q_fm"], 4096)
        k_fm = ar(oA["k_fm"], 4096)
        p_tm = ar(oA["p_tm"], 4096)
        v_tm = ar(oA["v_tm"], 8192)
        sgb = ar(oA["sg"], 8192)
        ogT = ar(oA["ogT"], 8192)
        xfb = xf_pin[:, :]
        ktb = ar(oA["kt"], 1024)
        p_fmb = ar(oA["p_fm"], 1024)
        pvn_tm = ar(oA["pvn_tm"], 1024)
        pvn_fm = ar(oA["pvn_fm"], 1024)
        oks = ar(oA["oks"], 1024, F32)
        sggb = ar(oA["oks"], 2048)
        pexp = ar(oA["pexp"], 1024, F32)
        ogtm = ar(oA["pexp"], 2048)
        rcs = ar(oA["rcs"], 1024, F32)
        osq = ar(oA["osq"], 512, F32)
        osq2 = ar(oA["osq2"], 512, F32)
        skq = [ar(oA["skm"] + 256 * i, 128) for i in range(2)] + \
              [ar(oA["pvm"] + 256 * i, 128) for i in range(2)]
        sclip = ar(oA["sclip"], 1024, F32)
        xrb = ar(oA["xr"], 2048, F32)
        thg = ar(oA["thg"], 1024)
        h_sr = ar(oF["h_sr"], 32768)
        x2fm = ar(oF["x2fm"], 8192)
        x2bf = ar(oF["x2bf"], 2048)
        rbuf = ar(oF["rbuf"], 1024)
        hsqb = ar(oF["hsq"], 1024)
        hsq4 = ar(oF["rbuf"], 2048)
        r4 = ar(86016, 1024)
        ssq_acc = ar(88064, 16, F32)
        xr2a = ar(oF["xr2a"], 4096, F32)
        xr2b = ar(oF["xr2b"], 1024, F32)
        x3b = ar(oF["x3"], 2048, F32)

        g_arena = q_done_all
        r.wait("pe", "ps", g_arena)
        g_tile = pe_gate()
        g_xf = g_tile
        c_dve_state = c_state
        r.wait("sync", "psA", c_conv_all[0])
        r.wait("sync", "psB", c_conv_all[1])

        NPH = {"wq": (KD, q_fm), "wk": (KD, k_fm)}

        c_ln2 = c_state
        for t in range(NT):
            r.mark(f"tile{t}_start")
            r.wait("act", "dout", r.cnt["dout"])
            r.wait("act", "dve", c_ln2)
            if t == 0:
                if isinstance(g_xf, tuple):
                    r.wait("pool", g_xf[0], g_xf[1])
                else:
                    r.wait("pool", "pe", g_xf)
                r.dma("pool", "pw",
                      xfb.rearrange("p (j t) -> p j t", t=512),
                      xT_in[:, t * 512:(t + 1) * 512]
                      .rearrange("(j p) t -> p j t", p=128))
                c_xf = r.cnt["pw"]
            else:
                c_xf = c_xf_pre
            r.wait("pe", "pw", c_xf)

            r.mark(f"tile{t}_proj")
            def proj_comp(i, slot, dest, scale, nb):
                for jj in range(2):
                    j = i * 2 + jj
                    for n in range(8):
                        mm(bank(n), slot[:, jj * 1024 + n * 128:
                                         jj * 1024 + (n + 1) * 128],
                           xfb[:, j * 512:(j + 1) * 512],
                           start=(j == 0), stop=(j == NJ - 1))
                return pe_mark()

            g_tile = stream(
                wstr, 2048,
                [wsc["wq"][p * 256:(p + 1) * 256, :]
                 .rearrange("(c p) n -> p c n", p=128) for p in range(8)],
                lambda i, slot: proj_comp(i, slot, q_fm, 0.125, 8),
                g_tile)
            c_qp = r.cnt["pe"]
            r.wait("dve", "pe", c_qp)
            for n in range(8):
                dve(lambda e, n=n: e.tensor_scalar(
                    out=q_fm[:, n * 512:(n + 1) * 512], in0=bank(n),
                    scalar1=0.125, scalar2=None, op0=ALU.mult))
            c_qe = r.cnt["dve"]
            r.wait("pe", "dve", c_qe)

            g_tile = stream(
                wstr, 2048,
                [wsc["wk"][p * 256:(p + 1) * 256, :]
                 .rearrange("(c p) n -> p c n", p=128) for p in range(8)],
                lambda i, slot: proj_comp(i, slot, k_fm, 1.0, 8),
                g_tile)
            c_kp = r.cnt["pe"]
            r.wait("dve", "pe", c_kp)
            for n in range(8):
                dve(lambda e, n=n: e.tensor_copy(
                    out=k_fm[:, n * 512:(n + 1) * 512], in_=bank(n)))
            c_ke = r.cnt["dve"]
            r.wait("pe", "dve", c_ke)

            if t == 0:
                dump("q_fm", q_fm[:, :], "dve", r.cnt["dve"])
                dump("k_fm", k_fm[:, :], "dve", r.cnt["dve"])

            def s_comp(i, slot):
                for jj in range(2):
                    j = i * 2 + jj
                    wsl = slot[:, jj * 1024:(jj + 1) * 1024]
                    for tc in range(NTC):
                        lh = xfb[:, j * 512 + tc * 128:j * 512 + (tc + 1) * 128]
                        for hf in range(2):
                            mm(bank(tc * 2 + hf), lh,
                               wsl[:, hf * 512:(hf + 1) * 512],
                               start=(j == 0), stop=(j == NJ - 1))
                return pe_mark()

            g_tile = stream(
                wstr, 2048,
                [wsc["ws"][p * 256:(p + 1) * 256, :]
                 .rearrange("(c p) n -> p c n", p=128) for p in range(8)],
                s_comp, g_tile)
            c_sp3 = r.cnt["pe"]
            for tc in range(NTC):
                r.wait("dve", "pe", c_sp3)
                if tc > 0:
                    r.wait("dve", "act", r.cnt["act"])
                for hf in range(2):
                    dve(lambda e, tc=tc, hf=hf: e.tensor_scalar(
                        out=sclip[:, hf * 512:(hf + 1) * 512],
                        in0=bank(tc * 2 + hf), scalar1=32.0, scalar2=-32.0,
                        op0=ALU.min, op1=ALU.max))
                c_cl = r.cnt["dve"]
                r.wait("act", "dve", c_cl)
                act(lambda e, tc=tc: e.activation(
                    out=p_tm[:, tc * 1024:(tc + 1) * 1024], in_=sclip[:, :],
                    func=AF.Exp))
            c_pex3 = r.cnt["act"]
            r.wait("pe", "act", c_pex3)
            if t == 0:
                dump("p_tm", p_tm[:, :], "act", c_pex3)

            for wname, dest, do_g in (("wv", v_tm, False), ("wg", None, True)):
                for half in range(2):
                    def vg_comp(i, slot, half=half):
                        for jj in range(2):
                            j = i * 2 + jj
                            wsl = slot[:, jj * 1024:(jj + 1) * 1024]
                            for tc in range(NTC):
                                lh = xfb[:, j * 512 + tc * 128:
                                         j * 512 + (tc + 1) * 128]
                                for sub in range(2):
                                    mm(bank(tc * 2 + sub), lh,
                                       wsl[:, sub * 512:(sub + 1) * 512],
                                       start=(j == 0), stop=(j == NJ - 1))
                        return pe_mark()

                    g_tile = stream(
                        wstr, 2048,
                        [wsc[wname][p * 256:(p + 1) * 256,
                                    half * 1024:(half + 1) * 1024]
                         .rearrange("(c p) n -> p c n", p=128)
                         for p in range(8)],
                        vg_comp, g_tile)
                    c_vg = r.cnt["pe"]
                    if not do_g:
                        r.wait("dve", "pe", c_vg)
                        for tc in range(NTC):
                            for sub in range(2):
                                dve(lambda e, tc=tc, sub=sub, half=half:
                                    e.tensor_copy(
                                        out=v_tm[:, tc * 2048 + half * 1024 +
                                                 sub * 512:tc * 2048 +
                                                 half * 1024 + (sub + 1) * 512],
                                        in_=bank(tc * 2 + sub)))
                        c_ve = r.cnt["dve"]
                        r.wait("pe", "dve", c_ve)
                    else:
                        r.wait("act", "pe", c_vg)
                        c_sg_dve = r.cnt["dve"]
                        for tc in range(NTC):
                            for sub in range(2):
                                act(lambda e, tc=tc, sub=sub: e.activation(
                                    out=thg[:, 0:512], in_=bank(tc * 2 + sub),
                                    func=AF.Tanh, scale=0.5))
                                act(lambda e, tc=tc, sub=sub: e.activation(
                                    out=thg[:, 512:1024],
                                    in_=bank(tc * 2 + sub), func=AF.Copy))
                                c_thg = r.cnt["act"]
                                r.wait("dve", "act", c_thg)
                                dve(lambda e, tc=tc, sub=sub, half=half:
                                    e.scalar_tensor_tensor(
                                        out=sgb[:, tc * 2048 + half * 1024 +
                                                sub * 512:tc * 2048 +
                                                half * 1024 + (sub + 1) * 512],
                                        in0=thg[:, 0:512], scalar=1.0,
                                        in1=thg[:, 512:1024], op0=ALU.add,
                                        op1=ALU.mult))
                                r.wait("act", "dve", r.cnt["dve"])
                        c_ge = r.cnt["dve"]
                        r.wait("pe", "dve", c_ge)

            c_xf_free = r.cnt["pe"]

            if t == 0:
                dump("v_tm", v_tm[:, :], "act", r.cnt["act"])
                dump("sgb", sgb[:, :], "dve", r.cnt["dve"])

            r.mark(f"tile{t}_attn")
            fcsrc = [fcq_s[ip * 4:(ip + 1) * 4, :, hb * 512:(hb + 1) * 512]
                     .rearrange("i p h -> p i h")
                     for hb in range(16) for ip in range(4)]
            fc_pre = (slot_load(wffn, 2048, fcsrc, 0, g_tile, "pe", "dfA"),
                      slot_load(wffn, 2048, fcsrc, 1, g_tile, "pe", "dfB"))
            wosrc = [wsc["wo"][p * 128:(p + 1) * 128, :] for p in range(NJ)]
            wo_pre = (slot_load(wstr, 2048, wosrc, 0, g_tile, "pe", "dgA"),
                      slot_load(wstr, 2048, wosrc, 1, g_tile, "pe", "dgB"))
            r.wait("pool", "pw", r.cnt["pw"])
            r.wait("pool", "dve", c_ln2)
            for tc in range(NTC):
                r.dma("pool", "pw", x2_f[:, tc * 2048:(tc + 1) * 2048],
                      x_in[(t * 4 + tc) * 128:(t * 4 + tc + 1) * 128, :])
            c_xres = r.cnt["pw"]
            c_s_upd = c_dve_state
            c_kh_upd = c_dve_state
            c_vh_upd = c_dve_state
            c_kh_upd_act = 0
            c_vh_upd_act = 0
            c_og_prev = 0
            g_sqred = [0, 0]
            for tc in range(NTC):
                r.mark(f"tile{t}_ch{tc}_A")
                r.wait("pe", "dve", max(c_s_upd, c_og_prev))
                r.wait("pe", "slo", c_slo)
                for hf in range(2):
                    mm(bank(6 + hf), ones2[:, :],
                       s_hl[0:2, hf * 512:(hf + 1) * 512], True, False)
                    mm(bank(6 + hf), triu[:, :],
                       p_tm[:, tc * 1024 + hf * 512:tc * 1024 + (hf + 1) * 512],
                       False, True)
                c_sfull = pe_mark()
                r.wait("act", "pe", c_sfull)
                act(lambda e: e.copy(out=oks[:, :], in_=pB[:, 1024:2048]))
                c_sfc = r.cnt["act"]
                r.wait("dve", "act", c_sfc)
                dve(lambda e: e.reciprocal(out=rcs[:, :], in_=oks[:, :]))
                r.wait("pool", "act", c_sfc)
                c_row = r.dma("pool", "slo", s_st[0:1, :], oks[127:128, :])
                r.wait("dve", "slo", c_row)
                dve(lambda e: e.tensor_copy(out=s_hl[0:1, :], in_=s_st[0:1, :]))
                dve(lambda e: e.tensor_tensor(out=s_lo[0:1, :],
                                              in0=s_st[0:1, :],
                                              in1=s_hl[0:1, :],
                                              op=ALU.subtract))
                c_s_upd = r.cnt["dve"]
                r.wait("pool", "dve", c_s_upd)
                c_slo = r.dma("pool", "slo", s_hl[1:2, :], s_lo[0:1, :])

                r.wait("pe", "dve", c_vh_upd)
                for n in range(8):
                    tr(pA[:, 1024 + n * 64:1024 + (n + 1) * 64].bitcast(BF16),
                       k_fm[:, n * 512 + tc * 128:n * 512 + (tc + 1) * 128])
                c_trk = pe_mark()
                r.wait("act", "pe", c_trk)
                act(lambda e: e.copy(
                    out=ktb[:, :], in_=pA[:, 1024:1536].bitcast(BF16)))
                c_kt = r.cnt["act"]
                r.wait("pe", "act", c_kt)
                for n in range(8):
                    tr(pA[:, 1024 + n * 64:1024 + (n + 1) * 64].bitcast(BF16),
                       p_tm[:, tc * 1024 + n * 128:tc * 1024 + (n + 1) * 128])
                c_trp = pe_mark()
                r.wait("act", "pe", c_trp)
                act(lambda e: e.copy(
                    out=p_fmb[:, :], in_=pA[:, 1024:1536].bitcast(BF16)))
                c_pfm = r.cnt["act"]
                r.wait("pe", "act", c_pfm)

                sk_m = [0] * H
                msk_m = [0] * H
                ok_m = [0] * H

                def emit_ok(h):
                    b0 = (h % 2) * 64
                    tn = h // 2
                    r.wait("pe", "dve", msk_m[h])
                    if h == 0:
                        r.wait("pe", "dve", c_kh_upd)
                        r.wait("pe", "act", c_kh_upd_act)
                    okb = bank(6 + h // 8)[:, (h % 8) * 64:(h % 8) * 64 + 64]
                    mm(okb, q_fm[b0:b0 + 64, tn * 512 + tc * 128:
                                 tn * 512 + (tc + 1) * 128],
                       kh_b[b0:b0 + 64, h * 64:(h + 1) * 64], True, False)
                    mm(okb, skq[h % 4][:, :],
                       p_tm[:, tc * 1024 + h * 64:tc * 1024 + (h + 1) * 64],
                       False, True)
                    ok_m[h] = r.cnt["pe"]

                r.wait("pe", "dve", c_kh_upd)
                for h in range(H):
                    b0 = (h % 2) * 64
                    tn = h // 2
                    mm(bank(h % 4)[:, 0:128],
                       k_fm[b0:b0 + 64, tn * 512 + tc * 128:
                            tn * 512 + (tc + 1) * 128],
                       q_fm[b0:b0 + 64, tn * 512 + tc * 128:
                            tn * 512 + (tc + 1) * 128],
                       True, True)
                    sk_m[h] = r.cnt["pe"]
                    r.wait("dve", "pe", sk_m[h])
                    if h >= 4:
                        r.wait("dve", "pe", ok_m[h - 4])
                    dve(lambda e, h=h: e.tensor_tensor(
                        out=skq[h % 4][:, :], in0=bank(h % 4)[:, 0:128],
                        in1=triu[:, :], op=ALU.mult))
                    msk_m[h] = r.cnt["dve"]
                    if h >= 3:
                        emit_ok(h - 3)
                for h in range(H - 3, H):
                    emit_ok(h)
                c_ok = pe_mark()

                r.wait("pe", "dve", msk_m[H - 1])
                for h in range(H):
                    mm(bank(h // 8, 0, 64)[:, (h % 8) * 64:(h % 8) * 64 + 64],
                       ktb[:, h * 64:(h + 1) * 64],
                       p_tm[:, tc * 1024 + h * 64:tc * 1024 + (h + 1) * 64],
                       True, True)
                c_kst = pe_mark()

                r.wait("dve", "pe", c_ok)
                r.wait("dve", "slo", c_row)
                dve(lambda e: e.tensor_tensor(out=oks[:, :],
                                              in0=pB[:, 1024:2048],
                                              in1=rcs[:, :], op=ALU.mult))
                dve(lambda e: e.reduce_max(
                    out=sm[0][:, 0:16],
                    in_=oks[:, :].rearrange("p (h m) -> p h m", m=64),
                    axis=AX.X))
                dve(lambda e: e.tensor_scalar(out=sm[1][:, 0:16],
                                              in0=sm[0][:, 0:16], scalar1=-1.0,
                                              scalar2=None, op0=ALU.mult))
                for h in range(H):
                    dve(lambda e, h=h: e.tensor_scalar(
                        out=pexp[:, h * 64:(h + 1) * 64],
                        in0=oks[:, h * 64:(h + 1) * 64],
                        scalar1=sm[1][:, h:h + 1], scalar2=None, op0=ALU.add))
                c_msub = r.cnt["dve"]
                r.wait("act", "dve", c_msub)
                act(lambda e: e.activation(out=oks[:, :], in_=pexp[:, :],
                                           func=AF.Exp))
                c_exp = r.cnt["act"]
                r.wait("dve", "act", c_exp)
                dve(lambda e: e.reduce_sum(
                    out=sm[2][:, 0:16],
                    in_=oks[:, :].rearrange("p (h m) -> p h m", m=64),
                    axis=AX.X))
                dve(lambda e: e.reciprocal(out=sm[3][:, 0:16],
                                           in_=sm[2][:, 0:16]))
                for h in range(H):
                    dve(lambda e, h=h: e.tensor_scalar(
                        out=pexp[:, h * 64:(h + 1) * 64],
                        in0=rcs[:, h * 64:(h + 1) * 64],
                        scalar1=sm[3][:, h:h + 1], scalar2=None, op0=ALU.mult))
                dve(lambda e: e.tensor_tensor(out=pvn_tm[:, :], in0=oks[:, :],
                                              in1=pexp[:, :], op=ALU.mult))
                c_pvn = r.cnt["dve"]
                if t == 0 and tc == 0:
                    dump("rcs0", rcs[:, :], "dve", c_pvn)
                    dump("pvn0", pvn_tm[:, :], "dve", c_pvn)
                r.wait("dve", "pe", c_kst)
                for i in range(2):
                    dve(lambda e, i=i: e.tensor_tensor(
                        out=kh_f[0:64, i * 512:(i + 1) * 512],
                        in0=bank(i, 0, 64),
                        in1=kh_f[0:64, i * 512:(i + 1) * 512], op=ALU.add))
                c_khf = r.cnt["dve"]
                c_kh_upd = c_khf
                r.wait("act", "dve", c_khf)
                for bb in (0, 64):
                    act(lambda e, bb=bb: e.copy(
                        out=kh_b[bb:bb + 64, :], in_=kh_f[0:64, :]))
                c_kh_upd_act = r.cnt["act"]

                r.wait("pe", "dve", c_pvn)
                for n in range(8):
                    tr(pA[:, 1024 + n * 64:1024 + (n + 1) * 64].bitcast(BF16),
                       pvn_tm[:, n * 128:(n + 1) * 128])
                c_trpv = pe_mark()
                r.wait("act", "pe", c_trpv)
                act(lambda e: e.copy(
                    out=pvn_fm[:, :], in_=pA[:, 1024:1536].bitcast(BF16)))
                c_pvf = r.cnt["act"]
                r.wait("pe", "act", c_pvf)

                pv_m = [0] * H
                mpv_m = [0] * H
                ovm = [0] * H

                def emit_ov(h):
                    b0 = (h % 2) * 64
                    r.wait("pe", "dve", mpv_m[h])
                    if h == 0:
                        r.wait("pe", "dve", c_vh_upd)
                        r.wait("pe", "act", c_vh_upd_act)
                    ovb = bank(4 + h // 4)[:, (h % 4) * 128:(h % 4) * 128 + 128]
                    mm(ovb, pvn_fm[b0:b0 + 64,
                                   (h // 2) * 128:(h // 2) * 128 + 128],
                       vh_b[b0:b0 + 64, h * 128:(h + 1) * 128], True, False)
                    mm(ovb, skq[h % 4][:, :],
                       v_tm[:, tc * 2048 + h * 128:tc * 2048 + (h + 1) * 128],
                       False, True)
                    ovm[h] = r.cnt["pe"]

                r.wait("pe", "dve", c_kh_upd)
                for h in range(H):
                    b0 = (h % 2) * 64
                    mm(bank(h % 4)[:, 0:128],
                       p_fmb[b0:b0 + 64, (h // 2) * 128:(h // 2) * 128 + 128],
                       pvn_fm[b0:b0 + 64, (h // 2) * 128:(h // 2) * 128 + 128],
                       True, True)
                    pv_m[h] = r.cnt["pe"]
                    r.wait("dve", "pe", pv_m[h])
                    if h >= 4:
                        r.wait("dve", "pe", ovm[h - 4])
                    dve(lambda e, h=h: e.tensor_tensor(
                        out=skq[h % 4][:, :], in0=bank(h % 4)[:, 0:128],
                        in1=triu[:, :], op=ALU.mult))
                    mpv_m[h] = r.cnt["dve"]
                    if h >= 3:
                        emit_ov(h - 3)
                for h in range(H - 3, H):
                    emit_ov(h)
                c_ov = pe_mark()

                r.wait("act", "pe", c_ov)
                for bq in range(4):
                    r.wait("act", "dve", g_sqred[bq % 2])
                    act(lambda e, bq=bq: e.activation(
                        out=osq[:, :] if bq % 2 == 0 else osq2[:, :],
                        in_=bank(4 + bq), func=AF.Square))
                    c_sqa = r.cnt["act"]
                    r.wait("dve", "act", c_sqa)
                    dve(lambda e, bq=bq: e.reduce_sum(
                        out=sm[4][:, bq * 4:(bq + 1) * 4],
                        in_=(osq if bq % 2 == 0 else osq2)[:, :]
                        .rearrange("p (h v) -> p h v", v=128),
                        axis=AX.X))
                    g_sqred[bq % 2] = r.cnt["dve"]
                dve(lambda e: e.tensor_scalar(
                    out=sm[5][:, 0:16], in0=sm[4][:, 0:16], scalar1=1.0 / DV,
                    scalar2=EPS, op0=ALU.mult, op1=ALU.add))
                rsqrt_rec(sm[5][:, 0:16], sm[6][:, 0:16], sm[0][:, 0:16],
                          sm[1][:, 0:16])
                dve(lambda e, tc=tc: e.tensor_tensor(
                    out=sggb[:, :], in0=sgb[:, tc * 2048:(tc + 1) * 2048],
                    in1=gnw_h[:, :], op=ALU.mult))
                for h in range(H):
                    dve(lambda e, h=h: e.scalar_tensor_tensor(
                        out=ogtm[:, h * 128:(h + 1) * 128],
                        in0=bank(4 + h // 4)[:, (h % 4) * 128:
                                             (h % 4) * 128 + 128],
                        scalar=sm[6][:, h:h + 1],
                        in1=sggb[:, h * 128:(h + 1) * 128],
                        op0=ALU.mult, op1=ALU.mult))
                c_og = r.cnt["dve"]
                c_og_prev = c_og
                if t == 0 and tc == 0:
                    dump("ogtm0", ogtm[:, :], "dve", c_og)
                r.wait("pe", "dve", c_og)
                for grp in range(2):
                    for n in range(8):
                        j = grp * 8 + n
                        tr(pA[:, 1024 + n * 64:1024 + (n + 1) * 64]
                           .bitcast(BF16), ogtm[:, j * 128:(j + 1) * 128])
                    c_trog = pe_mark()
                    r.wait("act", "pe", c_trog)
                    for n in range(8):
                        j = grp * 8 + n
                        act(lambda e, j=j, n=n, tc=tc: e.copy(
                            out=ogT[:, j * 512 + tc * 128:
                                    j * 512 + (tc + 1) * 128],
                            in_=pA[:, 1024 + n * 64:1024 + (n + 1) * 64]
                            .bitcast(BF16)))
                    if grp == 0:
                        r.wait("pe", "act", r.cnt["act"])
                c_ogT = r.cnt["act"]
                r.wait("pe", "act", c_ogT)

                for h in range(H):
                    mm(bank(2 + h // 4, 0, 64)
                       [:, (h % 4) * 128:(h % 4) * 128 + 128],
                       p_tm[:, tc * 1024 + h * 64:tc * 1024 + (h + 1) * 64],
                       v_tm[:, tc * 2048 + h * 128:tc * 2048 + (h + 1) * 128],
                       True, True)
                c_su = pe_mark()
                r.wait("dve", "pe", c_su)
                for i in range(4):
                    dve(lambda e, i=i: e.tensor_tensor(
                        out=vh_f[0:64, i * 512:(i + 1) * 512],
                        in0=bank(2 + i, 0, 64),
                        in1=vh_f[0:64, i * 512:(i + 1) * 512], op=ALU.add))
                c_vhf = r.cnt["dve"]
                r.wait("act", "dve", c_vhf)
                for bb in (0, 64):
                    act(lambda e, bb=bb: e.copy(
                        out=vh_b[bb:bb + 64, :], in_=vh_f[0:64, :]))
                c_vh_upd_act = r.cnt["act"]
                c_vh_upd = r.cnt["dve"]
                c_dve_state = r.cnt["dve"]

            r.mark(f"tile{t}_wo")
            r.wait("pe", "dve", c_vh_upd)
            r.wait("pe", "pw", c_xres)
            g_xf = pe_gate()
            if t + 1 < NT:
                r.wait("pool", "pe", c_xf_free)
                r.dma("pool", "pw",
                      xfb.rearrange("p (j t) -> p j t", t=512),
                      xT_in[:, (t + 1) * 512:(t + 2) * 512]
                      .rearrange("(j p) t -> p j t", p=128))
                c_xf_pre = r.cnt["pw"]
            c_x2 = [0] * NTC
            for hp in range(2):
                def wo_comp(i, slot, hp=hp):
                    for tcl in range(2):
                        tc = hp * 2 + tcl
                        for piece in range(4):
                            mm(bank(tcl * 4 + piece),
                               ogT[:, i * 512 + tc * 128:
                                   i * 512 + (tc + 1) * 128],
                               slot[:, piece * 512:(piece + 1) * 512],
                               start=(i == 0), stop=(i == NJ - 1))
                    return pe_mark()

                g_tile2 = stream(
                    wstr, 2048, wosrc,
                    wo_comp, g_tile if hp == 0 else g_tile2,
                    pre=wo_pre if hp == 0 else None)
                c_wo = r.cnt["pe"]
                for tcl in range(2):
                    tc = hp * 2 + tcl
                    r.wait("dve", "pe", c_wo)
                    for piece in range(4):
                        dve(lambda e, tc=tc, tcl=tcl, piece=piece:
                            e.tensor_tensor(
                                out=xrb[:, piece * 512:(piece + 1) * 512],
                                in0=bank(tcl * 4 + piece),
                                in1=x2_f[:, tc * 2048 + piece * 512:
                                         tc * 2048 + (piece + 1) * 512],
                                op=ALU.add))
                    if t == 0 and tc == 0:
                        dump("xrb0", xrb[:, :], "dve", r.cnt["dve"])
                    dve(lambda e: e.reduce_sum(out=sm[0][:, 0:1],
                                               in_=xrb[:, :], axis=AX.X))
                    dve(lambda e: e.tensor_scalar(
                        out=sm[1][:, 0:1], in0=sm[0][:, 0:1],
                        scalar1=-1.0 / D, scalar2=None, op0=ALU.mult))
                    dve(lambda e, tc=tc: e.tensor_scalar(
                        out=x2_f[:, tc * 2048:(tc + 1) * 2048], in0=xrb[:, :],
                        scalar1=sm[1][:, 0:1], scalar2=None, op0=ALU.add))
                    dve(lambda e, tc=tc: e.tensor_tensor(
                        out=xrb[:, :], in0=x2_f[:, tc * 2048:(tc + 1) * 2048],
                        in1=x2_f[:, tc * 2048:(tc + 1) * 2048], op=ALU.mult))
                    dve(lambda e: e.reduce_sum(out=sm[2][:, 0:1],
                                               in_=xrb[:, :], axis=AX.X))
                    dve(lambda e: e.tensor_scalar(
                        out=sm[3][:, 0:1], in0=sm[2][:, 0:1], scalar1=1.0 / D,
                        scalar2=EPS, op0=ALU.mult, op1=ALU.add))
                    rsqrt_rec(sm[3][:, 0:1], sm[4][:, 0:1], sm[0][:, 0:1],
                              sm[1][:, 0:1])
                    dve(lambda e, tc=tc: e.tensor_scalar(
                        out=x2_f[:, tc * 2048:(tc + 1) * 2048],
                        in0=x2_f[:, tc * 2048:(tc + 1) * 2048],
                        scalar1=sm[4][:, 0:1], scalar2=None, op0=ALU.mult))
                    c_x2[tc] = r.cnt["dve"]
                    r.wait("sync", "dve", c_x2[tc])

            r.mark(f"tile{t}_x2cast")
            c_trx = 0
            for tc in range(NTC):
                if tc > 0:
                    r.wait("dve", "pe", c_trx)
                dve(lambda e, tc=tc: e.tensor_copy(
                    out=x2bf[:, :], in_=x2_f[:, tc * 2048:(tc + 1) * 2048]))
                c_cast = r.cnt["dve"]
                r.wait("pe", "dve", c_cast)
                for grp in range(2):
                    for n in range(8):
                        j = grp * 8 + n
                        tr(pA[:, 1024 + n * 64:1024 + (n + 1) * 64]
                           .bitcast(BF16), x2bf[:, j * 128:(j + 1) * 128])
                    c_trx = pe_mark()
                    r.wait("act", "pe", c_trx)
                    for n in range(8):
                        j = grp * 8 + n
                        act(lambda e, j=j, n=n, tc=tc: e.copy(
                            out=x2fm[:, j * 512 + tc * 128:
                                     j * 512 + (tc + 1) * 128],
                            in_=pA[:, 1024 + n * 64:1024 + (n + 1) * 64]
                            .bitcast(BF16)))
                    if grp == 0:
                        r.wait("pe", "act", r.cnt["act"])
                c_x2f = r.cnt["act"]
                r.wait("pe", "act", c_x2f)

            r.mark(f"tile{t}_fc")

            fcb_act = [0] * 16
            fcb_dve = [0] * 16
            fcb_ssqm = [0] * 16
            fcb_acc = [0] * 16

            def emit_fc_tail(hb):
                B0 = (hb % 2) * 4
                m = pe_mark()
                r.wait("act", "pe", m)
                r.wait("dve", "pe", m)
                if hb >= 1:
                    r.wait("dve", "pe", fcb_ssqm[hb - 1])
                    r.wait("act", "dve", fcb_dve[hb - 1])
                for hs in range(4):
                    ht = hb * 4 + hs
                    rsl = r4[:, (hs % 2) * 512:(hs % 2) * 512 + 512]
                    if hs % 2 == 0:
                        if hs == 2:
                            r.wait("act", "dve", r.cnt["dve"])
                        act(lambda e, B0=B0, hs=hs, rsl=rsl: e.activation(
                            out=rsl, in_=bank(B0 + hs), func=AF.Relu))
                        r.wait("dve", "act", r.cnt["act"])
                    else:
                        dve(lambda e, B0=B0, hs=hs, rsl=rsl: e.tensor_scalar(
                            out=rsl, in0=bank(B0 + hs), scalar1=0.0,
                            scalar2=None, op0=ALU.max))
                    dve(lambda e, ht=ht, rsl=rsl: e.tensor_tensor(
                        out=h_sr[:, ht * 512:(ht + 1) * 512],
                        in0=rsl, in1=rsl, op=ALU.mult))
                    dve(lambda e, ht=ht, hs=hs: e.tensor_tensor(
                        out=hsq4[:, hs * 512:(hs + 1) * 512],
                        in0=h_sr[:, ht * 512:(ht + 1) * 512],
                        in1=h_sr[:, ht * 512:(ht + 1) * 512], op=ALU.mult))
                fcb_dve[hb] = r.cnt["dve"]
                fcb_act[hb] = r.cnt["act"]

            def emit_fc_ssq(hb):
                B0 = (hb % 2) * 4
                r.wait("pe", "dve", fcb_dve[hb])
                for tcl in range(4):
                    for hs in range(4):
                        mm(bank(B0 + tcl)[:, 0:1],
                           hsq4[:, hs * 512 + tcl * 128:
                                hs * 512 + (tcl + 1) * 128],
                           onescol[:, :], start=(hs == 0), stop=(hs == 3))
                fcb_ssqm[hb] = pe_mark()
                r.wait("dve", "pe", fcb_ssqm[hb])
                for tcl in range(4):
                    if hb == 0:
                        dve(lambda e, B0=B0, tcl=tcl: e.tensor_copy(
                            out=ssq_acc[:, tcl:tcl + 1],
                            in_=bank(B0 + tcl)[:, 0:1]))
                    else:
                        dve(lambda e, B0=B0, tcl=tcl: e.tensor_tensor(
                            out=ssq_acc[:, tcl:tcl + 1],
                            in0=bank(B0 + tcl)[:, 0:1],
                            in1=ssq_acc[:, tcl:tcl + 1], op=ALU.add))
                fcb_acc[hb] = r.cnt["dve"]

            def fc_comp(n, slot):
                hb, ip = divmod(n, 4)
                B0 = (hb % 2) * 4
                if ip == 0 and hb >= 2:
                    r.wait("pe", "act", fcb_act[hb - 2])
                    r.wait("pe", "dve", fcb_dve[hb - 2])
                    r.wait("pe", "dve", fcb_acc[hb - 2])
                for jj in range(4):
                    j = ip * 4 + jj
                    for hs in range(4):
                        mm(bank(B0 + hs),
                           slot[:, jj * 512 + hs * 128:
                                jj * 512 + (hs + 1) * 128],
                           x2fm[:, j * 512:(j + 1) * 512],
                           start=(j == 0), stop=(j == NJ - 1))
                if ip == 1 and hb >= 1:
                    emit_fc_ssq(hb - 1)
                if ip == 3:
                    emit_fc_tail(hb)
                return pe_mark()

            g_tile = stream(wffn, 2048, fcsrc, fc_comp, g_tile,
                            pre=fc_pre, dual=True)
            emit_fc_ssq(15)
            c_fc = r.cnt["pe"]
            r.wait("dve", "pe", c_fc)
            for tcl in range(4):
                dve(lambda e, tcl=tcl: e.tensor_scalar(
                    out=sm[5][:, tcl:tcl + 1], in0=ssq_acc[:, tcl:tcl + 1],
                    scalar1=1.0 / HID, scalar2=EPS_FFN, op0=ALU.mult,
                    op1=ALU.add))
            rsqrt_rec(sm[5][:, 0:4], sm[7][:, 0:4], sm[0][:, 0:4],
                      sm[1][:, 0:4])
            c_rsf = r.cnt["dve"]
            r.wait("pe", "dve", c_rsf)

            r.mark(f"tile{t}_proj2")
            r.wait("pe", "act", fcb_act[15])
            r.wait("pe", "act", fcb_act[14])
            for dh in range(2):
                pjsrc = [projq_s[hh * 2:hh * 2 + 2, :,
                                 dh * 1024:(dh + 1) * 1024]
                         .rearrange("c p n -> p c n") for hh in range(32)]

                def pj_comp(i, slot, dh=dh):
                    for hl in range(2):
                        h2 = i * 2 + hl
                        for tc in range(NTC):
                            for piece in range(2):
                                mm(bank(tc * 2 + piece),
                                   h_sr[:, h2 * 512 + tc * 128:
                                        h2 * 512 + (tc + 1) * 128],
                                   slot[:, hl * 1024 + piece * 512:
                                        hl * 1024 + (piece + 1) * 512],
                                   start=(h2 == 0), stop=(h2 == 63))
                    return pe_mark()

                g_tile = stream(wffn, 2048, pjsrc, pj_comp, g_tile,
                                dual=True)
                c_pj = r.cnt["pe"]
                r.wait("dve", "pe", c_pj)
                if dh == 0:
                    for tc in range(NTC):
                        for piece in range(2):
                            dve(lambda e, tc=tc, piece=piece:
                                e.scalar_tensor_tensor(
                                    out=xr2a[:, tc * 1024 + piece * 512:
                                             tc * 1024 + (piece + 1) * 512],
                                    in0=bank(tc * 2 + piece),
                                    scalar=sm[7][:, tc:tc + 1],
                                    in1=x2_f[:, tc * 2048 + piece * 512:
                                             tc * 2048 + (piece + 1) * 512],
                                    op0=ALU.mult, op1=ALU.add))
                    c_eva = r.cnt["dve"]
                    r.wait("pe", "dve", c_eva)
                else:
                    xr2b4 = lambda tc: x2_f[:, tc * 2048:tc * 2048 + 1024]
                    for tc in range(NTC):
                        for piece in range(2):
                            dve(lambda e, tc=tc, piece=piece:
                                e.scalar_tensor_tensor(
                                    out=xr2b4(tc)[:, piece * 512:
                                                  (piece + 1) * 512],
                                    in0=bank(tc * 2 + piece),
                                    scalar=sm[7][:, tc:tc + 1],
                                    in1=x2_f[:, tc * 2048 + 1024 + piece * 512:
                                             tc * 2048 + 1024 +
                                             (piece + 1) * 512],
                                    op0=ALU.mult, op1=ALU.add))
                    c_evb = r.cnt["dve"]
                    r.wait("pe", "dve", c_evb)
                    for tc in range(NTC):
                        dve(lambda e, tc=tc: e.reduce_sum(
                            out=sm[0][:, 0:1],
                            in_=xr2a[:, tc * 1024:(tc + 1) * 1024], axis=AX.X))
                        dve(lambda e, tc=tc: e.reduce_sum(
                            out=sm[1][:, 0:1], in_=xr2b4(tc), axis=AX.X))
                        dve(lambda e: e.tensor_tensor(
                            out=sm[0][:, 0:1], in0=sm[0][:, 0:1],
                            in1=sm[1][:, 0:1], op=ALU.add))
                        dve(lambda e: e.tensor_scalar(
                            out=sm[1][:, 0:1], in0=sm[0][:, 0:1],
                            scalar1=-1.0 / D, scalar2=None, op0=ALU.mult))
                        dve(lambda e, tc=tc: e.tensor_scalar(
                            out=x3b[:, 0:1024],
                            in0=xr2a[:, tc * 1024:(tc + 1) * 1024],
                            scalar1=sm[1][:, 0:1], scalar2=None, op0=ALU.add))
                        dve(lambda e, tc=tc: e.tensor_scalar(
                            out=x3b[:, 1024:2048], in0=xr2b4(tc),
                            scalar1=sm[1][:, 0:1], scalar2=None, op0=ALU.add))
                        dve(lambda e, tc=tc: e.tensor_tensor(
                            out=xr2a[:, tc * 1024:(tc + 1) * 1024],
                            in0=x3b[:, 0:1024], in1=x3b[:, 0:1024],
                            op=ALU.mult))
                        dve(lambda e, tc=tc: e.tensor_tensor(
                            out=xr2b4(tc), in0=x3b[:, 1024:2048],
                            in1=x3b[:, 1024:2048], op=ALU.mult))
                        dve(lambda e, tc=tc: e.reduce_sum(
                            out=sm[0][:, 0:1],
                            in_=xr2a[:, tc * 1024:(tc + 1) * 1024], axis=AX.X))
                        dve(lambda e, tc=tc: e.reduce_sum(
                            out=sm[1][:, 0:1], in_=xr2b4(tc), axis=AX.X))
                        dve(lambda e: e.tensor_tensor(
                            out=sm[0][:, 0:1], in0=sm[0][:, 0:1],
                            in1=sm[1][:, 0:1], op=ALU.add))
                        dve(lambda e: e.tensor_scalar(
                            out=sm[2][:, 0:1], in0=sm[0][:, 0:1],
                            scalar1=1.0 / D, scalar2=EPS, op0=ALU.mult,
                            op1=ALU.add))
                        rsqrt_rec(sm[2][:, 0:1], sm[3][:, 0:1], sm[0][:, 0:1],
                                  sm[1][:, 0:1])
                        dve(lambda e: e.tensor_scalar(
                            out=x3b[:, 0:1024], in0=x3b[:, 0:1024],
                            scalar1=sm[3][:, 0:1], scalar2=None, op0=ALU.mult))
                        dve(lambda e: e.tensor_scalar(
                            out=x3b[:, 1024:2048], in0=x3b[:, 1024:2048],
                            scalar1=sm[3][:, 0:1], scalar2=None, op0=ALU.mult))
                        c_x3 = r.cnt["dve"]
                        r.wait("act", "dve", c_x3)
                        r.streams["act"].append(
                            ("i", lambda e, t=t, tc=tc: e.dma_start(
                                out=out_dram[(t * 4 + tc) * 128:
                                             (t * 4 + tc + 1) * 128, :],
                                in_=x3b[:, :]), "dout", 16))
                        r.cnt["dout"] += 16
                        r.wait("dve", "dout", r.cnt["dout"])
                    c_ln2 = r.cnt["dve"]

        r.wait("sync", "dout", 16 * 4 * NT)

        def runner(eng_name):
            selfsync = eng_name in ("dve", "act")

            def f(eng):
                own = 0
                last_wait = -1
                for it in r.streams[eng_name]:
                    if it[0] == "w":
                        eng.wait_ge(sems[it[1]], it[2])
                    else:
                        if selfsync and own > 0 and last_wait < own:
                            eng.wait_ge(sems[eng_name], own)
                            last_wait = own
                        ins = it[1](eng)
                        if it[2] is not None:
                            ins.then_inc(sems[it[2]], it[3])
                            if it[2] == eng_name:
                                own += it[3]
            return f

        block.tensor(runner("pe"))
        block.vector(runner("dve"))
        block.scalar(runner("act"))
        block.gpsimd(runner("pool"))
        block.sync(runner("sync"))

    return nc


def _consts():
    ident = np.eye(128, dtype=bfdt)
    triu = np.triu(np.ones((128, 128), dtype=np.float32)).astype(bfdt)
    onescol = np.ones((128, 1), dtype=bfdt)
    ones2 = np.ones((2, 128), dtype=bfdt)
    sel2 = np.zeros((2, 128), dtype=bfdt)
    sel2[0, :64] = 1
    sel2[1, 64:] = 1
    sel4 = np.zeros((4, 128), dtype=bfdt)
    sel4[0, :64] = 1
    sel4[1, 64:] = 1
    sel4[2, :64] = 1
    sel4[3, 64:] = 1
    selg = np.zeros((128, 2), dtype=bfdt)
    selg[:64, 0] = 1
    selg[64:, 1] = 1
    halfrow = np.full((1, 128), 0.5, dtype=bfdt)
    return ident, triu, onescol, ones2, sel2, sel4, selg, halfrow


def kernel(L0_attn_hidden_states, L0_attn_Wq, L0_attn_Wk, L0_attn_Wv,
           L0_attn_Wg, L0_attn_Ws, L0_attn_Wo, L0_attn_g_norm_weight,
           L0_ffn_W_fc, L0_ffn_W_proj, _nt=8):
    x = np.ascontiguousarray(np.asarray(L0_attn_hidden_states,
                                        dtype=np.float32).reshape(B * T, D))
    tcore = _nt * 512
    ident, triu, onescol, ones2, sel2, sel4, selg, halfrow = _consts()
    wfcT = np.ascontiguousarray(np.asarray(L0_ffn_W_fc, np.float32).T)
    wprojT = np.ascontiguousarray(np.asarray(L0_ffn_W_proj, np.float32).T)
    gnw = np.ascontiguousarray(
        np.tile(np.asarray(L0_attn_g_norm_weight, np.float32), H)[None, :])

    nc_prog = build(_nt)
    in_maps = []
    for c in range(NC):
        seq = c // (NC // B)
        t0 = c * tcore if _nt == 8 else (seq * (T if _nt == 8 else 0))
        if _nt == 8:
            rows = slice(c * 4096, (c + 1) * 4096)
        else:
            per_seq = NC // B
            within = c % per_seq
            base = seq * T + within * tcore
            rows = slice(base, base + tcore)
        xs = np.ascontiguousarray(x[rows])
        xTs = np.ascontiguousarray(xs.T)
        pm = np.zeros((128, 8), np.float32)
        per_seq = NC // B
        for rk in range((c // per_seq) * per_seq, c):
            pm[:, rk] = 1.0
        in_maps.append(dict(
            x=xs, xT=xTs,
            wq=np.asarray(L0_attn_Wq, np.float32),
            wk=np.asarray(L0_attn_Wk, np.float32),
            ws=np.asarray(L0_attn_Ws, np.float32),
            wv=np.asarray(L0_attn_Wv, np.float32),
            wg=np.asarray(L0_attn_Wg, np.float32),
            wo=np.asarray(L0_attn_Wo, np.float32),
            wfcT=wfcT, wprojT=wprojT, gnw=gnw,
            ident=ident, triu=triu, onescol=onescol, ones2=ones2,
            sel2=sel2, sel4=sel4, selg=selg, halfrow=halfrow, pmask=pm))
    res = run_bass_kernel_spmd(nc_prog, in_maps, core_ids=list(range(NC)))
    outs = [np.asarray(res.results[c]["out"]) for c in range(NC)]
    if _nt == 8:
        full = np.concatenate(outs, axis=0)
        return full.reshape(B, T, D)
    return outs
